# version 13
# baseline (speedup 1.0000x reference)
import contextlib
import numpy as np
import ml_dtypes
import concourse.bass as bass
import concourse.mybir as mybir
from concourse.bass_utils import run_bass_kernel_spmd

F32 = mybir.dt.float32
BF16 = mybir.dt.bfloat16
FP16 = mybir.dt.float16
I8 = mybir.dt.int8
A_ = mybir.AluOpType
AF = mybir.ActivationFunctionType
AX = mybir.AxisListType
bfdt = ml_dtypes.bfloat16

D_MODEL = 1024
D_INNER = 2048
D_STATE = 16
FFN = 4096
TILE = 512
CHUNK = 128
MASK_BIG = 32768.0
LVLS = [128.0, 64.0, 32.0, 16.0, 8.0, 4.0, 2.0, 1.0]


class Buf:
    __slots__ = ("h", "writers", "readers", "name")

    def __init__(self, h, name):
        self.h = h
        self.name = name
        self.writers = {}
        self.readers = {}


class Builder:
    ENGINES = ("te", "ve", "se", "qs", "qg")

    def __init__(self, nc, stack):
        self.nc = nc
        self.stack = stack
        self.items = {e: [] for e in self.ENGINES}
        self.cnt = {e: 0 for e in self.ENGINES}
        self.sems = {e: stack.enter_context(nc.semaphore("sem_" + e)) for e in self.ENGINES}
        self.extra = {}
        self.waited = {e: {} for e in self.ENGINES}

    def sbuf(self, name, shape, dtype):
        return Buf(self.stack.enter_context(self.nc.sbuf_tensor(name, shape, dtype)), name)

    def psum(self, name, shape, dtype=F32):
        return Buf(self.stack.enter_context(self.nc.psum_tensor(name, shape, dtype)), name)

    def _sem_of(self, p):
        return self.sems[p] if p in self.sems else self.extra[p][0]

    def _wait(self, eng, p, v):
        w = self.waited[eng]
        if w.get(p, -1) >= v:
            return
        w[p] = v
        self.items[eng].append(("w", self._sem_of(p), v))

    def _edges(self, eng, ins, outs, accum):
        need = {}

        def add(p, v):
            if v > need.get(p, -1):
                need[p] = v

        for b in ins:
            if b is not None:
                for p, v in b.writers.items():
                    add(p, v)
        for b in outs:
            if b is None:
                continue
            for p, v in b.writers.items():
                add(p, v)
            for p, v in b.readers.items():
                add(p, v)
        for p in sorted(need):
            v = self.extra[p][1] if p in self.extra else need[p]
            self._wait(eng, p, v)

    def op(self, eng, fn, ins, outs, accum=False):
        self._edges(eng, ins, outs, accum)
        self.cnt[eng] += 1
        v = self.cnt[eng]
        self.items[eng].append(("o", fn, self.sems[eng], 1))
        for b in ins:
            if b is not None:
                b.readers[eng] = v
        for b in outs:
            if b is not None:
                b.writers[eng] = v
        return v

    def dma(self, queue, sem_name, fn, ins, outs):
        self._edges(queue, ins, outs, False)
        if sem_name not in self.extra:
            self.extra[sem_name] = [self.stack.enter_context(self.nc.semaphore("s_" + sem_name)), 0]
        ent = self.extra[sem_name]
        ent[1] += 16
        v = ent[1]
        self.items[queue].append(("o", fn, ent[0], 16))
        for b in ins:
            if b is not None:
                b.readers[sem_name] = v
        for b in outs:
            if b is not None:
                b.writers[sem_name] = v
        return v

    def wait_stream(self, eng, p):
        v = self.cnt[p] if p in self.sems else (self.extra[p][1] if p in self.extra else 0)
        if v > 0:
            self._wait(eng, p, v)

    def build_block(self):
        block = self.stack.enter_context(self.nc.Block())

        def make(name):
            seq = self.items[name]

            def body(eng):
                for it in seq:
                    if it[0] == "w":
                        eng.wait_ge(it[1], it[2])
                    else:
                        it[1](eng).then_inc(it[2], it[3])
            return body

        block.tensor(make("te"))
        block.vector(make("ve"))
        block.scalar(make("se"))
        block.sync(make("qs"))
        block.gpsimd(make("qg"))


def build_core_program(n_own, n_pre):
    NCH = n_own // CHUNK
    nc = bass.Bass("TRN2", target_bir_lowering=False)
    DI = nc.dram_tensor
    T = {}
    T["x_own"] = DI("x_own", [n_own, D_MODEL], F32, kind="ExternalInput").ap()
    T["x_pre"] = DI("x_pre", [max(n_pre, 1), D_MODEL], F32, kind="ExternalInput").ap()
    T["W_in"] = DI("W_in", [2 * D_INNER, D_MODEL], F32, kind="ExternalInput").ap()
    T["pre_Win"] = DI("pre_Win", [D_INNER, D_MODEL], F32, kind="ExternalInput").ap()
    T["conv_w"] = DI("conv_w", [D_INNER, 4], F32, kind="ExternalInput").ap()
    T["conv_b"] = DI("conv_b", [D_INNER], F32, kind="ExternalInput").ap()
    T["pre_conv_b"] = DI("pre_conv_b", [D_INNER], F32, kind="ExternalInput").ap()
    T["W_xproj"] = DI("W_xproj", [2 * D_STATE + 1, D_INNER], F32, kind="ExternalInput").ap()
    T["A_log"] = DI("A_log", [D_STATE], F32, kind="ExternalInput").ap()
    T["D_param"] = DI("D_param", [D_INNER], F32, kind="ExternalInput").ap()
    T["W_out"] = DI("W_out", [D_MODEL, D_INNER], F32, kind="ExternalInput").ap()
    T["w_gate"] = DI("w_gate", [FFN, D_MODEL], F32, kind="ExternalInput").ap()
    T["w_up"] = DI("w_up", [FFN, D_MODEL], F32, kind="ExternalInput").ap()
    T["w_down"] = DI("w_down", [D_MODEL, FFN], F32, kind="ExternalInput").ap()
    T["ident"] = DI("ident", [128, 128], BF16, kind="ExternalInput").ap()
    T["identf"] = DI("identf", [16, 16], F32, kind="ExternalInput").ap()
    T["maskbig"] = DI("maskbig", [128, 128], BF16, kind="ExternalInput").ap()
    T["out"] = DI("out", [n_own, D_MODEL], F32, kind="ExternalOutput").ap()
    T["z_sp"] = DI("z_sp", [128, 16, n_own], BF16).ap()
    T["xc_sp"] = DI("xc_sp", [128, 16, n_own], BF16).ap()
    T["xq1_sp"] = DI("xq1_sp", [128, 8, n_own], BF16).ap()
    T["xq2_sp"] = DI("xq2_sp", [128, 32, n_own], BF16).ap()
    T["t1_sp"] = DI("t1_sp", [128, 16, n_own], FP16).ap()
    T["xat_sp"] = DI("xat_sp", [128, 8, n_own], F32).ap()
    T["h_sp"] = DI("h_sp", [D_STATE, NCH * D_INNER], BF16).ap()

    with contextlib.ExitStack() as stack:
        B = Builder(nc, stack)
        _emit(B, T, n_own, n_pre)
        B.build_block()
    return nc


def _emit(B, T, n_own, n_pre):
    NCH = n_own // CHUNK
    NT_OWN = n_own // TILE
    NT_PRE = n_pre // TILE
    CPT = TILE // CHUNK
    NG = n_own // 128
    GPT = TILE // 128

    def vop(fn, ins, outs, accum=False):
        B.op("ve", fn, ins, outs, accum)

    def sop(fn, ins, outs, accum=False):
        B.op("se", fn, ins, outs, accum)

    def top(fn, ins, outs, accum=False):
        B.op("te", fn, ins, outs, accum)

    dz = Buf(None, "z_sp")
    dxc = Buf(None, "xc_sp")
    dq1 = Buf(None, "xq1_sp")
    dq2 = Buf(None, "xq2_sp")
    dt1 = Buf(None, "t1_sp")
    dxat = Buf(None, "xat_sp")
    dh = Buf(None, "h_sp")

    wpool = B.sbuf("wpool", [128, 32768], BF16)
    scr = [B.sbuf("scr%d" % i, [128, 2048], F32) for i in range(3)]
    sbf = [B.sbuf("sbf%d" % i, [128, 2048], BF16) for i in range(2)]
    wb16 = [B.sbuf("wb16_%d" % i, [128, 2048], BF16) for i in range(3)]
    WxT = B.sbuf("WxT", [128, 16 * 33], BF16)
    convw = B.sbuf("convw", [128, 16 * 4], F32)
    convb = B.sbuf("convb", [128, 16], F32)
    pconvb = B.sbuf("pconvb", [128, 16], F32)
    Dp = B.sbuf("Dp", [128, 16], F32)
    Acol = B.sbuf("Acol", [16, 1], F32)
    ident = B.sbuf("id16", [128, 128], BF16)
    identf = B.sbuf("idf32", [16, 16], F32)
    maskb = B.sbuf("maskb", [128, 128], BF16)
    ones1x16 = B.sbuf("ones1x16", [1, 16], F32)
    screp = B.sbuf("screp", [128, 2], F32)
    onesc = B.sbuf("onesc", [128, 1], F32)
    zcol16 = B.sbuf("zcol16", [16, 1], F32)
    cumT_own = B.sbuf("cumT_own", [128, NCH * 16], F32)
    BbT_own = B.sbuf("BbT_own", [128, NCH * 16], BF16)
    CT_own = B.sbuf("CT_own", [128, NCH * 16], BF16)
    Chp_all = B.sbuf("Chp_all", [16, n_own], BF16)
    c1_all = B.sbuf("c1_all", [128, NG], F32)
    c2_all = B.sbuf("c2_all", [128, NG], F32)
    swneg = [B.sbuf("swneg%d" % i, [128, 1], F32) for i in range(3)]
    thrp = [B.sbuf("thrp%d" % i, [128, 1], F32) for i in range(3)]
    thrn = [B.sbuf("thrn%d" % i, [128, 1], F32) for i in range(3)]
    cumc = B.sbuf("cumc", [16, 1 + TILE], F32)
    carry = B.sbuf("carry", [16, 1], F32)
    epscol = B.sbuf("epscol", [128, 1], F32)
    Hrow = scr[2]
    HROW = [0, 32]

    pb = [B.psum("pb%d" % i, [128, 512], F32) for i in range(6)]
    pb.append(pb[4])
    pt = [B.psum("pt%d" % i, [128, 512], BF16) for i in range(2)]
    PTOFF = [0, 0]

    B.dma("qs", "dsm", lambda q: q.dma_start(
        out=bass.AP(convw.h, 0, [[64, 128], [4, 16], [1, 4]]),
        in_=bass.AP(T["conv_w"].tensor, 0, [[4, 128], [512, 16], [1, 4]])), [], [convw])
    _pvec_loads = []
    for _nm, _dst in (("conv_b", convb), ("pre_conv_b", pconvb), ("D_param", Dp)):
        _pvec_loads.append((_nm, _dst))
    B.dma("qs", "dsm", lambda q: q.dma_start(out=Acol.h[:, 0:1], in_=T["A_log"][:, None]),
          [], [Acol])
    B.dma("qs", "dsm", lambda q: q.dma_start(out=ident.h[:, :], in_=T["ident"]), [], [ident])
    B.dma("qs", "dsm", lambda q: q.dma_start(out=identf.h[:, :], in_=T["identf"]), [], [identf])
    B.dma("qs", "dsm", lambda q: q.dma_start(out=maskb.h[:, :], in_=T["maskbig"]), [], [maskb])
    sop(lambda e: e.activation(Acol.h[:, 0:1], Acol.h[:, 0:1], AF.Exp), [Acol], [Acol])
    vop(lambda e: e.tensor_scalar_mul(Acol.h[:, 0:1], Acol.h[:, 0:1], -1.0), [Acol], [Acol])
    for bz, val in ((zcol16, 0.0), (ones1x16, 1.0), (onesc, 1.0)):
        vop(lambda e, bz=bz, val=val: e.memset(bz.h[:, :], val), [], [bz])
    vop(lambda e: e.memset(carry.h[:, :], 0.0), [], [carry])
    vop(lambda e: e.memset(epscol.h[:, :], 1e-5), [], [epscol])
    vop(lambda e: e.memset(Hrow.h[0:16, :], 0.0), [], [Hrow])

    trcnt = [0]

    def transpose_to(dst, dst_col, src_buf, src_ap, n, evac="ve"):
        slot = trcnt[0] % 2
        ps = pt[slot]
        off = PTOFF[slot]
        trcnt[0] += 1
        top(lambda e, ps=ps, src_ap=src_ap, n=n, off=off: e.transpose(
            ps.h[0:128, off:off + n], src_ap, ident.h[0:n, 0:n]),
            [src_buf, ident], [ps])
        if evac == "se":
            sop(lambda e, ps=ps, dst=dst, dst_col=dst_col, n=n, off=off: e.activation(
                dst.h[:, dst_col:dst_col + n], ps.h[0:128, off:off + n], AF.Identity),
                [ps], [dst], accum=True)
        else:
            vop(lambda e, ps=ps, dst=dst, dst_col=dst_col, n=n, off=off: e.tensor_copy(
                dst.h[:, dst_col:dst_col + n], ps.h[0:128, off:off + n]), [ps], [dst],
                accum=True)

    def load_transpose_bf16(src_t, rows, cols, dst_off, dst_span, tag):
        for rb in range(rows // 128):
            stg = sbf[rb % 2]
            B.dma("qg", tag + str(rb % 2), lambda q, rb=rb, stg=stg, cols=cols, src_t=src_t:
                  q.dma_start(out=stg.h[:, 0:cols],
                              in_=bass.AP(src_t.tensor, rb * 128 * cols,
                                          [[cols, 128], [1, cols]])), [], [stg])
            for kb in range(cols // 128):
                transpose_to(wpool, dst_off + kb * dst_span + rb * 128, stg,
                             stg.h[:, kb * 128:(kb + 1) * 128], 128)

    wx_stage = sbf[1]
    B.dma("qg", "dwx", lambda q: q.dma_start(out=wx_stage.h[0:33, :], in_=T["W_xproj"]),
          [], [wx_stage])
    for kb in range(16):
        transpose_to(WxT, kb * 33, wx_stage, wx_stage.h[0:33, kb * 128:(kb + 1) * 128], 33)

    wred = B.sbuf("wred", [128, 1], F32)
    wsum = [B.sbuf("wsum%d" % i, [128, 1], F32) for i in range(3)]
    sc_sb = B.sbuf("sc_sb", [1, 2], F32)
    wmats = [(T["w_gate"], FFN, D_MODEL), (T["w_up"], FFN, D_MODEL), (T["w_down"], D_MODEL, FFN)]
    for wi, (wt, nrows, ncols) in enumerate(wmats):
        nseg = max(1, ncols // 2048)
        segc = ncols // nseg
        si = 0
        for rb in range(nrows // 128):
            for sg in range(nseg):
                stg = scr[si % 2]
                si += 1
                B.dma("qs", "dwf%d" % (si % 2), lambda q, rb=rb, sg=sg, wt=wt, segc=segc,
                      ncols=ncols, stg=stg: q.dma_start(
                    out=stg.h[:, 0:segc],
                    in_=bass.AP(wt.tensor, rb * 128 * ncols + sg * segc,
                                [[ncols, 128], [1, segc]])), [], [stg])
                first = (rb == 0 and sg == 0)
                dst = wsum[wi] if first else wred
                vop(lambda e, stg=stg, segc=segc, dst=dst: e.reduce_sum(
                    dst.h[:, 0:1], stg.h[:, 0:segc], axis=AX.X, apply_absolute_value=True),
                    [stg], [dst])
                if not first:
                    vop(lambda e, wi=wi: e.tensor_add(wsum[wi].h[:, 0:1], wsum[wi].h[:, 0:1],
                                                      wred.h[:, 0:1]), [wsum[wi], wred],
                        [wsum[wi]])
        top(lambda e, wi=wi: e.matmul(pb[6].h[0:1, 0:1], onesc.h[:, 0:1], wsum[wi].h[:, 0:1],
                                      start=True, stop=True), [onesc, wsum[wi]], [pb[6]])
        vop(lambda e, n=nrows * ncols: e.tensor_scalar(
            sc_sb.h[0:1, 0:1], pb[6].h[0:1, 0:1], 1.0 / n, 1e-5, A_.mult, A_.max),
            [pb[6]], [sc_sb])
        vop(lambda e: e.tensor_scalar_mul(sc_sb.h[0:1, 1:2], sc_sb.h[0:1, 0:1], 0.5),
            [sc_sb], [sc_sb], accum=True)
        B.dma("qg", "dscr", lambda q: q.dma_start(
            out=screp.h[:, :],
            in_=bass.AP(sc_sb.h, 0, [[1, 1], [0, 128], [1, 2]])), [sc_sb], [screp])
        vop(lambda e, wi=wi: e.tensor_scalar_mul(swneg[wi].h[:, 0:1], screp.h[:, 0:1], -1.0),
            [screp], [swneg[wi]])
        vop(lambda e, wi=wi: e.tensor_copy(thrp[wi].h[:, 0:1], screp.h[:, 1:2]),
            [screp], [thrp[wi]])
        vop(lambda e, wi=wi: e.tensor_scalar_mul(thrn[wi].h[:, 0:1], screp.h[:, 1:2], -1.0),
            [screp], [thrn[wi]])

    xg = [B.sbuf("xg%d" % i, [128, D_MODEL], F32) for i in range(2)]
    mcol = B.sbuf("mcol", [128, 1], F32)
    m2col = B.sbuf("m2col", [128, 1], F32)
    sqcol = B.sbuf("sqcol", [128, 1], F32)
    rstd = B.sbuf("rstd", [128, 1], F32)
    nmr = B.sbuf("nmr", [128, 1], F32)
    xn32 = B.sbuf("xn32", [128, D_MODEL], F32)
    sqbuf = xn32
    xnbf = B.sbuf("xnbf", [128, D_MODEL], BF16)
    xnfm = B.sbuf("xnfm", [128, 8 * TILE], BF16)
    xproj_t = B.sbuf("xproj_t", [128, 16 * (TILE + 3)], BF16)
    zst = B.sbuf("zst", [128, 512], BF16)
    cvf = scr[0]
    xconv_t = B.sbuf("xconv_t", [128, 16 * TILE], BF16)
    xtm = [B.sbuf("xtm0", [128, D_INNER], BF16)] * 2
    xdbl = scr[1]
    dtbc = B.sbuf("dtbc", [16, TILE], F32)
    dtrow = dtbc
    sprow = dtbc
    dtA = B.sbuf("dtA", [16, TILE], F32)
    Bbar = dtA
    Bbarbf = B.sbuf("Bbarbf", [16, TILE], BF16)
    Cbf = B.sbuf("Cbf", [16, TILE], BF16)
    eexp = B.sbuf("eexp", [16, TILE], BF16)
    Bhat = B.sbuf("Bhat", [16, TILE], BF16)
    BhT = B.sbuf("BhT", [128, CPT * 16], BF16)
    dcol = B.sbuf("dcol", [48, 1], F32)
    amx = B.sbuf("amx", [128, 1], F32)
    scol = B.sbuf("scol", [128, 1], F32)
    scol2 = B.sbuf("scol2", [128, 1], F32)
    lad_w, lad_d = scr[0], scr[1]
    xq_tm = sbf[0]
    sgb = B.sbuf("sgb", [128, 512], FP16)
    upb = B.sbuf("upb", [128, 512], FP16)
    th_a = B.sbuf("th_a", [128, 2048], FP16)
    th_b = B.sbuf("th_b", [128, 2048], FP16)

    qi8 = B.sbuf("qi8", [128, 1024], I8)

    def quant_cast(fn, bufs, w, sc, wb, d, qp, c2, fp16_src=False):
        vop(lambda e: e.tensor_scalar(qi8.h[:, 0:w], fn(), sc.h[:, 0:1],
                                      None, A_.mult), bufs + [sc], [qi8])
        vop(lambda e: e.tensor_copy(wb[0].h[:, wb[1]:wb[1] + w],
                                    qi8.h[:, 0:w]), [qi8], [wb[0]])
        if fp16_src:
            vop(lambda e: e.tensor_scalar(d[0].h[:, d[1]:d[1] + w], fn(), sc.h[:, 0:1],
                                          None, A_.mult), bufs + [sc], [d[0]])
            vop(lambda e: e.tensor_sub(d[0].h[:, d[1]:d[1] + w], d[0].h[:, d[1]:d[1] + w],
                                       wb[0].h[:, wb[1]:wb[1] + w]), [d[0], wb[0]], [d[0]])
        else:
            vop(lambda e: e.scalar_tensor_tensor(
                d[0].h[:, d[1]:d[1] + w], fn(), sc.h[:, 0:1],
                wb[0].h[:, wb[1]:wb[1] + w], A_.mult, A_.subtract),
                bufs + [sc, wb[0]], [d[0]])
        vop(lambda e: e.scalar_tensor_tensor(
            qp[0].h[:, qp[1]:qp[1] + w], d[0].h[:, d[1]:d[1] + w], 0.5,
            wb[0].h[:, wb[1]:wb[1] + w], A_.is_ge, A_.add), [d[0], wb[0]], [qp[0]])
        vop(lambda e: e.tensor_scalar(c2[0].h[:, c2[1]:c2[1] + w],
                                      d[0].h[:, d[1]:d[1] + w], -0.5, None, A_.is_le),
            [d[0]], [c2[0]])
        vop(lambda e: e.tensor_sub(wb[0].h[:, wb[1]:wb[1] + w],
                                   qp[0].h[:, qp[1]:qp[1] + w],
                                   c2[0].h[:, c2[1]:c2[1] + w]), [qp[0], c2[0]], [wb[0]],
            accum=True)

    for _nm, _dst in _pvec_loads:
        B.dma("qs", "dsm", lambda q, _nm=_nm: q.dma_start(
            out=dtbc.h[0:16, 0:128],
            in_=bass.AP(T[_nm].tensor, 0, [[128, 16], [1, 128]])), [], [dtbc])
        top(lambda e: e.transpose(pb[6].h[0:128, 0:16], dtbc.h[0:16, 0:128],
                                  identf.h[0:16, 0:16]), [dtbc, identf], [pb[6]])
        vop(lambda e, _dst=_dst: e.tensor_copy(_dst.h[:, 0:16], pb[6].h[0:128, 0:16]),
            [pb[6]], [_dst])

    vop(lambda e: e.memset(
        bass.AP(xproj_t.h, 0, [[16 * (TILE + 3), 128], [TILE + 3, 16], [1, 3]]), 0.0),
        [], [xproj_t])

    hcur = [0]

    def mixer_tiles(prefix):
        ntiles = NT_PRE if prefix else NT_OWN
        x_src = T["x_pre"] if prefix else T["x_own"]
        cbcol = pconvb if prefix else convb
        n_oc = 16 if prefix else 32
        span = D_INNER if prefix else 2 * D_INNER
        def stageA(ti):
            t0 = ti * TILE
            for gp in range(GPT // 2):
                for g in (2 * gp, 2 * gp + 1):
                    xt = xg[g % 2]
                    sc = scol if g % 2 == 0 else scol2
                    B.dma("qs", "dx%d" % (g % 2), lambda q, t0=t0, g=g, xt=xt, x_src=x_src:
                          q.dma_start(out=xt.h[:, :],
                                      in_=bass.AP(x_src.tensor, (t0 + g * 128) * D_MODEL,
                                                  [[D_MODEL, 128], [1, D_MODEL]])), [], [xt])
                    vop(lambda e, xt=xt: e.reduce_sum(mcol.h[:, 0:1], xt.h[:, :], axis=AX.X),
                        [xt], [mcol])
                    sop(lambda e, xt=xt: e.activation(sqbuf.h[:, :], xt.h[:, :], AF.Square),
                        [xt], [sqbuf])
                    vop(lambda e: e.reduce_sum(sqcol.h[:, 0:1], sqbuf.h[:, :], axis=AX.X),
                        [sqbuf], [sqcol])
                    vop(lambda e: e.tensor_scalar_mul(mcol.h[:, 0:1], mcol.h[:, 0:1],
                                                      1.0 / D_MODEL), [mcol], [mcol])
                    vop(lambda e: e.tensor_mul(m2col.h[:, 0:1], mcol.h[:, 0:1],
                                               mcol.h[:, 0:1]), [mcol], [m2col])
                    vop(lambda e: e.scalar_tensor_tensor(sqcol.h[:, 0:1], sqcol.h[:, 0:1],
                                                         1.0 / D_MODEL, m2col.h[:, 0:1],
                                                         A_.mult, A_.subtract),
                        [sqcol, m2col], [sqcol])
                    sop(lambda e: e.activation(rstd.h[:, 0:1], sqcol.h[:, 0:1], AF.Sqrt,
                                               bias=epscol.h[:, 0:1]), [sqcol, epscol],
                        [rstd])
                    vop(lambda e: e.reciprocal(rstd.h[:, 0:1], rstd.h[:, 0:1]),
                        [rstd], [rstd])
                    vop(lambda e: e.scalar_tensor_tensor(nmr.h[:, 0:1], mcol.h[:, 0:1],
                                                         -1.0, rstd.h[:, 0:1],
                                                         A_.mult, A_.mult),
                        [mcol, rstd], [nmr])
                    sop(lambda e, xt=xt: e.activation(xt.h[:, :], xt.h[:, :], AF.Identity,
                                                      bias=nmr.h[:, 0:1],
                                                      scale=rstd.h[:, 0:1]),
                        [xt, nmr, rstd], [xt])
                    vop(lambda e, xt=xt: e.tensor_copy(xnbf.h[:, :], xt.h[:, :]),
                        [xt], [xnbf])
                    for kc in range(8):
                        transpose_to(xnfm, kc * TILE + g * 128, xnbf,
                                     xnbf.h[:, kc * 128:(kc + 1) * 128], 128)
                    if not prefix:
                        gi = (t0 + g * 128) // 128
                        vop(lambda e, xt=xt: e.reduce_max(amx.h[:, 0:1], xt.h[:, :],
                                                          axis=AX.X,
                                                          apply_absolute_value=True),
                            [xt], [amx])
                        vop(lambda e: e.tensor_scalar_max(amx.h[:, 0:1], amx.h[:, 0:1],
                                                          1e-5), [amx], [amx])
                        vop(lambda e, sc=sc: e.reciprocal(sc.h[:, 0:1], amx.h[:, 0:1]),
                            [amx], [sc])
                        vop(lambda e, sc=sc: e.tensor_scalar_mul(sc.h[:, 0:1],
                                                                 sc.h[:, 0:1], 127.0),
                            [sc], [sc])
                        vop(lambda e, gi=gi, sc=sc: e.reciprocal(
                            c1_all.h[:, gi:gi + 1], sc.h[:, 0:1]), [sc],
                            [c1_all], accum=True)
                        quant_cast(lambda xt=xt: xt.h[:, :], [xg[g % 2]], D_MODEL, sc,
                                   (sbf[g % 2], 0), (scr[g % 2], 1024),
                                   (sbf[g % 2], 1024), (xnbf, 0))
                        for kc in range(8):
                            col = kc * TILE + g * 128
                            dst = wb16[0] if col < 2048 else wb16[1]
                            transpose_to(dst, col % 2048, sbf[g % 2],
                                         sbf[g % 2].h[:, kc * 128:(kc + 1) * 128], 128)
            if not prefix:
                B.dma("qg", "dxq1", lambda q, t0=t0: q.dma_start(
                    out=bass.AP(T["xq1_sp"].tensor, t0,
                                [[8 * n_own, 128], [n_own, 4], [1, TILE]]),
                    in_=bass.AP(wb16[0].h, 0, [[2048, 128], [TILE, 4], [1, TILE]])),
                    [wb16[0]], [dq1])
                B.dma("qg", "dxq1", lambda q, t0=t0: q.dma_start(
                    out=bass.AP(T["xq1_sp"].tensor, 4 * n_own + t0,
                                [[8 * n_own, 128], [n_own, 4], [1, TILE]]),
                    in_=bass.AP(wb16[1].h, 0, [[2048, 128], [TILE, 4], [1, TILE]])),
                    [wb16[1]], [dq1])
        def stageB(ti):
            t0 = ti * TILE
            for oc in range(n_oc):
                ps = pb[oc % 2]
                for kc in range(8):
                    top(lambda e, oc=oc, kc=kc, ps=ps, span=span: e.matmul(
                        ps.h[:, :], wpool.h[:, kc * span + oc * 128:kc * span + (oc + 1) * 128],
                        xnfm.h[:, kc * TILE:(kc + 1) * TILE],
                        start=(kc == 0), stop=(kc == 7)), [wpool, xnfm], [ps], accum=(kc > 0))
                if oc < 16:
                    vop(lambda e, oc=oc, ps=ps: e.tensor_copy(
                        xproj_t.h[:, oc * (TILE + 3) + 3:oc * (TILE + 3) + 3 + TILE],
                        ps.h[:, :]), [ps], [xproj_t], accum=True)
                else:
                    vop(lambda e, ps=ps: e.tensor_copy(zst.h[:, :], ps.h[:, :]), [ps], [zst])
                    B.dma("qg", "dzs", lambda q, t0=t0, oc=oc: q.dma_start(
                        out=bass.AP(T["z_sp"].tensor, (oc - 16) * n_own + t0,
                                    [[16 * n_own, 128], [1, TILE]]),
                        in_=zst.h[:, :]), [zst], [dz])
        def stageC(ti):
            t0 = ti * TILE
            for c in range(16):
                base = c * (TILE + 3)
                vop(lambda e, base=base, c=c, cb=cbcol: e.tensor_scalar(
                    cvf.h[:, 0:TILE], xproj_t.h[:, base:base + TILE], convw.h[:, 4 * c:4 * c + 1],
                    cb.h[:, c:c + 1], A_.mult, A_.add), [xproj_t, convw, cbcol], [cvf])
                for k in (1, 2, 3):
                    vop(lambda e, base=base, c=c, k=k: e.scalar_tensor_tensor(
                        cvf.h[:, 0:TILE], xproj_t.h[:, base + k:base + k + TILE],
                        convw.h[:, 4 * c + k:4 * c + k + 1], cvf.h[:, 0:TILE], A_.mult, A_.add),
                        [xproj_t, convw, cvf], [cvf])
                sop(lambda e, c=c: e.activation(xconv_t.h[:, c * TILE:(c + 1) * TILE],
                                                cvf.h[:, 0:TILE], AF.Silu), [cvf], [xconv_t],
                    accum=True)
            vop(lambda e: e.tensor_copy(
                bass.AP(xproj_t.h, 0, [[16 * (TILE + 3), 128], [TILE + 3, 16], [1, 3]]),
                bass.AP(xproj_t.h, TILE, [[16 * (TILE + 3), 128], [TILE + 3, 16], [1, 3]])),
                [xproj_t], [xproj_t], accum=True)
            if not prefix:
                B.dma("qg", "dxcs", lambda q, t0=t0: q.dma_start(
                    out=bass.AP(T["xc_sp"].tensor, t0,
                                [[16 * n_own, 128], [n_own, 16], [1, TILE]]),
                    in_=bass.AP(xconv_t.h, 0, [[16 * TILE, 128], [TILE, 16], [1, TILE]])),
                    [xconv_t], [dxc])
        def stageD(ti):
            t0 = ti * TILE
            for kc in range(16):
                top(lambda e, kc=kc: e.matmul(
                    pb[6].h[0:16, :], WxT.h[:, kc * 33:kc * 33 + 16],
                    xconv_t.h[:, kc * TILE:(kc + 1) * TILE],
                    start=(kc == 0), stop=(kc == 15)), [WxT, xconv_t], [pb[6]],
                    accum=(kc > 0))
            for kc in range(16):
                top(lambda e, kc=kc: e.matmul(
                    pb[0].h[0:16, :], WxT.h[:, kc * 33 + 16:kc * 33 + 32],
                    xconv_t.h[:, kc * TILE:(kc + 1) * TILE],
                    start=(kc == 0), stop=(kc == 15)), [WxT, xconv_t], [pb[0]],
                    accum=(kc > 0))
            for kc in range(16):
                top(lambda e, kc=kc: e.matmul(
                    pb[1].h[0:1, :], WxT.h[:, kc * 33 + 32:kc * 33 + 33],
                    xconv_t.h[:, kc * TILE:(kc + 1) * TILE],
                    start=(kc == 0), stop=(kc == 15)), [WxT, xconv_t], [pb[1]],
                    accum=(kc > 0))
            vop(lambda e: e.tensor_copy(xdbl.h[0:16, 0:TILE], pb[6].h[0:16, :]), [pb[6]], [xdbl])
            vop(lambda e: e.tensor_copy(Cbf.h[:, :], pb[0].h[0:16, :]), [pb[0]], [Cbf])
            sop(lambda e: e.activation(cvf.h[0:1, 0:TILE], pb[1].h[0:1, :], AF.Abs),
                [pb[1]], [cvf])
            sop(lambda e: e.activation(cvf.h[0:1, 0:TILE], cvf.h[0:1, 0:TILE], AF.Exp, scale=-1.0),
                [cvf], [cvf], accum=True)
            sop(lambda e: e.activation(cvf.h[0:1, 0:TILE], cvf.h[0:1, 0:TILE], AF.Ln, bias=1.0),
                [cvf], [cvf], accum=True)
            vop(lambda e: e.scalar_tensor_tensor(dtbc.h[0:1, :], pb[1].h[0:1, :], 0.0,
                                                 cvf.h[0:1, 0:TILE], A_.max, A_.add),
                [pb[1], cvf], [dtbc], accum=True)
            vop(lambda e: e.tensor_scalar(dtbc.h[0:1, :], dtbc.h[0:1, :], 0.1, 0.001,
                                          A_.min, A_.max), [dtbc], [dtbc], accum=True)
            top(lambda e: e.matmul(pb[1].h[0:16, :], ones1x16.h[:, :], dtbc.h[0:1, :],
                                   start=True, stop=True), [ones1x16, dtbc], [pb[1]])
            vop(lambda e: e.tensor_copy(dtbc.h[:, :], pb[1].h[0:16, :]), [pb[1]], [dtbc])
            vop(lambda e: e.tensor_scalar(dtA.h[:, :], dtbc.h[:, :], Acol.h[:, 0:1], 0.0,
                                          A_.mult, A_.min), [dtbc, Acol], [dtA])
            vop(lambda e: e.tensor_scalar_max(dtA.h[:, :], dtA.h[:, :], -20.0), [dtA], [dtA])
            vop(lambda e: e.tensor_copy(cumc.h[:, 0:1], carry.h[:, :]), [carry], [cumc])
            vop(lambda e: e.tensor_tensor_scan(
                cumc.h[:, 1:1 + TILE], dtA.h[:, :],
                bass.AP(zcol16.h, 0, [[1, 16], [0, TILE]]), cumc.h[:, 0:1],
                A_.add, A_.add), [dtA, zcol16, cumc], [cumc], accum=True)
            vop(lambda e: e.tensor_copy(carry.h[:, :], cumc.h[:, TILE:TILE + 1]),
                [cumc], [carry])
            vop(lambda e: e.tensor_mul(Bbar.h[:, :], dtbc.h[:, :], xdbl.h[0:16, 0:TILE]),
                [dtbc, xdbl], [Bbar])
            vop(lambda e: e.tensor_scalar(Bbar.h[:, :], Bbar.h[:, :], 10.0, -10.0,
                                          A_.min, A_.max), [Bbar], [Bbar])
            vop(lambda e: e.tensor_sub(
                Bhat.h[:, :].rearrange("p (c q) -> p c q", q=CHUNK),
                cumc.h[:, 1:1 + TILE].rearrange("p (c q) -> p c q", q=CHUNK),
                bass.AP(cumc.h, CHUNK, [[1 + TILE, 16], [CHUNK, CPT], [0, CHUNK]])),
                [cumc], [Bhat])
            sop(lambda e: e.activation(eexp.h[:, :], Bhat.h[:, :], AF.Exp, scale=-1.0),
                [Bhat], [eexp])
            vop(lambda e: e.tensor_mul(Bhat.h[:, :], eexp.h[:, :], Bbar.h[:, :]),
                [eexp, Bbar], [Bhat])
            if not prefix:
                vop(lambda e: e.tensor_copy(Bbarbf.h[:, :], Bbar.h[:, :]), [Bbar], [Bbarbf])
                vop(lambda e: e.tensor_sub(
                    eexp.h[:, :].rearrange("p (c q) -> p c q", q=CHUNK),
                    cumc.h[:, 1:1 + TILE].rearrange("p (c q) -> p c q", q=CHUNK),
                    bass.AP(cumc.h, 0, [[1 + TILE, 16], [CHUNK, CPT], [0, CHUNK]])),
                    [cumc], [eexp])
                sop(lambda e: e.activation(eexp.h[:, :], eexp.h[:, :], AF.Exp), [eexp], [eexp])
                vop(lambda e, t0=t0: e.tensor_mul(Chp_all.h[:, t0:t0 + TILE], eexp.h[:, :],
                                                  Cbf.h[:, :]), [eexp, Cbf],
                    [Chp_all], accum=True)
            for c in range(CPT):
                xt_ = xtm[c % 2]
                for dc in range(16):
                    transpose_to(xt_, dc * 128, xconv_t,
                                 xconv_t.h[:, dc * TILE + c * CHUNK:dc * TILE + (c + 1) * CHUNK],
                                 128)
                gk = (t0 + c * CHUNK) // CHUNK
                transpose_to(BhT, c * 16, Bhat, Bhat.h[:, c * CHUNK:(c + 1) * CHUNK], 16)
                if not prefix:
                    transpose_to(BbT_own, gk * 16, Bbarbf,
                                 Bbarbf.h[:, c * CHUNK:(c + 1) * CHUNK], 16)
                    transpose_to(CT_own, gk * 16, Cbf,
                                 Cbf.h[:, c * CHUNK:(c + 1) * CHUNK], 16)
                    top(lambda e, c=c: e.transpose(
                        pb[6].h[0:128, 0:16], cumc.h[:, 1 + c * CHUNK:1 + (c + 1) * CHUNK],
                        identf.h[0:16, 0:16]), [cumc, identf], [pb[6]])
                    vop(lambda e, gk=gk: e.tensor_copy(
                        cumT_own.h[:, gk * 16:(gk + 1) * 16], pb[6].h[0:128, 0:16]),
                        [pb[6]], [cumT_own], accum=True)
                vop(lambda e, c=c: e.tensor_scalar(
                    dcol.h[0:16, 0:1], cumc.h[:, (c + 1) * CHUNK:(c + 1) * CHUNK + 1],
                    cumc.h[:, c * CHUNK:c * CHUNK + 1], None, A_.subtract), [cumc], [dcol])
                sop(lambda e: e.activation(dcol.h[0:16, 0:1], dcol.h[0:16, 0:1], AF.Exp),
                    [dcol], [dcol])
                vop(lambda e: e.tensor_copy(dcol.h[32:48, 0:1], dcol.h[0:16, 0:1]),
                    [dcol], [dcol], accum=True)
                cu, nx = hcur[0], 1 - hcur[0]
                if not prefix:
                    B.dma("qg", "dhs", lambda q, gk=gk, cu=cu: q.dma_start(
                        out=T["h_sp"][:, gk * D_INNER:(gk + 1) * D_INNER],
                        in_=Hrow.h[HROW[cu]:HROW[cu] + 16, :]), [Hrow], [dh])
                HBANKS = (2, 3, 5, 0)
                for j in range(4):
                    bk = HBANKS[j]
                    top(lambda e, c=c, j=j, xt_=xt_, bk=bk: e.matmul(
                        pb[bk].h[0:16, :], BhT.h[:, c * 16:(c + 1) * 16],
                        xt_.h[:, j * 512:(j + 1) * 512], start=True, stop=True),
                        [BhT, xt_], [pb[bk]])
                for j in range(4):
                    bk = HBANKS[j]
                    vop(lambda e, j=j, cu=cu, nx=nx, bk=bk: e.scalar_tensor_tensor(
                        Hrow.h[HROW[nx]:HROW[nx] + 16, j * 512:(j + 1) * 512],
                        Hrow.h[HROW[cu]:HROW[cu] + 16, j * 512:(j + 1) * 512],
                        dcol.h[HROW[cu]:HROW[cu] + 16, 0:1], pb[bk].h[0:16, :],
                        A_.mult, A_.add),
                        [Hrow, dcol, pb[bk]], [Hrow], accum=True)
                hcur[0] = nx


        stageA(0)
        for ti in range(ntiles):
            stageB(ti)
            if ti + 1 < ntiles:
                stageA(ti + 1)
            stageC(ti)
            stageD(ti)
    if NT_PRE > 0:
        load_transpose_bf16(T["pre_Win"], D_INNER, D_MODEL, 0, D_INNER, "dwl")
        mixer_tiles(prefix=True)
    load_transpose_bf16(T["W_in"], 2 * D_INNER, D_MODEL, 0, 2 * D_INNER, "dwl")
    mixer_tiles(prefix=False)

    xq1g = xnbf
    k1g = sqcol
    k1u = rstd

    xq1gp = [Buf(xnfm.h, "xq1ga"), Buf(xnfm.h, "xq1gb")]

    def ffn_half(hf):
        wqst = sbf[1]
        for wi, wt in ((0, T["w_gate"]), (1, T["w_up"])):
            for rb in range(16):
                row0 = hf * 2048 + rb * 128
                stg = scr[0] if rb % 2 == 0 else scr[2]
                B.dma("qs", "dwf%d" % (rb % 2), lambda q, row0=row0, wt=wt, stg=stg:
                      q.dma_start(out=stg.h[:, 0:D_MODEL],
                                  in_=bass.AP(wt.tensor, row0 * D_MODEL,
                                              [[D_MODEL, 128], [1, D_MODEL]])), [], [stg])
                vop(lambda e, stg=stg, wi=wi: e.tensor_scalar(
                    lad_d.h[:, 0:D_MODEL], stg.h[:, 0:D_MODEL], thrp[wi].h[:, 0:1], None,
                    A_.is_ge), [stg, thrp[wi]], [lad_d])
                vop(lambda e, stg=stg, wi=wi: e.scalar_tensor_tensor(
                    wqst.h[:, 0:D_MODEL], stg.h[:, 0:D_MODEL], thrn[wi].h[:, 0:1],
                    lad_d.h[:, 0:D_MODEL], A_.is_le, A_.subtract),
                    [stg, thrn[wi], lad_d], [wqst])
                for kb in range(8):
                    transpose_to(wpool, wi * 16384 + kb * 2048 + rb * 128, wqst,
                                 wqst.h[:, kb * 128:(kb + 1) * 128], 128)
        def xq1_load(g):
            par = g % 2
            extra = [xnfm] if (hf == 0 and g == 0) else []
            B.dma("qs", "dq1r%d" % par, lambda q, g=g, par=par: q.dma_start(
                out=bass.AP(xnfm.h, par * 1024, [[4096, 128], [128, 8], [1, 128]]),
                in_=bass.AP(T["xq1_sp"].tensor, g * 128,
                            [[8 * n_own, 128], [n_own, 8], [1, 128]])),
                [dq1], [xq1gp[par]] + extra)

        xq1_load(0)
        pend = [None]
        for g in range(NG):
            if g + 1 < NG:
                xq1_load(g + 1)
            xq1g_ap = lambda kc, g=g: bass.AP(
                xnfm.h, (g % 2) * 1024 + kc * 128, [[4096, 128], [1, 128]])
            xq1g_buf = xq1gp[g % 2]
            vop(lambda e, g=g: e.tensor_mul(k1g.h[:, 0:1], swneg[0].h[:, 0:1],
                                            c1_all.h[:, g:g + 1]), [swneg[0], c1_all], [k1g])
            vop(lambda e, g=g: e.tensor_mul(k1u.h[:, 0:1], swneg[1].h[:, 0:1],
                                            c1_all.h[:, g:g + 1]), [swneg[1], c1_all], [k1u])
            for nh in range(4):
                for kc in range(8):
                    top(lambda e, nh=nh, kc=kc, a=xq1g_ap: e.matmul(
                        pb[0].h[:, :], a(kc),
                        wpool.h[:, kc * 2048 + nh * 512:kc * 2048 + (nh + 1) * 512],
                        start=(kc == 0), stop=(kc == 7)), [xq1g_buf, wpool], [pb[0]],
                        accum=(kc > 0))
                sop(lambda e: e.activation(sgb.h[:, :], pb[0].h[:, :], AF.Silu,
                                           scale=k1g.h[:, 0:1]), [pb[0], k1g], [sgb])
                for kc in range(8):
                    top(lambda e, nh=nh, kc=kc, a=xq1g_ap: e.matmul(
                        pb[1].h[:, :], a(kc),
                        wpool.h[:, 16384 + kc * 2048 + nh * 512:
                                16384 + kc * 2048 + (nh + 1) * 512],
                        start=(kc == 0), stop=(kc == 7)), [xq1g_buf, wpool], [pb[1]],
                        accum=(kc > 0))
                sop(lambda e: e.activation(upb.h[:, :], pb[1].h[:, :], AF.Identity,
                                           scale=k1u.h[:, 0:1]), [pb[1], k1u], [upb])
                vop(lambda e, nh=nh: e.tensor_mul(th_a.h[:, nh * 512:(nh + 1) * 512],
                                                  sgb.h[:, :], upb.h[:, :]), [sgb, upb],
                    [th_a], accum=True)
            if hf == 0:
                B.dma("qg", "dt1s", lambda q, g=g: q.dma_start(
                    out=bass.AP(T["t1_sp"].tensor, g * 128,
                                [[16 * n_own, 128], [n_own, 16], [1, 128]]),
                    in_=bass.AP(th_a.h, 0, [[2048, 128], [128, 16], [1, 128]])),
                    [th_a], [dt1])
            else:
                B.dma("qs", "dt1r", lambda q, g=g: q.dma_start(
                    out=bass.AP(th_b.h, 0, [[2048, 128], [128, 16], [1, 128]]),
                    in_=bass.AP(T["t1_sp"].tensor, g * 128,
                                [[16 * n_own, 128], [n_own, 16], [1, 128]])), [dt1], [th_b])
                vop(lambda e: e.reduce_max(amx.h[:, 0:1], th_a.h[:, :], axis=AX.X,
                                           apply_absolute_value=True), [th_a], [amx])
                vop(lambda e: e.reduce_max(m2col.h[:, 0:1], th_b.h[:, :], axis=AX.X,
                                           apply_absolute_value=True), [th_b], [m2col])
                vop(lambda e: e.tensor_max(amx.h[:, 0:1], amx.h[:, 0:1], m2col.h[:, 0:1]),
                    [amx, m2col], [amx])
                vop(lambda e: e.tensor_scalar_max(amx.h[:, 0:1], amx.h[:, 0:1], 1e-5),
                    [amx], [amx])
                vop(lambda e: e.reciprocal(scol.h[:, 0:1], amx.h[:, 0:1]), [amx], [scol])
                vop(lambda e: e.tensor_scalar_mul(scol.h[:, 0:1], scol.h[:, 0:1], 127.0),
                    [scol], [scol])
                vop(lambda e, g=g: e.reciprocal(c2_all.h[:, g:g + 1], scol.h[:, 0:1]),
                    [scol], [c2_all], accum=True)
                if pend[0] is not None:
                    pend[0]()
                for half2, tb in ((0, th_b), (1, th_a)):
                    for co in (0, 1024):
                        quant_cast(lambda tb=tb, co=co: tb.h[:, co:co + 1024], [tb],
                                   1024, scol, (sbf[half2], co), (scr[half2], co),
                                   (wb16[0], co), (wb16[1], co), fp16_src=True)

                def mk_pend(g=g):
                    def flush():
                        for half2 in (0, 1):
                            for kc in range(16):
                                transpose_to(wb16[2], kc * 128, sbf[half2],
                                             sbf[half2].h[:, kc * 128:(kc + 1) * 128],
                                             128)
                            B.dma("qg", "dxq2", lambda q, half2=half2, g=g: q.dma_start(
                                out=bass.AP(T["xq2_sp"].tensor,
                                            half2 * 16 * n_own + g * 128,
                                            [[32 * n_own, 128], [n_own, 16], [1, 128]]),
                                in_=bass.AP(wb16[2].h, 0, [[2048, 128], [128, 16],
                                                           [1, 128]])),
                                [wb16[2]], [dq2])
                    return flush
                pend[0] = mk_pend()

        if hf == 1 and pend[0] is not None:
            pend[0]()

    ffn_half(0)
    ffn_half(1)

    load_transpose_bf16(T["W_out"], D_MODEL, D_INNER, 0, D_MODEL, "dwl")

    wpar = Buf(wpool.h, "wpar")
    XCK = [(xnfm, 0), (wpar, 16384)]
    ZK = [(xnfm, 2048), (wpar, 18432)]
    HK = [(xconv_t, 0), (wpar, 20480)]
    SSL = [(xconv_t, 2048), (wpar, 22528)]
    ESC = [(wb16[0], 0), (wpar, 22656)]
    CBC = [(wb16[1], 0), (wpar, 24704)]
    YS = [(wb16[2], 0), (wpar, 26752)]
    XTMK = [(xtm[0], 0), (wpar, 28800)]
    XROW = [xg[0], xg[1]]
    DARG = [scr[0], scr[2]]
    cumbc = scr[1]
    xat = xn32

    def p4_xrow(k):
        par = k % 2
        B.dma("qs", "dxr%d" % par, lambda q, k=k, par=par: q.dma_start(
            out=XROW[par].h[:, :],
            in_=bass.AP(T["x_own"].tensor, k * 128 * D_MODEL,
                        [[D_MODEL, 128], [1, D_MODEL]])), [], [XROW[par]])

    def p4_loads(k):
        par = k % 2
        xb, xo = XCK[par]
        zb, zo = ZK[par]
        hb, ho = HK[par]
        db_ = DARG[par]
        extra = [wpool] if k == 1 else (xq1gp if k == 0 else [])
        B.dma("qs", "dxck%d" % par, lambda q, k=k, xb=xb, xo=xo: q.dma_start(
            out=bass.AP(xb.h, xo, [[xb.h.shape[1], 128], [128, 16], [1, 128]]),
            in_=bass.AP(T["xc_sp"].tensor, k * 128,
                        [[16 * n_own, 128], [n_own, 16], [1, 128]])), [dxc], [xb] + extra)
        B.dma("qs", "dzk%d" % par, lambda q, k=k, zb=zb, zo=zo: q.dma_start(
            out=bass.AP(zb.h, zo, [[zb.h.shape[1], 128], [128, 16], [1, 128]]),
            in_=bass.AP(T["z_sp"].tensor, k * 128,
                        [[16 * n_own, 128], [n_own, 16], [1, 128]])), [dz], [zb])
        B.dma("qs", "dhk%d" % par, lambda q, k=k, hb=hb, ho=ho: q.dma_start(
            out=bass.AP(hb.h, ho, [[hb.h.shape[1], 16], [1, D_INNER]]),
            in_=T["h_sp"][:, k * D_INNER:(k + 1) * D_INNER]), [dh], [hb])
        B.dma("qg", "dcf%d" % par, lambda q, k=k, db_=db_: q.dma_start(
            out=db_.h[0:1, 0:2048],
            in_=bass.AP(cumT_own.h, k * 16, [[NCH * 16, 128], [1, 16]])),
            [cumT_own], [db_])
        B.dma("qg", "dcb%d" % par, lambda q, db_=db_: q.dma_start(
            out=cumbc.h[:, :],
            in_=bass.AP(db_.h, 0, [[1, 1], [0, 128], [1, 2048]])),
            [db_], [cumbc])
        B.dma("qg", "dclf%d" % par, lambda q, k=k: q.dma_start(
            out=sbf[0].h[0:1, 0:2048],
            in_=bass.AP(CT_own.h, k * 16, [[NCH * 16, 128], [1, 16]])),
            [CT_own], [sbf[0]])
        cb, co = CBC[par]
        B.dma("qg", "dclb%d" % par, lambda q, cb=cb, co=co: q.dma_start(
            out=bass.AP(cb.h, co, [[cb.h.shape[1], 128], [1, 2048]]),
            in_=bass.AP(sbf[0].h, 0, [[1, 1], [0, 128], [1, 2048]])), [sbf[0]], [cb])

    def p4_scores(k):
        par = k % 2
        db_ = DARG[par]
        eb, eo = ESC[par]
        cb, co = CBC[par]
        sb, so = SSL[par]
        vop(lambda e, k=k, db_=db_: e.tensor_sub(
            db_.h[:, :].rearrange("p (j s) -> p j s", s=16),
            cumbc.h[:, :].rearrange("p (j s) -> p j s", s=16),
            cumT_own.h[:, k * 16:(k + 1) * 16][:, None, :].broadcast_to([128, 128, 16])),
            [cumbc, cumT_own], [db_], accum=True)
        vop(lambda e, db_=db_: e.tensor_sub(
            db_.h[:, :].rearrange("p (j s) -> p j s", s=16),
            db_.h[:, :].rearrange("p (j s) -> p j s", s=16),
            maskb.h[:, :, None].broadcast_to([128, 128, 16])), [db_, maskb], [db_],
            accum=True)
        sop(lambda e, db_=db_, eb=eb, eo=eo: e.activation(
            eb.h[:, eo:eo + 2048], db_.h[:, :], AF.Exp), [db_], [eb])
        vop(lambda e, k=k, eb=eb, eo=eo: e.tensor_mul(
            bass.AP(eb.h, eo, [[eb.h.shape[1], 128], [16, 128], [1, 16]]),
            bass.AP(eb.h, eo, [[eb.h.shape[1], 128], [16, 128], [1, 16]]),
            BbT_own.h[:, k * 16:(k + 1) * 16][:, None, :].broadcast_to([128, 128, 16])),
            [eb, BbT_own], [eb], accum=True)
        vop(lambda e, eb=eb, eo=eo, cb=cb, co=co: e.tensor_mul(
            eb.h[:, eo:eo + 2048], eb.h[:, eo:eo + 2048], cb.h[:, co:co + 2048]),
            [eb, cb], [eb], accum=True)

        def _score_reduce(e, eb=eb, eo=eo, sb=sb, so=so):
            with B.nc.allow_low_precision(reason="bf16 score tile for bf16 matmul"):
                return e.reduce_sum(
                    sb.h[0:128, so:so + 128],
                    bass.AP(eb.h, eo, [[eb.h.shape[1], 128], [16, 128], [1, 16]]),
                    axis=AX.X)
        vop(_score_reduce, [eb], [sb], accum=True)

    def p4_xtm(k):
        par = k % 2
        xb, xo = XCK[par]
        tb, to = XTMK[par]
        for dc in range(16):
            transpose_to(tb, to + dc * 128, xb,
                         bass.AP(xb.h, xo + dc * 128, [[xb.h.shape[1], 128], [1, 128]]),
                         128, evac="se")

    def p4_y(k):
        par = k % 2
        tb, to = XTMK[par]
        hb, ho = HK[par]
        sb, so = SSL[par]
        yb, yo = YS[par]
        for dc in range(16):
            bank = 2 + dc % 2
            top(lambda e, dc=dc, bank=bank, tb=tb, to=to, sb=sb, so=so: e.matmul(
                pb[bank].h[:, 0:128],
                bass.AP(tb.h, to + dc * 128, [[tb.h.shape[1], 128], [1, 128]]),
                sb.h[0:128, so:so + 128], start=True, stop=False),
                [tb, sb], [pb[bank]])
            top(lambda e, dc=dc, bank=bank, k=k, hb=hb, ho=ho: e.matmul(
                pb[bank].h[:, 0:128],
                bass.AP(hb.h, ho + dc * 128, [[hb.h.shape[1], 16], [1, 128]]),
                Chp_all.h[:, k * 128:(k + 1) * 128], start=False, stop=True),
                [hb, Chp_all], [pb[bank]], accum=True)
            vop(lambda e, dc=dc, bank=bank, yb=yb, yo=yo: e.tensor_copy(
                yb.h[:, yo + dc * 128:yo + (dc + 1) * 128], pb[bank].h[:, 0:128]),
                [pb[bank]], [yb], accum=True)

    def p4_ypost(k):
        par = k % 2
        xb, xo = XCK[par]
        zb, zo = ZK[par]
        eb, eo = ESC[par]
        cb, co = CBC[par]
        yb, yo = YS[par]
        sop(lambda e, zb=zb, zo=zo, cb=cb, co=co: e.activation(
            cb.h[:, co:co + 2048], zb.h[:, zo:zo + 2048], AF.Silu), [zb], [cb],
            accum=True)
        vop(lambda e, yb=yb, yo=yo, cb=cb, co=co: e.tensor_mul(
            yb.h[:, yo:yo + 2048], yb.h[:, yo:yo + 2048], cb.h[:, co:co + 2048]),
            [yb, cb], [yb], accum=True)
        vop(lambda e, xb=xb, xo=xo, eb=eb, eo=eo: e.tensor_mul(
            bass.AP(eb.h, eo, [[eb.h.shape[1], 128], [128, 16], [1, 128]]),
            bass.AP(xb.h, xo, [[xb.h.shape[1], 128], [128, 16], [1, 128]]),
            Dp.h[:, :, None].broadcast_to([128, 16, 128])), [xb, Dp], [eb], accum=True)
        vop(lambda e, yb=yb, yo=yo, eb=eb, eo=eo: e.tensor_add(
            yb.h[:, yo:yo + 2048], yb.h[:, yo:yo + 2048], eb.h[:, eo:eo + 2048]),
            [yb, eb], [yb], accum=True)

    def p4_attn(k):
        par = k % 2
        yb, yo = YS[par]
        for nh in range(2):
            for dc in range(16):
                top(lambda e, nh=nh, dc=dc, yb=yb, yo=yo: e.matmul(
                    pb[nh].h[:, :],
                    bass.AP(yb.h, yo + dc * 128, [[yb.h.shape[1], 128], [1, 128]]),
                    wpool.h[:, dc * D_MODEL + nh * 512:dc * D_MODEL + (nh + 1) * 512],
                    start=(dc == 0), stop=(dc == 15)), [yb, wpool], [pb[nh]],
                    accum=(dc != 0))
        for nh in range(2):
            vop(lambda e, nh=nh, par=par: e.tensor_add(
                xat.h[:, nh * 512:(nh + 1) * 512],
                XROW[par].h[:, nh * 512:(nh + 1) * 512],
                pb[nh].h[:, :]), [XROW[par], pb[nh]], [xat], accum=True)
        B.dma("qg", "dxats", lambda q, k=k: q.dma_start(
            out=bass.AP(T["xat_sp"].tensor, k * 128,
                        [[8 * n_own, 128], [n_own, 8], [1, 128]]),
            in_=bass.AP(xat.h, 0, [[1024, 128], [128, 8], [1, 128]])), [xat], [dxat])

    p4_loads(0)
    p4_xrow(0)
    p4_scores(0)
    p4_xtm(0)
    if NCH > 1:
        p4_loads(1)
        p4_xrow(1)
    for k in range(NCH):
        p4_y(k)
        p4_ypost(k)
        if k + 1 < NCH:
            p4_scores(k + 1)
            p4_xtm(k + 1)
        if k + 2 < NCH:
            p4_loads(k + 2)
        p4_attn(k)
        if k + 2 < NCH:
            p4_xrow(k + 2)

    wqst = sbf[1]
    for rb in range(D_MODEL // 128):
        for sg in range(2):
            stg = scr[0] if sg == 0 else scr[2]
            extra = [wpar] if rb == 0 else []
            B.dma("qs", "dwf%d" % sg, lambda q, rb=rb, sg=sg, stg=stg: q.dma_start(
                out=stg.h[:, 0:2048],
                in_=bass.AP(T["w_down"].tensor, rb * 128 * FFN + sg * 2048,
                            [[FFN, 128], [1, 2048]])), [], [stg] + extra)
            vop(lambda e, stg=stg: e.tensor_scalar(
                lad_d.h[:, 0:2048], stg.h[:, 0:2048], thrp[2].h[:, 0:1], None, A_.is_ge),
                [stg, thrp[2]], [lad_d])
            vop(lambda e, stg=stg: e.scalar_tensor_tensor(
                wqst.h[:, 0:2048], stg.h[:, 0:2048], thrn[2].h[:, 0:1],
                lad_d.h[:, 0:2048], A_.is_le, A_.subtract), [stg, thrn[2], lad_d], [wqst])
            for kb in range(16):
                transpose_to(wpool, (sg * 16 + kb) * D_MODEL + rb * 128, wqst,
                             wqst.h[:, kb * 128:(kb + 1) * 128], 128)

    xatr = [xg[0]] * 2
    outb = [xn32] * 2
    k2c = nmr

    for k in range(NCH):
        par = k % 2
        B.dma("qs", "dq2r%d" % par, lambda q, k=k, par=par: q.dma_start(
            out=bass.AP(wb16[0].h, 0, [[2048, 128], [128, 16], [1, 128]]),
            in_=bass.AP(T["xq2_sp"].tensor, k * 128,
                        [[32 * n_own, 128], [n_own, 16], [1, 128]])), [dq2], [wb16[0]])
        B.dma("qs", "dq2r%d" % par, lambda q, k=k, par=par: q.dma_start(
            out=bass.AP(wb16[1].h, 0, [[2048, 128], [128, 16], [1, 128]]),
            in_=bass.AP(T["xq2_sp"].tensor, 16 * n_own + k * 128,
                        [[32 * n_own, 128], [n_own, 16], [1, 128]])), [dq2], [wb16[1]])
        B.dma("qs", "dxatr%d" % par, lambda q, k=k, par=par: q.dma_start(
            out=bass.AP(xatr[par].h, 0, [[1024, 128], [128, 8], [1, 128]]),
            in_=bass.AP(T["xat_sp"].tensor, k * 128,
                        [[8 * n_own, 128], [n_own, 8], [1, 128]])), [dxat], [xatr[par]])
        for nh in range(2):
            for kc in range(32):
                src = wb16[kc // 16]
                top(lambda e, nh=nh, kc=kc, src=src: e.matmul(
                    pb[nh].h[:, :], src.h[:, (kc % 16) * 128:((kc % 16) + 1) * 128],
                    wpool.h[:, kc * D_MODEL + nh * 512:kc * D_MODEL + (nh + 1) * 512],
                    start=(kc == 0), stop=(kc == 31)), [src, wpool], [pb[nh]],
                    accum=(kc != 0))
        vop(lambda e, k=k: e.tensor_mul(k2c.h[:, 0:1], swneg[2].h[:, 0:1],
                                        c2_all.h[:, k:k + 1]), [swneg[2], c2_all], [k2c])
        ob = outb[par]
        for nh in range(2):
            vop(lambda e, nh=nh, ob=ob, par=par: e.scalar_tensor_tensor(
                ob.h[:, nh * 512:(nh + 1) * 512], pb[nh].h[:, :], k2c.h[:, 0:1],
                xatr[par].h[:, nh * 512:(nh + 1) * 512], A_.mult, A_.add),
                [pb[nh], k2c, xatr[par]], [ob], accum=True)
        B.dma("qg", "dout", lambda q, k=k, ob=ob: q.dma_start(
            out=bass.AP(T["out"].tensor, k * 128 * D_MODEL, [[D_MODEL, 128], [1, D_MODEL]]),
            in_=ob.h[:, :]), [ob], [])

    B.wait_stream("qs", "dout")


_NC_CACHE = {}


def _get_nc(n_own, n_pre):
    key = (n_own, n_pre)
    if key not in _NC_CACHE:
        _NC_CACHE[key] = build_core_program(n_own, n_pre)
    return _NC_CACHE[key]


def kernel(L0_attn_x, L0_attn_W_in, L0_attn_conv_w, L0_attn_conv_b, L0_attn_W_xproj,
           L0_attn_A_log, L0_attn_D_param, L0_attn_W_out, L0_ffn_w_gate, L0_ffn_w_up,
           L0_ffn_w_down):
    x = np.ascontiguousarray(L0_attn_x, dtype=np.float32)
    b_, s_, d_ = x.shape
    half = s_ // 2
    nc = _get_nc(half, half)
    shared = {
        "W_in": np.ascontiguousarray(L0_attn_W_in, np.float32),
        "conv_w": np.ascontiguousarray(L0_attn_conv_w, np.float32),
        "conv_b": np.ascontiguousarray(L0_attn_conv_b, np.float32),
        "W_xproj": np.ascontiguousarray(L0_attn_W_xproj, np.float32),
        "A_log": np.ascontiguousarray(L0_attn_A_log, np.float32),
        "D_param": np.ascontiguousarray(L0_attn_D_param, np.float32),
        "W_out": np.ascontiguousarray(L0_attn_W_out, np.float32),
        "w_gate": np.ascontiguousarray(L0_ffn_w_gate, np.float32),
        "w_up": np.ascontiguousarray(L0_ffn_w_up, np.float32),
        "w_down": np.ascontiguousarray(L0_ffn_w_down, np.float32),
        "ident": np.eye(128, dtype=bfdt),
        "identf": np.eye(16, dtype=np.float32),
        "maskbig": np.ascontiguousarray(
            (np.triu(np.ones((128, 128), np.float32), 1).T * MASK_BIG).astype(bfdt)),
    }
    zwin = np.zeros((D_INNER, D_MODEL), np.float32)
    zcb = np.zeros((D_INNER,), np.float32)
    zx = np.zeros((half, D_MODEL), np.float32)
    in_maps = []
    for c in range(8):
        b = c // 2
        h = c % 2
        m = dict(shared)
        m["x_own"] = np.ascontiguousarray(x[b, h * half:(h + 1) * half])
        if h == 1:
            m["x_pre"] = np.ascontiguousarray(x[b, 0:half])
            m["pre_Win"] = shared["W_in"][0:D_INNER].copy()
            m["pre_conv_b"] = shared["conv_b"].copy()
        else:
            m["x_pre"] = zx
            m["pre_Win"] = zwin
            m["pre_conv_b"] = zcb
        in_maps.append(m)
    res = run_bass_kernel_spmd(nc, in_maps, core_ids=list(range(8)))
    out = np.empty((b_, s_, d_), np.float32)
    for c in range(8):
        b = c // 2
        h = c % 2
        out[b, h * half:(h + 1) * half] = np.asarray(res.results[c]["out"]).reshape(half, d_)
    return out


# version 14
# speedup vs baseline: 1.0140x; 1.0140x over previous
import contextlib
import numpy as np
import ml_dtypes
import concourse.bass as bass
import concourse.mybir as mybir
from concourse.bass_utils import run_bass_kernel_spmd

F32 = mybir.dt.float32
BF16 = mybir.dt.bfloat16
FP16 = mybir.dt.float16
I8 = mybir.dt.int8
A_ = mybir.AluOpType
AF = mybir.ActivationFunctionType
AX = mybir.AxisListType
bfdt = ml_dtypes.bfloat16

D_MODEL = 1024
D_INNER = 2048
D_STATE = 16
FFN = 4096
TILE = 512
CHUNK = 128
MASK_BIG = 32768.0
LVLS = [128.0, 64.0, 32.0, 16.0, 8.0, 4.0, 2.0, 1.0]


class Buf:
    __slots__ = ("h", "writers", "readers", "name")

    def __init__(self, h, name):
        self.h = h
        self.name = name
        self.writers = {}
        self.readers = {}


class Builder:
    ENGINES = ("te", "ve", "se", "qs", "qg")

    def __init__(self, nc, stack):
        self.nc = nc
        self.stack = stack
        self.items = {e: [] for e in self.ENGINES}
        self.cnt = {e: 0 for e in self.ENGINES}
        self.sems = {e: stack.enter_context(nc.semaphore("sem_" + e)) for e in self.ENGINES}
        self.extra = {}
        self.waited = {e: {} for e in self.ENGINES}

    def sbuf(self, name, shape, dtype):
        return Buf(self.stack.enter_context(self.nc.sbuf_tensor(name, shape, dtype)), name)

    def psum(self, name, shape, dtype=F32):
        return Buf(self.stack.enter_context(self.nc.psum_tensor(name, shape, dtype)), name)

    def _sem_of(self, p):
        return self.sems[p] if p in self.sems else self.extra[p][0]

    def _wait(self, eng, p, v):
        w = self.waited[eng]
        if w.get(p, -1) >= v:
            return
        w[p] = v
        self.items[eng].append(("w", self._sem_of(p), v))

    def _edges(self, eng, ins, outs, accum):
        need = {}

        def add(p, v):
            if v > need.get(p, -1):
                need[p] = v

        for b in ins:
            if b is not None:
                for p, v in b.writers.items():
                    add(p, v)
        for b in outs:
            if b is None:
                continue
            for p, v in b.writers.items():
                add(p, v)
            for p, v in b.readers.items():
                add(p, v)
        for p in sorted(need):
            v = self.extra[p][1] if p in self.extra else need[p]
            self._wait(eng, p, v)

    def op(self, eng, fn, ins, outs, accum=False):
        self._edges(eng, ins, outs, accum)
        self.cnt[eng] += 1
        v = self.cnt[eng]
        self.items[eng].append(("o", fn, self.sems[eng], 1))
        for b in ins:
            if b is not None:
                b.readers[eng] = v
        for b in outs:
            if b is not None:
                b.writers[eng] = v
        return v

    def dma(self, queue, sem_name, fn, ins, outs):
        self._edges(queue, ins, outs, False)
        if sem_name not in self.extra:
            self.extra[sem_name] = [self.stack.enter_context(self.nc.semaphore("s_" + sem_name)), 0]
        ent = self.extra[sem_name]
        ent[1] += 16
        v = ent[1]
        self.items[queue].append(("o", fn, ent[0], 16))
        for b in ins:
            if b is not None:
                b.readers[sem_name] = v
        for b in outs:
            if b is not None:
                b.writers[sem_name] = v
        return v

    def wait_stream(self, eng, p):
        v = self.cnt[p] if p in self.sems else (self.extra[p][1] if p in self.extra else 0)
        if v > 0:
            self._wait(eng, p, v)

    def build_block(self):
        block = self.stack.enter_context(self.nc.Block())

        def make(name):
            seq = self.items[name]

            def body(eng):
                for it in seq:
                    if it[0] == "w":
                        eng.wait_ge(it[1], it[2])
                    else:
                        it[1](eng).then_inc(it[2], it[3])
            return body

        block.tensor(make("te"))
        block.vector(make("ve"))
        block.scalar(make("se"))
        block.sync(make("qs"))
        block.gpsimd(make("qg"))


def build_core_program(n_own, n_pre):
    NCH = n_own // CHUNK
    nc = bass.Bass("TRN2", target_bir_lowering=False)
    DI = nc.dram_tensor
    T = {}
    T["x_own"] = DI("x_own", [n_own, D_MODEL], F32, kind="ExternalInput").ap()
    T["x_pre"] = DI("x_pre", [max(n_pre, 1), D_MODEL], F32, kind="ExternalInput").ap()
    T["W_in"] = DI("W_in", [2 * D_INNER, D_MODEL], F32, kind="ExternalInput").ap()
    T["pre_Win"] = DI("pre_Win", [D_INNER, D_MODEL], F32, kind="ExternalInput").ap()
    T["conv_w"] = DI("conv_w", [D_INNER, 4], F32, kind="ExternalInput").ap()
    T["conv_b"] = DI("conv_b", [D_INNER], F32, kind="ExternalInput").ap()
    T["pre_conv_b"] = DI("pre_conv_b", [D_INNER], F32, kind="ExternalInput").ap()
    T["W_xproj"] = DI("W_xproj", [2 * D_STATE + 1, D_INNER], F32, kind="ExternalInput").ap()
    T["A_log"] = DI("A_log", [D_STATE], F32, kind="ExternalInput").ap()
    T["D_param"] = DI("D_param", [D_INNER], F32, kind="ExternalInput").ap()
    T["W_out"] = DI("W_out", [D_MODEL, D_INNER], F32, kind="ExternalInput").ap()
    T["w_gate"] = DI("w_gate", [FFN, D_MODEL], F32, kind="ExternalInput").ap()
    T["w_up"] = DI("w_up", [FFN, D_MODEL], F32, kind="ExternalInput").ap()
    T["w_down"] = DI("w_down", [D_MODEL, FFN], F32, kind="ExternalInput").ap()
    T["ident"] = DI("ident", [128, 128], BF16, kind="ExternalInput").ap()
    T["identf"] = DI("identf", [16, 16], F32, kind="ExternalInput").ap()
    T["maskbig"] = DI("maskbig", [128, 128], BF16, kind="ExternalInput").ap()
    T["out"] = DI("out", [n_own, D_MODEL], F32, kind="ExternalOutput").ap()
    T["z_sp"] = DI("z_sp", [128, 16, n_own], BF16).ap()
    T["xc_sp"] = DI("xc_sp", [128, 16, n_own], BF16).ap()
    T["xq1_sp"] = DI("xq1_sp", [128, 8, n_own], BF16).ap()
    T["xq2_sp"] = DI("xq2_sp", [128, 32, n_own], BF16).ap()
    T["t1_sp"] = DI("t1_sp", [128, 16, n_own], FP16).ap()
    T["xat_sp"] = DI("xat_sp", [128, 8, n_own], F32).ap()
    T["h_sp"] = DI("h_sp", [D_STATE, NCH * D_INNER], BF16).ap()

    with contextlib.ExitStack() as stack:
        B = Builder(nc, stack)
        _emit(B, T, n_own, n_pre)
        B.build_block()
    return nc


def _emit(B, T, n_own, n_pre):
    NCH = n_own // CHUNK
    NT_OWN = n_own // TILE
    NT_PRE = n_pre // TILE
    CPT = TILE // CHUNK
    NG = n_own // 128
    GPT = TILE // 128

    def vop(fn, ins, outs, accum=False):
        B.op("ve", fn, ins, outs, accum)

    def sop(fn, ins, outs, accum=False):
        B.op("se", fn, ins, outs, accum)

    def top(fn, ins, outs, accum=False):
        B.op("te", fn, ins, outs, accum)

    dz = Buf(None, "z_sp")
    dxc = Buf(None, "xc_sp")
    dq1 = Buf(None, "xq1_sp")
    dq2 = Buf(None, "xq2_sp")
    dt1 = Buf(None, "t1_sp")
    dxat = Buf(None, "xat_sp")
    dh = Buf(None, "h_sp")

    wpool = B.sbuf("wpool", [128, 32768], BF16)
    scr = [B.sbuf("scr%d" % i, [128, 2048], F32) for i in range(3)]
    sbf = [B.sbuf("sbf%d" % i, [128, 2048], BF16) for i in range(2)]
    wb16 = [B.sbuf("wb16_%d" % i, [128, 2048], BF16) for i in range(3)]
    WxT = B.sbuf("WxT", [128, 16 * 33], BF16)
    convw = B.sbuf("convw", [128, 16 * 4], F32)
    convb = B.sbuf("convb", [128, 16], F32)
    pconvb = B.sbuf("pconvb", [128, 16], F32)
    Dp = B.sbuf("Dp", [128, 16], F32)
    Acol = B.sbuf("Acol", [16, 1], F32)
    ident = B.sbuf("id16", [128, 128], BF16)
    identf = B.sbuf("idf32", [16, 16], F32)
    maskb = B.sbuf("maskb", [128, 128], BF16)
    ones1x16 = B.sbuf("ones1x16", [1, 16], F32)
    screp = B.sbuf("screp", [128, 2], F32)
    onesc = B.sbuf("onesc", [128, 1], F32)
    zcol16 = B.sbuf("zcol16", [16, 1], F32)
    cumT_own = B.sbuf("cumT_own", [128, NCH * 16], F32)
    BbT_own = B.sbuf("BbT_own", [128, NCH * 16], BF16)
    CT_own = B.sbuf("CT_own", [128, NCH * 16], BF16)
    Chp_all = B.sbuf("Chp_all", [16, n_own], BF16)
    c1_all = B.sbuf("c1_all", [128, NG], F32)
    c2_all = B.sbuf("c2_all", [128, NG], F32)
    swneg = [B.sbuf("swneg%d" % i, [128, 1], F32) for i in range(3)]
    thrp = [B.sbuf("thrp%d" % i, [128, 1], F32) for i in range(3)]
    thrn = [B.sbuf("thrn%d" % i, [128, 1], F32) for i in range(3)]
    cumc = B.sbuf("cumc", [16, 1 + TILE], F32)
    carry = B.sbuf("carry", [16, 1], F32)
    epscol = B.sbuf("epscol", [128, 1], F32)
    Hrow = scr[2]
    HROW = [0, 32]

    pb = [B.psum("pb%d" % i, [128, 512], F32) for i in range(6)]
    pb.append(pb[4])
    pt = [B.psum("pt%d" % i, [128, 512], BF16) for i in range(2)]
    PTOFF = [0, 0]

    B.dma("qs", "dsm", lambda q: q.dma_start(
        out=bass.AP(convw.h, 0, [[64, 128], [4, 16], [1, 4]]),
        in_=bass.AP(T["conv_w"].tensor, 0, [[4, 128], [512, 16], [1, 4]])), [], [convw])
    _pvec_loads = []
    for _nm, _dst in (("conv_b", convb), ("pre_conv_b", pconvb), ("D_param", Dp)):
        _pvec_loads.append((_nm, _dst))
    B.dma("qs", "dsm", lambda q: q.dma_start(out=Acol.h[:, 0:1], in_=T["A_log"][:, None]),
          [], [Acol])
    B.dma("qs", "dsm", lambda q: q.dma_start(out=ident.h[:, :], in_=T["ident"]), [], [ident])
    B.dma("qs", "dsm", lambda q: q.dma_start(out=identf.h[:, :], in_=T["identf"]), [], [identf])
    B.dma("qs", "dsm", lambda q: q.dma_start(out=maskb.h[:, :], in_=T["maskbig"]), [], [maskb])
    sop(lambda e: e.activation(Acol.h[:, 0:1], Acol.h[:, 0:1], AF.Exp), [Acol], [Acol])
    vop(lambda e: e.tensor_scalar_mul(Acol.h[:, 0:1], Acol.h[:, 0:1], -1.0), [Acol], [Acol])
    for bz, val in ((zcol16, 0.0), (ones1x16, 1.0), (onesc, 1.0)):
        vop(lambda e, bz=bz, val=val: e.memset(bz.h[:, :], val), [], [bz])
    vop(lambda e: e.memset(carry.h[:, :], 0.0), [], [carry])
    vop(lambda e: e.memset(epscol.h[:, :], 1e-5), [], [epscol])
    vop(lambda e: e.memset(Hrow.h[0:16, :], 0.0), [], [Hrow])

    trcnt = [0]

    def transpose_to(dst, dst_col, src_buf, src_ap, n, evac="ve"):
        slot = trcnt[0] % 2
        ps = pt[slot]
        off = PTOFF[slot]
        trcnt[0] += 1
        top(lambda e, ps=ps, src_ap=src_ap, n=n, off=off: e.transpose(
            ps.h[0:128, off:off + n], src_ap, ident.h[0:n, 0:n]),
            [src_buf, ident], [ps])
        if evac == "se":
            sop(lambda e, ps=ps, dst=dst, dst_col=dst_col, n=n, off=off: e.activation(
                dst.h[:, dst_col:dst_col + n], ps.h[0:128, off:off + n], AF.Identity),
                [ps], [dst], accum=True)
        else:
            vop(lambda e, ps=ps, dst=dst, dst_col=dst_col, n=n, off=off: e.tensor_copy(
                dst.h[:, dst_col:dst_col + n], ps.h[0:128, off:off + n]), [ps], [dst],
                accum=True)

    def load_transpose_bf16(src_t, rows, cols, dst_off, dst_span, tag):
        for rb in range(rows // 128):
            stg = sbf[rb % 2]
            B.dma("qg", tag + str(rb % 2), lambda q, rb=rb, stg=stg, cols=cols, src_t=src_t:
                  q.dma_start(out=stg.h[:, 0:cols],
                              in_=bass.AP(src_t.tensor, rb * 128 * cols,
                                          [[cols, 128], [1, cols]])), [], [stg])
            for kb in range(cols // 128):
                transpose_to(wpool, dst_off + kb * dst_span + rb * 128, stg,
                             stg.h[:, kb * 128:(kb + 1) * 128], 128)

    wx_stage = sbf[1]
    B.dma("qg", "dwx", lambda q: q.dma_start(out=wx_stage.h[0:33, :], in_=T["W_xproj"]),
          [], [wx_stage])
    for kb in range(16):
        transpose_to(WxT, kb * 33, wx_stage, wx_stage.h[0:33, kb * 128:(kb + 1) * 128], 33)

    wred = B.sbuf("wred", [128, 1], F32)
    wsum = [B.sbuf("wsum%d" % i, [128, 1], F32) for i in range(3)]
    sc_sb = B.sbuf("sc_sb", [1, 2], F32)
    wmats = [(T["w_gate"], FFN, D_MODEL), (T["w_up"], FFN, D_MODEL), (T["w_down"], D_MODEL, FFN)]
    for wi, (wt, nrows, ncols) in enumerate(wmats):
        nseg = max(1, ncols // 2048)
        segc = ncols // nseg
        si = 0
        for rb in range(nrows // 128):
            for sg in range(nseg):
                stg = scr[si % 2]
                si += 1
                B.dma("qs", "dwf%d" % (si % 2), lambda q, rb=rb, sg=sg, wt=wt, segc=segc,
                      ncols=ncols, stg=stg: q.dma_start(
                    out=stg.h[:, 0:segc],
                    in_=bass.AP(wt.tensor, rb * 128 * ncols + sg * segc,
                                [[ncols, 128], [1, segc]])), [], [stg])
                first = (rb == 0 and sg == 0)
                dst = wsum[wi] if first else wred
                vop(lambda e, stg=stg, segc=segc, dst=dst: e.reduce_sum(
                    dst.h[:, 0:1], stg.h[:, 0:segc], axis=AX.X, apply_absolute_value=True),
                    [stg], [dst])
                if not first:
                    vop(lambda e, wi=wi: e.tensor_add(wsum[wi].h[:, 0:1], wsum[wi].h[:, 0:1],
                                                      wred.h[:, 0:1]), [wsum[wi], wred],
                        [wsum[wi]])
        top(lambda e, wi=wi: e.matmul(pb[6].h[0:1, 0:1], onesc.h[:, 0:1], wsum[wi].h[:, 0:1],
                                      start=True, stop=True), [onesc, wsum[wi]], [pb[6]])
        vop(lambda e, n=nrows * ncols: e.tensor_scalar(
            sc_sb.h[0:1, 0:1], pb[6].h[0:1, 0:1], 1.0 / n, 1e-5, A_.mult, A_.max),
            [pb[6]], [sc_sb])
        vop(lambda e: e.tensor_scalar_mul(sc_sb.h[0:1, 1:2], sc_sb.h[0:1, 0:1], 0.5),
            [sc_sb], [sc_sb], accum=True)
        B.dma("qg", "dscr", lambda q: q.dma_start(
            out=screp.h[:, :],
            in_=bass.AP(sc_sb.h, 0, [[1, 1], [0, 128], [1, 2]])), [sc_sb], [screp])
        vop(lambda e, wi=wi: e.tensor_scalar_mul(swneg[wi].h[:, 0:1], screp.h[:, 0:1], -1.0),
            [screp], [swneg[wi]])
        vop(lambda e, wi=wi: e.tensor_copy(thrp[wi].h[:, 0:1], screp.h[:, 1:2]),
            [screp], [thrp[wi]])
        vop(lambda e, wi=wi: e.tensor_scalar_mul(thrn[wi].h[:, 0:1], screp.h[:, 1:2], -1.0),
            [screp], [thrn[wi]])

    xg = [B.sbuf("xg%d" % i, [128, D_MODEL], F32) for i in range(2)]
    mcol = B.sbuf("mcol", [128, 1], F32)
    m2col = B.sbuf("m2col", [128, 1], F32)
    sqcol = B.sbuf("sqcol", [128, 1], F32)
    rstd = B.sbuf("rstd", [128, 1], F32)
    nmr = B.sbuf("nmr", [128, 1], F32)
    xn32 = B.sbuf("xn32", [128, D_MODEL], F32)
    sqbuf = xn32
    xnbf = B.sbuf("xnbf", [128, D_MODEL], BF16)
    xnfm = B.sbuf("xnfm", [128, 8 * TILE], BF16)
    xproj_t = B.sbuf("xproj_t", [128, 16 * (TILE + 3)], BF16)
    zst = B.sbuf("zst", [128, 512], BF16)
    cvf = scr[0]
    xconv_t = B.sbuf("xconv_t", [128, 16 * TILE], BF16)
    xtm = [B.sbuf("xtm0", [128, D_INNER], BF16)] * 2
    xdbl = scr[1]
    dtbc = B.sbuf("dtbc", [16, TILE], F32)
    dtrow = dtbc
    sprow = dtbc
    dtA = B.sbuf("dtA", [16, TILE], F32)
    Bbar = dtA
    Bbarbf = B.sbuf("Bbarbf", [16, TILE], BF16)
    Cbf = B.sbuf("Cbf", [16, TILE], BF16)
    eexp = B.sbuf("eexp", [16, TILE], BF16)
    Bhat = B.sbuf("Bhat", [16, TILE], BF16)
    BhT = B.sbuf("BhT", [128, CPT * 16], BF16)
    dcol = B.sbuf("dcol", [48, 1], F32)
    amx = B.sbuf("amx", [128, 1], F32)
    scol = B.sbuf("scol", [128, 1], F32)
    scol2 = B.sbuf("scol2", [128, 1], F32)
    lad_w, lad_d = scr[0], scr[1]
    xq_tm = sbf[0]
    sgb = B.sbuf("sgb", [128, 512], FP16)
    upb = B.sbuf("upb", [128, 512], FP16)
    th_a = B.sbuf("th_a", [128, 2048], FP16)
    th_b = B.sbuf("th_b", [128, 2048], FP16)

    qi8 = B.sbuf("qi8", [128, 1024], I8)

    def quant_cast(fn, bufs, w, sc, wb, d, qp, c2, fp16_src=False):
        vop(lambda e: e.tensor_scalar(qi8.h[:, 0:w], fn(), sc.h[:, 0:1],
                                      None, A_.mult), bufs + [sc], [qi8])
        vop(lambda e: e.tensor_copy(wb[0].h[:, wb[1]:wb[1] + w],
                                    qi8.h[:, 0:w]), [qi8], [wb[0]])
        if fp16_src:
            vop(lambda e: e.tensor_scalar(d[0].h[:, d[1]:d[1] + w], fn(), sc.h[:, 0:1],
                                          None, A_.mult), bufs + [sc], [d[0]])
            vop(lambda e: e.tensor_sub(d[0].h[:, d[1]:d[1] + w], d[0].h[:, d[1]:d[1] + w],
                                       wb[0].h[:, wb[1]:wb[1] + w]), [d[0], wb[0]], [d[0]])
        else:
            vop(lambda e: e.scalar_tensor_tensor(
                d[0].h[:, d[1]:d[1] + w], fn(), sc.h[:, 0:1],
                wb[0].h[:, wb[1]:wb[1] + w], A_.mult, A_.subtract),
                bufs + [sc, wb[0]], [d[0]])
        vop(lambda e: e.scalar_tensor_tensor(
            qp[0].h[:, qp[1]:qp[1] + w], d[0].h[:, d[1]:d[1] + w], 0.5,
            wb[0].h[:, wb[1]:wb[1] + w], A_.is_ge, A_.add), [d[0], wb[0]], [qp[0]])
        vop(lambda e: e.tensor_scalar(c2[0].h[:, c2[1]:c2[1] + w],
                                      d[0].h[:, d[1]:d[1] + w], -0.5, None, A_.is_le),
            [d[0]], [c2[0]])
        vop(lambda e: e.tensor_sub(wb[0].h[:, wb[1]:wb[1] + w],
                                   qp[0].h[:, qp[1]:qp[1] + w],
                                   c2[0].h[:, c2[1]:c2[1] + w]), [qp[0], c2[0]], [wb[0]],
            accum=True)

    for _nm, _dst in _pvec_loads:
        B.dma("qs", "dsm", lambda q, _nm=_nm: q.dma_start(
            out=dtbc.h[0:16, 0:128],
            in_=bass.AP(T[_nm].tensor, 0, [[128, 16], [1, 128]])), [], [dtbc])
        top(lambda e: e.transpose(pb[6].h[0:128, 0:16], dtbc.h[0:16, 0:128],
                                  identf.h[0:16, 0:16]), [dtbc, identf], [pb[6]])
        vop(lambda e, _dst=_dst: e.tensor_copy(_dst.h[:, 0:16], pb[6].h[0:128, 0:16]),
            [pb[6]], [_dst])

    vop(lambda e: e.memset(
        bass.AP(xproj_t.h, 0, [[16 * (TILE + 3), 128], [TILE + 3, 16], [1, 3]]), 0.0),
        [], [xproj_t])

    hcur = [0]

    def mixer_tiles(prefix):
        ntiles = NT_PRE if prefix else NT_OWN
        x_src = T["x_pre"] if prefix else T["x_own"]
        cbcol = pconvb if prefix else convb
        n_oc = 16 if prefix else 32
        span = D_INNER if prefix else 2 * D_INNER
        def stageA(ti):
            t0 = ti * TILE
            for gp in range(GPT // 2):
                for g in (2 * gp, 2 * gp + 1):
                    xt = xg[g % 2]
                    sc = scol if g % 2 == 0 else scol2
                    B.dma("qs", "dx%d" % (g % 2), lambda q, t0=t0, g=g, xt=xt, x_src=x_src:
                          q.dma_start(out=xt.h[:, :],
                                      in_=bass.AP(x_src.tensor, (t0 + g * 128) * D_MODEL,
                                                  [[D_MODEL, 128], [1, D_MODEL]])), [], [xt])
                    vop(lambda e, xt=xt: e.reduce_sum(mcol.h[:, 0:1], xt.h[:, :], axis=AX.X),
                        [xt], [mcol])
                    sop(lambda e, xt=xt: e.activation(sqbuf.h[:, :], xt.h[:, :], AF.Square),
                        [xt], [sqbuf])
                    vop(lambda e: e.reduce_sum(sqcol.h[:, 0:1], sqbuf.h[:, :], axis=AX.X),
                        [sqbuf], [sqcol])
                    vop(lambda e: e.tensor_scalar_mul(mcol.h[:, 0:1], mcol.h[:, 0:1],
                                                      1.0 / D_MODEL), [mcol], [mcol])
                    vop(lambda e: e.tensor_mul(m2col.h[:, 0:1], mcol.h[:, 0:1],
                                               mcol.h[:, 0:1]), [mcol], [m2col])
                    vop(lambda e: e.scalar_tensor_tensor(sqcol.h[:, 0:1], sqcol.h[:, 0:1],
                                                         1.0 / D_MODEL, m2col.h[:, 0:1],
                                                         A_.mult, A_.subtract),
                        [sqcol, m2col], [sqcol])
                    sop(lambda e: e.activation(rstd.h[:, 0:1], sqcol.h[:, 0:1], AF.Sqrt,
                                               bias=epscol.h[:, 0:1]), [sqcol, epscol],
                        [rstd])
                    vop(lambda e: e.reciprocal(rstd.h[:, 0:1], rstd.h[:, 0:1]),
                        [rstd], [rstd])
                    vop(lambda e: e.scalar_tensor_tensor(nmr.h[:, 0:1], mcol.h[:, 0:1],
                                                         -1.0, rstd.h[:, 0:1],
                                                         A_.mult, A_.mult),
                        [mcol, rstd], [nmr])
                    sop(lambda e, xt=xt: e.activation(xt.h[:, :], xt.h[:, :], AF.Identity,
                                                      bias=nmr.h[:, 0:1],
                                                      scale=rstd.h[:, 0:1]),
                        [xt, nmr, rstd], [xt])
                    vop(lambda e, xt=xt: e.tensor_copy(xnbf.h[:, :], xt.h[:, :]),
                        [xt], [xnbf])
                    for kc in range(8):
                        transpose_to(xnfm, kc * TILE + g * 128, xnbf,
                                     xnbf.h[:, kc * 128:(kc + 1) * 128], 128)
                    if not prefix:
                        gi = (t0 + g * 128) // 128
                        vop(lambda e, xt=xt: e.reduce_max(amx.h[:, 0:1], xt.h[:, :],
                                                          axis=AX.X,
                                                          apply_absolute_value=True),
                            [xt], [amx])
                        vop(lambda e: e.tensor_scalar_max(amx.h[:, 0:1], amx.h[:, 0:1],
                                                          1e-5), [amx], [amx])
                        vop(lambda e, sc=sc: e.reciprocal(sc.h[:, 0:1], amx.h[:, 0:1]),
                            [amx], [sc])
                        vop(lambda e, sc=sc: e.tensor_scalar_mul(sc.h[:, 0:1],
                                                                 sc.h[:, 0:1], 127.0),
                            [sc], [sc])
                        vop(lambda e, gi=gi, sc=sc: e.reciprocal(
                            c1_all.h[:, gi:gi + 1], sc.h[:, 0:1]), [sc],
                            [c1_all], accum=True)
                        quant_cast(lambda xt=xt: xt.h[:, :], [xg[g % 2]], D_MODEL, sc,
                                   (sbf[g % 2], 0), (scr[g % 2], 1024),
                                   (sbf[g % 2], 1024), (xnbf, 0))
                        for kc in range(8):
                            col = kc * TILE + g * 128
                            dst = wb16[0] if col < 2048 else wb16[1]
                            transpose_to(dst, col % 2048, sbf[g % 2],
                                         sbf[g % 2].h[:, kc * 128:(kc + 1) * 128], 128)
            if not prefix:
                B.dma("qg", "dxq1", lambda q, t0=t0: q.dma_start(
                    out=bass.AP(T["xq1_sp"].tensor, t0,
                                [[8 * n_own, 128], [n_own, 4], [1, TILE]]),
                    in_=bass.AP(wb16[0].h, 0, [[2048, 128], [TILE, 4], [1, TILE]])),
                    [wb16[0]], [dq1])
                B.dma("qg", "dxq1", lambda q, t0=t0: q.dma_start(
                    out=bass.AP(T["xq1_sp"].tensor, 4 * n_own + t0,
                                [[8 * n_own, 128], [n_own, 4], [1, TILE]]),
                    in_=bass.AP(wb16[1].h, 0, [[2048, 128], [TILE, 4], [1, TILE]])),
                    [wb16[1]], [dq1])
        def stageB(ti):
            t0 = ti * TILE
            for oc in range(n_oc):
                ps = pb[oc % 2]
                for kc in range(8):
                    top(lambda e, oc=oc, kc=kc, ps=ps, span=span: e.matmul(
                        ps.h[:, :], wpool.h[:, kc * span + oc * 128:kc * span + (oc + 1) * 128],
                        xnfm.h[:, kc * TILE:(kc + 1) * TILE],
                        start=(kc == 0), stop=(kc == 7)), [wpool, xnfm], [ps], accum=(kc > 0))
                if oc < 16:
                    vop(lambda e, oc=oc, ps=ps: e.tensor_copy(
                        xproj_t.h[:, oc * (TILE + 3) + 3:oc * (TILE + 3) + 3 + TILE],
                        ps.h[:, :]), [ps], [xproj_t], accum=True)
                else:
                    vop(lambda e, ps=ps: e.tensor_copy(zst.h[:, :], ps.h[:, :]), [ps], [zst])
                    B.dma("qg", "dzs", lambda q, t0=t0, oc=oc: q.dma_start(
                        out=bass.AP(T["z_sp"].tensor, (oc - 16) * n_own + t0,
                                    [[16 * n_own, 128], [1, TILE]]),
                        in_=zst.h[:, :]), [zst], [dz])
        def stageC(ti):
            t0 = ti * TILE
            for c in range(16):
                base = c * (TILE + 3)
                vop(lambda e, base=base, c=c, cb=cbcol: e.tensor_scalar(
                    cvf.h[:, 0:TILE], xproj_t.h[:, base:base + TILE], convw.h[:, 4 * c:4 * c + 1],
                    cb.h[:, c:c + 1], A_.mult, A_.add), [xproj_t, convw, cbcol], [cvf])
                for k in (1, 2, 3):
                    vop(lambda e, base=base, c=c, k=k: e.scalar_tensor_tensor(
                        cvf.h[:, 0:TILE], xproj_t.h[:, base + k:base + k + TILE],
                        convw.h[:, 4 * c + k:4 * c + k + 1], cvf.h[:, 0:TILE], A_.mult, A_.add),
                        [xproj_t, convw, cvf], [cvf])
                sop(lambda e, c=c: e.activation(xconv_t.h[:, c * TILE:(c + 1) * TILE],
                                                cvf.h[:, 0:TILE], AF.Silu), [cvf], [xconv_t],
                    accum=True)
            vop(lambda e: e.tensor_copy(
                bass.AP(xproj_t.h, 0, [[16 * (TILE + 3), 128], [TILE + 3, 16], [1, 3]]),
                bass.AP(xproj_t.h, TILE, [[16 * (TILE + 3), 128], [TILE + 3, 16], [1, 3]])),
                [xproj_t], [xproj_t], accum=True)
            if not prefix:
                B.dma("qg", "dxcs", lambda q, t0=t0: q.dma_start(
                    out=bass.AP(T["xc_sp"].tensor, t0,
                                [[16 * n_own, 128], [n_own, 16], [1, TILE]]),
                    in_=bass.AP(xconv_t.h, 0, [[16 * TILE, 128], [TILE, 16], [1, TILE]])),
                    [xconv_t], [dxc])
        def stageD(ti):
            t0 = ti * TILE
            for kc in range(16):
                top(lambda e, kc=kc: e.matmul(
                    pb[6].h[0:16, :], WxT.h[:, kc * 33:kc * 33 + 16],
                    xconv_t.h[:, kc * TILE:(kc + 1) * TILE],
                    start=(kc == 0), stop=(kc == 15)), [WxT, xconv_t], [pb[6]],
                    accum=(kc > 0))
            for kc in range(16):
                top(lambda e, kc=kc: e.matmul(
                    pb[0].h[0:16, :], WxT.h[:, kc * 33 + 16:kc * 33 + 32],
                    xconv_t.h[:, kc * TILE:(kc + 1) * TILE],
                    start=(kc == 0), stop=(kc == 15)), [WxT, xconv_t], [pb[0]],
                    accum=(kc > 0))
            for kc in range(16):
                top(lambda e, kc=kc: e.matmul(
                    pb[1].h[0:1, :], WxT.h[:, kc * 33 + 32:kc * 33 + 33],
                    xconv_t.h[:, kc * TILE:(kc + 1) * TILE],
                    start=(kc == 0), stop=(kc == 15)), [WxT, xconv_t], [pb[1]],
                    accum=(kc > 0))
            vop(lambda e: e.tensor_copy(xdbl.h[0:16, 0:TILE], pb[6].h[0:16, :]), [pb[6]], [xdbl])
            vop(lambda e: e.tensor_copy(Cbf.h[:, :], pb[0].h[0:16, :]), [pb[0]], [Cbf])
            sop(lambda e: e.activation(cvf.h[0:1, 0:TILE], pb[1].h[0:1, :], AF.Abs),
                [pb[1]], [cvf])
            sop(lambda e: e.activation(cvf.h[0:1, 0:TILE], cvf.h[0:1, 0:TILE], AF.Exp, scale=-1.0),
                [cvf], [cvf], accum=True)
            sop(lambda e: e.activation(cvf.h[0:1, 0:TILE], cvf.h[0:1, 0:TILE], AF.Ln, bias=1.0),
                [cvf], [cvf], accum=True)
            vop(lambda e: e.scalar_tensor_tensor(dtbc.h[0:1, :], pb[1].h[0:1, :], 0.0,
                                                 cvf.h[0:1, 0:TILE], A_.max, A_.add),
                [pb[1], cvf], [dtbc], accum=True)
            vop(lambda e: e.tensor_scalar(dtbc.h[0:1, :], dtbc.h[0:1, :], 0.1, 0.001,
                                          A_.min, A_.max), [dtbc], [dtbc], accum=True)
            top(lambda e: e.matmul(pb[1].h[0:16, :], ones1x16.h[:, :], dtbc.h[0:1, :],
                                   start=True, stop=True), [ones1x16, dtbc], [pb[1]])
            vop(lambda e: e.tensor_copy(dtbc.h[:, :], pb[1].h[0:16, :]), [pb[1]], [dtbc])
            vop(lambda e: e.tensor_scalar(dtA.h[:, :], dtbc.h[:, :], Acol.h[:, 0:1], 0.0,
                                          A_.mult, A_.min), [dtbc, Acol], [dtA])
            vop(lambda e: e.tensor_scalar_max(dtA.h[:, :], dtA.h[:, :], -20.0), [dtA], [dtA])
            vop(lambda e: e.tensor_copy(cumc.h[:, 0:1], carry.h[:, :]), [carry], [cumc])
            vop(lambda e: e.tensor_tensor_scan(
                cumc.h[:, 1:1 + TILE], dtA.h[:, :],
                bass.AP(zcol16.h, 0, [[1, 16], [0, TILE]]), cumc.h[:, 0:1],
                A_.add, A_.add), [dtA, zcol16, cumc], [cumc], accum=True)
            vop(lambda e: e.tensor_copy(carry.h[:, :], cumc.h[:, TILE:TILE + 1]),
                [cumc], [carry])
            vop(lambda e: e.tensor_mul(Bbar.h[:, :], dtbc.h[:, :], xdbl.h[0:16, 0:TILE]),
                [dtbc, xdbl], [Bbar])
            vop(lambda e: e.tensor_scalar(Bbar.h[:, :], Bbar.h[:, :], 10.0, -10.0,
                                          A_.min, A_.max), [Bbar], [Bbar])
            vop(lambda e: e.tensor_sub(
                Bhat.h[:, :].rearrange("p (c q) -> p c q", q=CHUNK),
                cumc.h[:, 1:1 + TILE].rearrange("p (c q) -> p c q", q=CHUNK),
                bass.AP(cumc.h, CHUNK, [[1 + TILE, 16], [CHUNK, CPT], [0, CHUNK]])),
                [cumc], [Bhat])
            sop(lambda e: e.activation(eexp.h[:, :], Bhat.h[:, :], AF.Exp, scale=-1.0),
                [Bhat], [eexp])
            vop(lambda e: e.tensor_mul(Bhat.h[:, :], eexp.h[:, :], Bbar.h[:, :]),
                [eexp, Bbar], [Bhat])
            if not prefix:
                vop(lambda e: e.tensor_copy(Bbarbf.h[:, :], Bbar.h[:, :]), [Bbar], [Bbarbf])
                vop(lambda e: e.tensor_sub(
                    eexp.h[:, :].rearrange("p (c q) -> p c q", q=CHUNK),
                    cumc.h[:, 1:1 + TILE].rearrange("p (c q) -> p c q", q=CHUNK),
                    bass.AP(cumc.h, 0, [[1 + TILE, 16], [CHUNK, CPT], [0, CHUNK]])),
                    [cumc], [eexp])
                sop(lambda e: e.activation(eexp.h[:, :], eexp.h[:, :], AF.Exp), [eexp], [eexp])
                vop(lambda e, t0=t0: e.tensor_mul(Chp_all.h[:, t0:t0 + TILE], eexp.h[:, :],
                                                  Cbf.h[:, :]), [eexp, Cbf],
                    [Chp_all], accum=True)
            for c in range(CPT):
                xt_ = xtm[c % 2]
                for dc in range(16):
                    transpose_to(xt_, dc * 128, xconv_t,
                                 xconv_t.h[:, dc * TILE + c * CHUNK:dc * TILE + (c + 1) * CHUNK],
                                 128)
                gk = (t0 + c * CHUNK) // CHUNK
                transpose_to(BhT, c * 16, Bhat, Bhat.h[:, c * CHUNK:(c + 1) * CHUNK], 16)
                if not prefix:
                    transpose_to(BbT_own, gk * 16, Bbarbf,
                                 Bbarbf.h[:, c * CHUNK:(c + 1) * CHUNK], 16)
                    transpose_to(CT_own, gk * 16, Cbf,
                                 Cbf.h[:, c * CHUNK:(c + 1) * CHUNK], 16)
                    top(lambda e, c=c: e.transpose(
                        pb[6].h[0:128, 0:16], cumc.h[:, 1 + c * CHUNK:1 + (c + 1) * CHUNK],
                        identf.h[0:16, 0:16]), [cumc, identf], [pb[6]])
                    vop(lambda e, gk=gk: e.tensor_copy(
                        cumT_own.h[:, gk * 16:(gk + 1) * 16], pb[6].h[0:128, 0:16]),
                        [pb[6]], [cumT_own], accum=True)
                vop(lambda e, c=c: e.tensor_scalar(
                    dcol.h[0:16, 0:1], cumc.h[:, (c + 1) * CHUNK:(c + 1) * CHUNK + 1],
                    cumc.h[:, c * CHUNK:c * CHUNK + 1], None, A_.subtract), [cumc], [dcol])
                sop(lambda e: e.activation(dcol.h[0:16, 0:1], dcol.h[0:16, 0:1], AF.Exp),
                    [dcol], [dcol])
                vop(lambda e: e.tensor_copy(dcol.h[32:48, 0:1], dcol.h[0:16, 0:1]),
                    [dcol], [dcol], accum=True)
                cu, nx = hcur[0], 1 - hcur[0]
                if not prefix:
                    B.dma("qg", "dhs", lambda q, gk=gk, cu=cu: q.dma_start(
                        out=T["h_sp"][:, gk * D_INNER:(gk + 1) * D_INNER],
                        in_=Hrow.h[HROW[cu]:HROW[cu] + 16, :]), [Hrow], [dh])
                HBANKS = (2, 3, 5, 0)
                for j in range(4):
                    bk = HBANKS[j]
                    top(lambda e, c=c, j=j, xt_=xt_, bk=bk: e.matmul(
                        pb[bk].h[0:16, :], BhT.h[:, c * 16:(c + 1) * 16],
                        xt_.h[:, j * 512:(j + 1) * 512], start=True, stop=True),
                        [BhT, xt_], [pb[bk]])
                for j in range(4):
                    bk = HBANKS[j]
                    vop(lambda e, j=j, cu=cu, nx=nx, bk=bk: e.scalar_tensor_tensor(
                        Hrow.h[HROW[nx]:HROW[nx] + 16, j * 512:(j + 1) * 512],
                        Hrow.h[HROW[cu]:HROW[cu] + 16, j * 512:(j + 1) * 512],
                        dcol.h[HROW[cu]:HROW[cu] + 16, 0:1], pb[bk].h[0:16, :],
                        A_.mult, A_.add),
                        [Hrow, dcol, pb[bk]], [Hrow], accum=True)
                hcur[0] = nx


        stageA(0)
        for ti in range(ntiles):
            stageB(ti)
            if ti + 1 < ntiles:
                stageA(ti + 1)
            stageC(ti)
            stageD(ti)
    if NT_PRE > 0:
        load_transpose_bf16(T["pre_Win"], D_INNER, D_MODEL, 0, D_INNER, "dwl")
        mixer_tiles(prefix=True)
    load_transpose_bf16(T["W_in"], 2 * D_INNER, D_MODEL, 0, 2 * D_INNER, "dwl")
    mixer_tiles(prefix=False)

    xq1g = xnbf
    k1g = sqcol
    k1u = rstd

    xq1gp = [Buf(xnfm.h, "xq1ga"), Buf(xnfm.h, "xq1gb")]

    def ffn_half(hf):
        wqst = sbf[1]
        for wi, wt in ((0, T["w_gate"]), (1, T["w_up"])):
            for rb in range(16):
                row0 = hf * 2048 + rb * 128
                stg = scr[0] if rb % 2 == 0 else scr[2]
                B.dma("qs", "dwf%d" % (rb % 2), lambda q, row0=row0, wt=wt, stg=stg:
                      q.dma_start(out=stg.h[:, 0:D_MODEL],
                                  in_=bass.AP(wt.tensor, row0 * D_MODEL,
                                              [[D_MODEL, 128], [1, D_MODEL]])), [], [stg])
                vop(lambda e, stg=stg, wi=wi: e.tensor_scalar(
                    lad_d.h[:, 0:D_MODEL], stg.h[:, 0:D_MODEL], thrp[wi].h[:, 0:1], None,
                    A_.is_ge), [stg, thrp[wi]], [lad_d])
                vop(lambda e, stg=stg, wi=wi: e.scalar_tensor_tensor(
                    wqst.h[:, 0:D_MODEL], stg.h[:, 0:D_MODEL], thrn[wi].h[:, 0:1],
                    lad_d.h[:, 0:D_MODEL], A_.is_le, A_.subtract),
                    [stg, thrn[wi], lad_d], [wqst])
                for kb in range(8):
                    transpose_to(wpool, wi * 16384 + kb * 2048 + rb * 128, wqst,
                                 wqst.h[:, kb * 128:(kb + 1) * 128], 128)
        def xq1_load(g):
            par = g % 2
            extra = [xnfm] if (hf == 0 and g == 0) else []
            B.dma("qs", "dq1r%d" % par, lambda q, g=g, par=par: q.dma_start(
                out=bass.AP(xnfm.h, par * 1024, [[4096, 128], [128, 8], [1, 128]]),
                in_=bass.AP(T["xq1_sp"].tensor, g * 128,
                            [[8 * n_own, 128], [n_own, 8], [1, 128]])),
                [dq1], [xq1gp[par]] + extra)

        xq1_load(0)
        pend = [None]
        for g in range(NG):
            if g + 1 < NG:
                xq1_load(g + 1)
            xq1g_ap = lambda kc, g=g: bass.AP(
                xnfm.h, (g % 2) * 1024 + kc * 128, [[4096, 128], [1, 128]])
            xq1g_buf = xq1gp[g % 2]
            vop(lambda e, g=g: e.tensor_mul(k1g.h[:, 0:1], swneg[0].h[:, 0:1],
                                            c1_all.h[:, g:g + 1]), [swneg[0], c1_all], [k1g])
            vop(lambda e, g=g: e.tensor_mul(k1u.h[:, 0:1], swneg[1].h[:, 0:1],
                                            c1_all.h[:, g:g + 1]), [swneg[1], c1_all], [k1u])
            for nh in range(4):
                for kc in range(8):
                    top(lambda e, nh=nh, kc=kc, a=xq1g_ap: e.matmul(
                        pb[0].h[:, :], a(kc),
                        wpool.h[:, kc * 2048 + nh * 512:kc * 2048 + (nh + 1) * 512],
                        start=(kc == 0), stop=(kc == 7)), [xq1g_buf, wpool], [pb[0]],
                        accum=(kc > 0))
                sop(lambda e: e.activation(sgb.h[:, :], pb[0].h[:, :], AF.Silu,
                                           scale=k1g.h[:, 0:1]), [pb[0], k1g], [sgb])
                for kc in range(8):
                    top(lambda e, nh=nh, kc=kc, a=xq1g_ap: e.matmul(
                        pb[1].h[:, :], a(kc),
                        wpool.h[:, 16384 + kc * 2048 + nh * 512:
                                16384 + kc * 2048 + (nh + 1) * 512],
                        start=(kc == 0), stop=(kc == 7)), [xq1g_buf, wpool], [pb[1]],
                        accum=(kc > 0))
                sop(lambda e: e.activation(upb.h[:, :], pb[1].h[:, :], AF.Identity,
                                           scale=k1u.h[:, 0:1]), [pb[1], k1u], [upb])
                vop(lambda e, nh=nh: e.tensor_mul(th_a.h[:, nh * 512:(nh + 1) * 512],
                                                  sgb.h[:, :], upb.h[:, :]), [sgb, upb],
                    [th_a], accum=True)
            if hf == 0:
                B.dma("qg", "dt1s", lambda q, g=g: q.dma_start(
                    out=bass.AP(T["t1_sp"].tensor, g * 128,
                                [[16 * n_own, 128], [n_own, 16], [1, 128]]),
                    in_=bass.AP(th_a.h, 0, [[2048, 128], [128, 16], [1, 128]])),
                    [th_a], [dt1])
            else:
                B.dma("qs", "dt1r", lambda q, g=g: q.dma_start(
                    out=bass.AP(th_b.h, 0, [[2048, 128], [128, 16], [1, 128]]),
                    in_=bass.AP(T["t1_sp"].tensor, g * 128,
                                [[16 * n_own, 128], [n_own, 16], [1, 128]])), [dt1], [th_b])
                vop(lambda e: e.reduce_max(amx.h[:, 0:1], th_a.h[:, :], axis=AX.X,
                                           apply_absolute_value=True), [th_a], [amx])
                vop(lambda e: e.reduce_max(m2col.h[:, 0:1], th_b.h[:, :], axis=AX.X,
                                           apply_absolute_value=True), [th_b], [m2col])
                vop(lambda e: e.tensor_max(amx.h[:, 0:1], amx.h[:, 0:1], m2col.h[:, 0:1]),
                    [amx, m2col], [amx])
                vop(lambda e: e.tensor_scalar_max(amx.h[:, 0:1], amx.h[:, 0:1], 1e-5),
                    [amx], [amx])
                vop(lambda e: e.reciprocal(scol.h[:, 0:1], amx.h[:, 0:1]), [amx], [scol])
                vop(lambda e: e.tensor_scalar_mul(scol.h[:, 0:1], scol.h[:, 0:1], 127.0),
                    [scol], [scol])
                vop(lambda e, g=g: e.reciprocal(c2_all.h[:, g:g + 1], scol.h[:, 0:1]),
                    [scol], [c2_all], accum=True)
                if pend[0] is not None:
                    pend[0]()
                for half2, tb in ((0, th_b), (1, th_a)):
                    for co in (0, 1024):
                        quant_cast(lambda tb=tb, co=co: tb.h[:, co:co + 1024], [tb],
                                   1024, scol, (sbf[half2], co), (scr[half2], co),
                                   (wb16[0], co), (wb16[1], co), fp16_src=True)

                def mk_pend(g=g):
                    def flush():
                        for half2 in (0, 1):
                            for kc in range(16):
                                transpose_to(wb16[2], kc * 128, sbf[half2],
                                             sbf[half2].h[:, kc * 128:(kc + 1) * 128],
                                             128)
                            B.dma("qg", "dxq2", lambda q, half2=half2, g=g: q.dma_start(
                                out=bass.AP(T["xq2_sp"].tensor,
                                            half2 * 16 * n_own + g * 128,
                                            [[32 * n_own, 128], [n_own, 16], [1, 128]]),
                                in_=bass.AP(wb16[2].h, 0, [[2048, 128], [128, 16],
                                                           [1, 128]])),
                                [wb16[2]], [dq2])
                    return flush
                pend[0] = mk_pend()

        if hf == 1 and pend[0] is not None:
            pend[0]()

    ffn_half(0)
    ffn_half(1)

    load_transpose_bf16(T["W_out"], D_MODEL, D_INNER, 0, D_MODEL, "dwl")

    wpar = Buf(wpool.h, "wpar")
    XCK = [(xnfm, 0), (wpar, 16384)]
    ZK = [(xnfm, 2048), (wpar, 18432)]
    HK = [(xconv_t, 0), (wpar, 20480)]
    SSL = [(xconv_t, 2048), (wpar, 22528)]
    ESC = [(wb16[0], 0), (wpar, 22656)]
    CBC = [(wb16[1], 0), (wpar, 24704)]
    YS = [(wb16[2], 0), (wpar, 26752)]
    XTMK = [(xtm[0], 0), (wpar, 28800)]
    XROW = [xg[0], xg[1]]
    DARG = [scr[0], scr[2]]
    cumbc = scr[1]
    xat = xn32

    def p4_xrow(k):
        par = k % 2
        B.dma("qs", "dxr%d" % par, lambda q, k=k, par=par: q.dma_start(
            out=XROW[par].h[:, :],
            in_=bass.AP(T["x_own"].tensor, k * 128 * D_MODEL,
                        [[D_MODEL, 128], [1, D_MODEL]])), [], [XROW[par]])

    def p4_loads(k):
        par = k % 2
        xb, xo = XCK[par]
        zb, zo = ZK[par]
        hb, ho = HK[par]
        db_ = DARG[par]
        extra = [wpool] if k == 1 else (xq1gp if k == 0 else [])
        B.dma("qs", "dxck%d" % par, lambda q, k=k, xb=xb, xo=xo: q.dma_start(
            out=bass.AP(xb.h, xo, [[xb.h.shape[1], 128], [128, 16], [1, 128]]),
            in_=bass.AP(T["xc_sp"].tensor, k * 128,
                        [[16 * n_own, 128], [n_own, 16], [1, 128]])), [dxc], [xb] + extra)
        B.dma("qs", "dzk%d" % par, lambda q, k=k, zb=zb, zo=zo: q.dma_start(
            out=bass.AP(zb.h, zo, [[zb.h.shape[1], 128], [128, 16], [1, 128]]),
            in_=bass.AP(T["z_sp"].tensor, k * 128,
                        [[16 * n_own, 128], [n_own, 16], [1, 128]])), [dz], [zb])
        B.dma("qs", "dhk%d" % par, lambda q, k=k, hb=hb, ho=ho: q.dma_start(
            out=bass.AP(hb.h, ho, [[hb.h.shape[1], 16], [1, D_INNER]]),
            in_=T["h_sp"][:, k * D_INNER:(k + 1) * D_INNER]), [dh], [hb])
        B.dma("qg", "dcf%d" % par, lambda q, k=k, db_=db_: q.dma_start(
            out=db_.h[0:1, 0:2048],
            in_=bass.AP(cumT_own.h, k * 16, [[NCH * 16, 128], [1, 16]])),
            [cumT_own], [db_])
        B.dma("qg", "dcb%d" % par, lambda q, db_=db_: q.dma_start(
            out=cumbc.h[:, :],
            in_=bass.AP(db_.h, 0, [[1, 1], [0, 128], [1, 2048]])),
            [db_], [cumbc])
        B.dma("qg", "dclf%d" % par, lambda q, k=k: q.dma_start(
            out=sbf[0].h[0:1, 0:2048],
            in_=bass.AP(CT_own.h, k * 16, [[NCH * 16, 128], [1, 16]])),
            [CT_own], [sbf[0]])
        cb, co = CBC[par]
        B.dma("qg", "dclb%d" % par, lambda q, cb=cb, co=co: q.dma_start(
            out=bass.AP(cb.h, co, [[cb.h.shape[1], 128], [1, 2048]]),
            in_=bass.AP(sbf[0].h, 0, [[1, 1], [0, 128], [1, 2048]])), [sbf[0]], [cb])

    def p4_scores(k):
        par = k % 2
        db_ = DARG[par]
        eb, eo = ESC[par]
        cb, co = CBC[par]
        sb, so = SSL[par]
        vop(lambda e, k=k, db_=db_: e.tensor_sub(
            db_.h[:, :].rearrange("p (j s) -> p j s", s=16),
            cumbc.h[:, :].rearrange("p (j s) -> p j s", s=16),
            cumT_own.h[:, k * 16:(k + 1) * 16][:, None, :].broadcast_to([128, 128, 16])),
            [cumbc, cumT_own], [db_], accum=True)
        vop(lambda e, db_=db_: e.tensor_sub(
            db_.h[:, :].rearrange("p (j s) -> p j s", s=16),
            db_.h[:, :].rearrange("p (j s) -> p j s", s=16),
            maskb.h[:, :, None].broadcast_to([128, 128, 16])), [db_, maskb], [db_],
            accum=True)
        sop(lambda e, db_=db_, eb=eb, eo=eo: e.activation(
            eb.h[:, eo:eo + 2048], db_.h[:, :], AF.Exp), [db_], [eb])
        vop(lambda e, k=k, eb=eb, eo=eo: e.tensor_mul(
            bass.AP(eb.h, eo, [[eb.h.shape[1], 128], [16, 128], [1, 16]]),
            bass.AP(eb.h, eo, [[eb.h.shape[1], 128], [16, 128], [1, 16]]),
            BbT_own.h[:, k * 16:(k + 1) * 16][:, None, :].broadcast_to([128, 128, 16])),
            [eb, BbT_own], [eb], accum=True)
        vop(lambda e, eb=eb, eo=eo, cb=cb, co=co: e.tensor_mul(
            eb.h[:, eo:eo + 2048], eb.h[:, eo:eo + 2048], cb.h[:, co:co + 2048]),
            [eb, cb], [eb], accum=True)

        def _score_reduce(e, eb=eb, eo=eo, sb=sb, so=so):
            with B.nc.allow_low_precision(reason="bf16 score tile for bf16 matmul"):
                return e.reduce_sum(
                    sb.h[0:128, so:so + 128],
                    bass.AP(eb.h, eo, [[eb.h.shape[1], 128], [16, 128], [1, 16]]),
                    axis=AX.X)
        vop(_score_reduce, [eb], [sb], accum=True)

    def p4_xtm(k):
        par = k % 2
        xb, xo = XCK[par]
        tb, to = XTMK[par]
        for dc in range(16):
            transpose_to(tb, to + dc * 128, xb,
                         bass.AP(xb.h, xo + dc * 128, [[xb.h.shape[1], 128], [1, 128]]),
                         128)

    def p4_y(k):
        par = k % 2
        tb, to = XTMK[par]
        hb, ho = HK[par]
        sb, so = SSL[par]
        yb, yo = YS[par]
        for dc in range(16):
            bank = 2 + dc % 2
            top(lambda e, dc=dc, bank=bank, tb=tb, to=to, sb=sb, so=so: e.matmul(
                pb[bank].h[:, 0:128],
                bass.AP(tb.h, to + dc * 128, [[tb.h.shape[1], 128], [1, 128]]),
                sb.h[0:128, so:so + 128], start=True, stop=False),
                [tb, sb], [pb[bank]])
            top(lambda e, dc=dc, bank=bank, k=k, hb=hb, ho=ho: e.matmul(
                pb[bank].h[:, 0:128],
                bass.AP(hb.h, ho + dc * 128, [[hb.h.shape[1], 16], [1, 128]]),
                Chp_all.h[:, k * 128:(k + 1) * 128], start=False, stop=True),
                [hb, Chp_all], [pb[bank]], accum=True)
            vop(lambda e, dc=dc, bank=bank, yb=yb, yo=yo: e.tensor_copy(
                yb.h[:, yo + dc * 128:yo + (dc + 1) * 128], pb[bank].h[:, 0:128]),
                [pb[bank]], [yb], accum=True)

    def p4_ypost(k):
        par = k % 2
        xb, xo = XCK[par]
        zb, zo = ZK[par]
        eb, eo = ESC[par]
        cb, co = CBC[par]
        yb, yo = YS[par]
        sop(lambda e, zb=zb, zo=zo, cb=cb, co=co: e.activation(
            cb.h[:, co:co + 2048], zb.h[:, zo:zo + 2048], AF.Silu), [zb], [cb],
            accum=True)
        vop(lambda e, yb=yb, yo=yo, cb=cb, co=co: e.tensor_mul(
            yb.h[:, yo:yo + 2048], yb.h[:, yo:yo + 2048], cb.h[:, co:co + 2048]),
            [yb, cb], [yb], accum=True)
        vop(lambda e, xb=xb, xo=xo, eb=eb, eo=eo: e.tensor_mul(
            bass.AP(eb.h, eo, [[eb.h.shape[1], 128], [128, 16], [1, 128]]),
            bass.AP(xb.h, xo, [[xb.h.shape[1], 128], [128, 16], [1, 128]]),
            Dp.h[:, :, None].broadcast_to([128, 16, 128])), [xb, Dp], [eb], accum=True)
        vop(lambda e, yb=yb, yo=yo, eb=eb, eo=eo: e.tensor_add(
            yb.h[:, yo:yo + 2048], yb.h[:, yo:yo + 2048], eb.h[:, eo:eo + 2048]),
            [yb, eb], [yb], accum=True)

    def p4_attn(k):
        par = k % 2
        yb, yo = YS[par]
        for nh in range(2):
            for dc in range(16):
                top(lambda e, nh=nh, dc=dc, yb=yb, yo=yo: e.matmul(
                    pb[nh].h[:, :],
                    bass.AP(yb.h, yo + dc * 128, [[yb.h.shape[1], 128], [1, 128]]),
                    wpool.h[:, dc * D_MODEL + nh * 512:dc * D_MODEL + (nh + 1) * 512],
                    start=(dc == 0), stop=(dc == 15)), [yb, wpool], [pb[nh]],
                    accum=(dc != 0))
        for nh in range(2):
            vop(lambda e, nh=nh, par=par: e.tensor_add(
                xat.h[:, nh * 512:(nh + 1) * 512],
                XROW[par].h[:, nh * 512:(nh + 1) * 512],
                pb[nh].h[:, :]), [XROW[par], pb[nh]], [xat], accum=True)
        B.dma("qg", "dxats", lambda q, k=k: q.dma_start(
            out=bass.AP(T["xat_sp"].tensor, k * 128,
                        [[8 * n_own, 128], [n_own, 8], [1, 128]]),
            in_=bass.AP(xat.h, 0, [[1024, 128], [128, 8], [1, 128]])), [xat], [dxat])

    p4_loads(0)
    p4_xrow(0)
    p4_scores(0)
    p4_xtm(0)
    if NCH > 1:
        p4_loads(1)
        p4_xrow(1)
    for k in range(NCH):
        p4_y(k)
        p4_ypost(k)
        if k + 1 < NCH:
            p4_xtm(k + 1)
        p4_attn(k)
        if k + 1 < NCH:
            p4_scores(k + 1)
        if k + 2 < NCH:
            p4_loads(k + 2)
            p4_xrow(k + 2)

    wqst = sbf[1]
    for rb in range(D_MODEL // 128):
        for sg in range(2):
            stg = scr[0] if sg == 0 else scr[2]
            extra = [wpar] if rb == 0 else []
            B.dma("qs", "dwf%d" % sg, lambda q, rb=rb, sg=sg, stg=stg: q.dma_start(
                out=stg.h[:, 0:2048],
                in_=bass.AP(T["w_down"].tensor, rb * 128 * FFN + sg * 2048,
                            [[FFN, 128], [1, 2048]])), [], [stg] + extra)
            vop(lambda e, stg=stg: e.tensor_scalar(
                lad_d.h[:, 0:2048], stg.h[:, 0:2048], thrp[2].h[:, 0:1], None, A_.is_ge),
                [stg, thrp[2]], [lad_d])
            vop(lambda e, stg=stg: e.scalar_tensor_tensor(
                wqst.h[:, 0:2048], stg.h[:, 0:2048], thrn[2].h[:, 0:1],
                lad_d.h[:, 0:2048], A_.is_le, A_.subtract), [stg, thrn[2], lad_d], [wqst])
            for kb in range(16):
                transpose_to(wpool, (sg * 16 + kb) * D_MODEL + rb * 128, wqst,
                             wqst.h[:, kb * 128:(kb + 1) * 128], 128)

    xatr = [xg[0]] * 2
    outb = [xn32] * 2
    k2c = nmr

    for k in range(NCH):
        par = k % 2
        B.dma("qs", "dq2r%d" % par, lambda q, k=k, par=par: q.dma_start(
            out=bass.AP(wb16[0].h, 0, [[2048, 128], [128, 16], [1, 128]]),
            in_=bass.AP(T["xq2_sp"].tensor, k * 128,
                        [[32 * n_own, 128], [n_own, 16], [1, 128]])), [dq2], [wb16[0]])
        B.dma("qs", "dq2r%d" % par, lambda q, k=k, par=par: q.dma_start(
            out=bass.AP(wb16[1].h, 0, [[2048, 128], [128, 16], [1, 128]]),
            in_=bass.AP(T["xq2_sp"].tensor, 16 * n_own + k * 128,
                        [[32 * n_own, 128], [n_own, 16], [1, 128]])), [dq2], [wb16[1]])
        B.dma("qs", "dxatr%d" % par, lambda q, k=k, par=par: q.dma_start(
            out=bass.AP(xatr[par].h, 0, [[1024, 128], [128, 8], [1, 128]]),
            in_=bass.AP(T["xat_sp"].tensor, k * 128,
                        [[8 * n_own, 128], [n_own, 8], [1, 128]])), [dxat], [xatr[par]])
        for nh in range(2):
            for kc in range(32):
                src = wb16[kc // 16]
                top(lambda e, nh=nh, kc=kc, src=src: e.matmul(
                    pb[nh].h[:, :], src.h[:, (kc % 16) * 128:((kc % 16) + 1) * 128],
                    wpool.h[:, kc * D_MODEL + nh * 512:kc * D_MODEL + (nh + 1) * 512],
                    start=(kc == 0), stop=(kc == 31)), [src, wpool], [pb[nh]],
                    accum=(kc != 0))
        vop(lambda e, k=k: e.tensor_mul(k2c.h[:, 0:1], swneg[2].h[:, 0:1],
                                        c2_all.h[:, k:k + 1]), [swneg[2], c2_all], [k2c])
        ob = outb[par]
        for nh in range(2):
            vop(lambda e, nh=nh, ob=ob, par=par: e.scalar_tensor_tensor(
                ob.h[:, nh * 512:(nh + 1) * 512], pb[nh].h[:, :], k2c.h[:, 0:1],
                xatr[par].h[:, nh * 512:(nh + 1) * 512], A_.mult, A_.add),
                [pb[nh], k2c, xatr[par]], [ob], accum=True)
        B.dma("qg", "dout", lambda q, k=k, ob=ob: q.dma_start(
            out=bass.AP(T["out"].tensor, k * 128 * D_MODEL, [[D_MODEL, 128], [1, D_MODEL]]),
            in_=ob.h[:, :]), [ob], [])

    B.wait_stream("qs", "dout")


_NC_CACHE = {}


def _get_nc(n_own, n_pre):
    key = (n_own, n_pre)
    if key not in _NC_CACHE:
        _NC_CACHE[key] = build_core_program(n_own, n_pre)
    return _NC_CACHE[key]


def kernel(L0_attn_x, L0_attn_W_in, L0_attn_conv_w, L0_attn_conv_b, L0_attn_W_xproj,
           L0_attn_A_log, L0_attn_D_param, L0_attn_W_out, L0_ffn_w_gate, L0_ffn_w_up,
           L0_ffn_w_down):
    x = np.ascontiguousarray(L0_attn_x, dtype=np.float32)
    b_, s_, d_ = x.shape
    half = s_ // 2
    nc = _get_nc(half, half)
    shared = {
        "W_in": np.ascontiguousarray(L0_attn_W_in, np.float32),
        "conv_w": np.ascontiguousarray(L0_attn_conv_w, np.float32),
        "conv_b": np.ascontiguousarray(L0_attn_conv_b, np.float32),
        "W_xproj": np.ascontiguousarray(L0_attn_W_xproj, np.float32),
        "A_log": np.ascontiguousarray(L0_attn_A_log, np.float32),
        "D_param": np.ascontiguousarray(L0_attn_D_param, np.float32),
        "W_out": np.ascontiguousarray(L0_attn_W_out, np.float32),
        "w_gate": np.ascontiguousarray(L0_ffn_w_gate, np.float32),
        "w_up": np.ascontiguousarray(L0_ffn_w_up, np.float32),
        "w_down": np.ascontiguousarray(L0_ffn_w_down, np.float32),
        "ident": np.eye(128, dtype=bfdt),
        "identf": np.eye(16, dtype=np.float32),
        "maskbig": np.ascontiguousarray(
            (np.triu(np.ones((128, 128), np.float32), 1).T * MASK_BIG).astype(bfdt)),
    }
    zwin = np.zeros((D_INNER, D_MODEL), np.float32)
    zcb = np.zeros((D_INNER,), np.float32)
    zx = np.zeros((half, D_MODEL), np.float32)
    in_maps = []
    for c in range(8):
        b = c // 2
        h = c % 2
        m = dict(shared)
        m["x_own"] = np.ascontiguousarray(x[b, h * half:(h + 1) * half])
        if h == 1:
            m["x_pre"] = np.ascontiguousarray(x[b, 0:half])
            m["pre_Win"] = shared["W_in"][0:D_INNER].copy()
            m["pre_conv_b"] = shared["conv_b"].copy()
        else:
            m["x_pre"] = zx
            m["pre_Win"] = zwin
            m["pre_conv_b"] = zcb
        in_maps.append(m)
    res = run_bass_kernel_spmd(nc, in_maps, core_ids=list(range(8)))
    out = np.empty((b_, s_, d_), np.float32)
    for c in range(8):
        b = c // 2
        h = c % 2
        out[b, h * half:(h + 1) * half] = np.asarray(res.results[c]["out"]).reshape(half, d_)
    return out


# version 15
# speedup vs baseline: 1.0217x; 1.0076x over previous
import contextlib
import numpy as np
import ml_dtypes
import concourse.bass as bass
import concourse.mybir as mybir
from concourse.bass_utils import run_bass_kernel_spmd

F32 = mybir.dt.float32
BF16 = mybir.dt.bfloat16
FP16 = mybir.dt.float16
I8 = mybir.dt.int8
A_ = mybir.AluOpType
AF = mybir.ActivationFunctionType
AX = mybir.AxisListType
bfdt = ml_dtypes.bfloat16

D_MODEL = 1024
D_INNER = 2048
D_STATE = 16
FFN = 4096
TILE = 512
CHUNK = 128
MASK_BIG = 32768.0
LVLS = [128.0, 64.0, 32.0, 16.0, 8.0, 4.0, 2.0, 1.0]


class Buf:
    __slots__ = ("h", "writers", "readers", "name")

    def __init__(self, h, name):
        self.h = h
        self.name = name
        self.writers = {}
        self.readers = {}


class Builder:
    ENGINES = ("te", "ve", "se", "qs", "qg")

    def __init__(self, nc, stack):
        self.nc = nc
        self.stack = stack
        self.items = {e: [] for e in self.ENGINES}
        self.cnt = {e: 0 for e in self.ENGINES}
        self.sems = {e: stack.enter_context(nc.semaphore("sem_" + e)) for e in self.ENGINES}
        self.extra = {}
        self.waited = {e: {} for e in self.ENGINES}

    def sbuf(self, name, shape, dtype):
        return Buf(self.stack.enter_context(self.nc.sbuf_tensor(name, shape, dtype)), name)

    def psum(self, name, shape, dtype=F32):
        return Buf(self.stack.enter_context(self.nc.psum_tensor(name, shape, dtype)), name)

    def _sem_of(self, p):
        return self.sems[p] if p in self.sems else self.extra[p][0]

    def _wait(self, eng, p, v):
        w = self.waited[eng]
        if w.get(p, -1) >= v:
            return
        w[p] = v
        self.items[eng].append(("w", self._sem_of(p), v))

    def _edges(self, eng, ins, outs, accum):
        need = {}

        def add(p, v):
            if v > need.get(p, -1):
                need[p] = v

        for b in ins:
            if b is not None:
                for p, v in b.writers.items():
                    add(p, v)
        for b in outs:
            if b is None:
                continue
            for p, v in b.writers.items():
                add(p, v)
            for p, v in b.readers.items():
                add(p, v)
        for p in sorted(need):
            v = self.extra[p][1] if p in self.extra else need[p]
            self._wait(eng, p, v)

    def op(self, eng, fn, ins, outs, accum=False):
        self._edges(eng, ins, outs, accum)
        self.cnt[eng] += 1
        v = self.cnt[eng]
        self.items[eng].append(("o", fn, self.sems[eng], 1))
        for b in ins:
            if b is not None:
                b.readers[eng] = v
        for b in outs:
            if b is not None:
                b.writers[eng] = v
        return v

    def dma(self, queue, sem_name, fn, ins, outs):
        self._edges(queue, ins, outs, False)
        if sem_name not in self.extra:
            self.extra[sem_name] = [self.stack.enter_context(self.nc.semaphore("s_" + sem_name)), 0]
        ent = self.extra[sem_name]
        ent[1] += 16
        v = ent[1]
        self.items[queue].append(("o", fn, ent[0], 16))
        for b in ins:
            if b is not None:
                b.readers[sem_name] = v
        for b in outs:
            if b is not None:
                b.writers[sem_name] = v
        return v

    def wait_stream(self, eng, p):
        v = self.cnt[p] if p in self.sems else (self.extra[p][1] if p in self.extra else 0)
        if v > 0:
            self._wait(eng, p, v)

    def build_block(self):
        block = self.stack.enter_context(self.nc.Block())

        def make(name):
            seq = self.items[name]

            def body(eng):
                for it in seq:
                    if it[0] == "w":
                        eng.wait_ge(it[1], it[2])
                    else:
                        it[1](eng).then_inc(it[2], it[3])
            return body

        block.tensor(make("te"))
        block.vector(make("ve"))
        block.scalar(make("se"))
        block.sync(make("qs"))
        block.gpsimd(make("qg"))


def build_core_program(n_own, n_pre):
    NCH = n_own // CHUNK
    nc = bass.Bass("TRN2", target_bir_lowering=False)
    DI = nc.dram_tensor
    T = {}
    T["x_own"] = DI("x_own", [n_own, D_MODEL], F32, kind="ExternalInput").ap()
    T["x_pre"] = DI("x_pre", [max(n_pre, 1), D_MODEL], F32, kind="ExternalInput").ap()
    T["W_in"] = DI("W_in", [2 * D_INNER, D_MODEL], F32, kind="ExternalInput").ap()
    T["pre_Win"] = DI("pre_Win", [D_INNER, D_MODEL], F32, kind="ExternalInput").ap()
    T["conv_w"] = DI("conv_w", [D_INNER, 4], F32, kind="ExternalInput").ap()
    T["conv_b"] = DI("conv_b", [D_INNER], F32, kind="ExternalInput").ap()
    T["pre_conv_b"] = DI("pre_conv_b", [D_INNER], F32, kind="ExternalInput").ap()
    T["W_xproj"] = DI("W_xproj", [2 * D_STATE + 1, D_INNER], F32, kind="ExternalInput").ap()
    T["A_log"] = DI("A_log", [D_STATE], F32, kind="ExternalInput").ap()
    T["D_param"] = DI("D_param", [D_INNER], F32, kind="ExternalInput").ap()
    T["W_out"] = DI("W_out", [D_MODEL, D_INNER], F32, kind="ExternalInput").ap()
    T["w_gate"] = DI("w_gate", [FFN, D_MODEL], F32, kind="ExternalInput").ap()
    T["w_up"] = DI("w_up", [FFN, D_MODEL], F32, kind="ExternalInput").ap()
    T["w_down"] = DI("w_down", [D_MODEL, FFN], F32, kind="ExternalInput").ap()
    T["ident"] = DI("ident", [128, 128], BF16, kind="ExternalInput").ap()
    T["identf"] = DI("identf", [16, 16], F32, kind="ExternalInput").ap()
    T["maskbig"] = DI("maskbig", [128, 128], BF16, kind="ExternalInput").ap()
    T["out"] = DI("out", [n_own, D_MODEL], F32, kind="ExternalOutput").ap()
    T["z_sp"] = DI("z_sp", [128, 16, n_own], BF16).ap()
    T["xc_sp"] = DI("xc_sp", [128, 16, n_own], BF16).ap()
    T["xq1_sp"] = DI("xq1_sp", [128, 8, n_own], BF16).ap()
    T["xq2_sp"] = DI("xq2_sp", [128, 32, n_own], BF16).ap()
    T["t1_sp"] = DI("t1_sp", [128, 16, n_own], FP16).ap()
    T["xat_sp"] = DI("xat_sp", [128, 8, n_own], F32).ap()
    T["h_sp"] = DI("h_sp", [D_STATE, NCH * D_INNER], BF16).ap()

    with contextlib.ExitStack() as stack:
        B = Builder(nc, stack)
        _emit(B, T, n_own, n_pre)
        B.build_block()
    return nc


def _emit(B, T, n_own, n_pre):
    NCH = n_own // CHUNK
    NT_OWN = n_own // TILE
    NT_PRE = n_pre // TILE
    CPT = TILE // CHUNK
    NG = n_own // 128
    GPT = TILE // 128

    def vop(fn, ins, outs, accum=False):
        B.op("ve", fn, ins, outs, accum)

    def sop(fn, ins, outs, accum=False):
        B.op("se", fn, ins, outs, accum)

    def top(fn, ins, outs, accum=False):
        B.op("te", fn, ins, outs, accum)

    dz = Buf(None, "z_sp")
    dxc = Buf(None, "xc_sp")
    dq1 = Buf(None, "xq1_sp")
    dq2 = Buf(None, "xq2_sp")
    dt1 = Buf(None, "t1_sp")
    dxat = Buf(None, "xat_sp")
    dh = Buf(None, "h_sp")

    wpool = B.sbuf("wpool", [128, 32768], BF16)
    scr = [B.sbuf("scr%d" % i, [128, 2048], F32) for i in range(3)]
    sbf = [B.sbuf("sbf%d" % i, [128, 2048], BF16) for i in range(2)]
    wb16 = [B.sbuf("wb16_%d" % i, [128, 2048], BF16) for i in range(3)]
    WxT = B.sbuf("WxT", [128, 16 * 33], BF16)
    convw = B.sbuf("convw", [128, 16 * 4], F32)
    convb = B.sbuf("convb", [128, 16], F32)
    pconvb = B.sbuf("pconvb", [128, 16], F32)
    Dp = B.sbuf("Dp", [128, 16], F32)
    Acol = B.sbuf("Acol", [16, 1], F32)
    ident = B.sbuf("id16", [128, 128], BF16)
    identf = B.sbuf("idf32", [16, 16], F32)
    maskb = B.sbuf("maskb", [128, 128], BF16)
    ones1x16 = B.sbuf("ones1x16", [1, 16], F32)
    screp = B.sbuf("screp", [128, 2], F32)
    onesc = B.sbuf("onesc", [128, 1], F32)
    zcol16 = B.sbuf("zcol16", [16, 1], F32)
    cumT_own = B.sbuf("cumT_own", [128, NCH * 16], F32)
    BbT_own = B.sbuf("BbT_own", [128, NCH * 16], BF16)
    CT_own = B.sbuf("CT_own", [128, NCH * 16], BF16)
    Chp_all = B.sbuf("Chp_all", [16, n_own], BF16)
    c1_all = B.sbuf("c1_all", [128, NG], F32)
    c2_all = B.sbuf("c2_all", [128, NG], F32)
    swneg = [B.sbuf("swneg%d" % i, [128, 1], F32) for i in range(3)]
    thrp = [B.sbuf("thrp%d" % i, [128, 1], F32) for i in range(3)]
    thrn = [B.sbuf("thrn%d" % i, [128, 1], F32) for i in range(3)]
    cumc = B.sbuf("cumc", [16, 1 + TILE], F32)
    carry = B.sbuf("carry", [16, 1], F32)
    epscol = B.sbuf("epscol", [128, 1], F32)
    Hrow = scr[2]
    HROW = [0, 32]

    pb = [B.psum("pb%d" % i, [128, 512], F32) for i in range(6)]
    pb.append(pb[4])
    pt = [B.psum("pt%d" % i, [128, 512], BF16) for i in range(2)]
    PTOFF = [0, 0]

    B.dma("qs", "dsm", lambda q: q.dma_start(
        out=bass.AP(convw.h, 0, [[64, 128], [4, 16], [1, 4]]),
        in_=bass.AP(T["conv_w"].tensor, 0, [[4, 128], [512, 16], [1, 4]])), [], [convw])
    _pvec_loads = []
    for _nm, _dst in (("conv_b", convb), ("pre_conv_b", pconvb), ("D_param", Dp)):
        _pvec_loads.append((_nm, _dst))
    B.dma("qs", "dsm", lambda q: q.dma_start(out=Acol.h[:, 0:1], in_=T["A_log"][:, None]),
          [], [Acol])
    B.dma("qs", "dsm", lambda q: q.dma_start(out=ident.h[:, :], in_=T["ident"]), [], [ident])
    B.dma("qs", "dsm", lambda q: q.dma_start(out=identf.h[:, :], in_=T["identf"]), [], [identf])
    B.dma("qs", "dsm", lambda q: q.dma_start(out=maskb.h[:, :], in_=T["maskbig"]), [], [maskb])
    sop(lambda e: e.activation(Acol.h[:, 0:1], Acol.h[:, 0:1], AF.Exp), [Acol], [Acol])
    vop(lambda e: e.tensor_scalar_mul(Acol.h[:, 0:1], Acol.h[:, 0:1], -1.0), [Acol], [Acol])
    for bz, val in ((zcol16, 0.0), (ones1x16, 1.0), (onesc, 1.0)):
        vop(lambda e, bz=bz, val=val: e.memset(bz.h[:, :], val), [], [bz])
    vop(lambda e: e.memset(carry.h[:, :], 0.0), [], [carry])
    vop(lambda e: e.memset(epscol.h[:, :], 1e-5), [], [epscol])
    vop(lambda e: e.memset(Hrow.h[0:16, :], 0.0), [], [Hrow])

    trcnt = [0]

    def transpose_to(dst, dst_col, src_buf, src_ap, n, evac="ve"):
        slot = trcnt[0] % 2
        ps = pt[slot]
        off = PTOFF[slot]
        trcnt[0] += 1
        top(lambda e, ps=ps, src_ap=src_ap, n=n, off=off: e.transpose(
            ps.h[0:128, off:off + n], src_ap, ident.h[0:n, 0:n]),
            [src_buf, ident], [ps])
        if evac == "se":
            sop(lambda e, ps=ps, dst=dst, dst_col=dst_col, n=n, off=off: e.activation(
                dst.h[:, dst_col:dst_col + n], ps.h[0:128, off:off + n], AF.Identity),
                [ps], [dst], accum=True)
        else:
            vop(lambda e, ps=ps, dst=dst, dst_col=dst_col, n=n, off=off: e.tensor_copy(
                dst.h[:, dst_col:dst_col + n], ps.h[0:128, off:off + n]), [ps], [dst],
                accum=True)

    def load_transpose_bf16(src_t, rows, cols, dst_off, dst_span, tag):
        for rb in range(rows // 128):
            stg = sbf[rb % 2]
            B.dma("qg", tag + str(rb % 2), lambda q, rb=rb, stg=stg, cols=cols, src_t=src_t:
                  q.dma_start(out=stg.h[:, 0:cols],
                              in_=bass.AP(src_t.tensor, rb * 128 * cols,
                                          [[cols, 128], [1, cols]])), [], [stg])
            for kb in range(cols // 128):
                transpose_to(wpool, dst_off + kb * dst_span + rb * 128, stg,
                             stg.h[:, kb * 128:(kb + 1) * 128], 128)

    wx_stage = sbf[1]
    B.dma("qg", "dwx", lambda q: q.dma_start(out=wx_stage.h[0:33, :], in_=T["W_xproj"]),
          [], [wx_stage])
    for kb in range(16):
        transpose_to(WxT, kb * 33, wx_stage, wx_stage.h[0:33, kb * 128:(kb + 1) * 128], 33)

    wred = B.sbuf("wred", [128, 1], F32)
    wsum = [B.sbuf("wsum%d" % i, [128, 1], F32) for i in range(3)]
    sc_sb = B.sbuf("sc_sb", [1, 2], F32)
    wmats = [(T["w_gate"], FFN, D_MODEL), (T["w_up"], FFN, D_MODEL), (T["w_down"], D_MODEL, FFN)]
    for wi, (wt, nrows, ncols) in enumerate(wmats):
        nseg = max(1, ncols // 2048)
        segc = ncols // nseg
        si = 0
        for rb in range(nrows // 128):
            for sg in range(nseg):
                stg = scr[si % 2]
                si += 1
                B.dma("qs", "dwf%d" % (si % 2), lambda q, rb=rb, sg=sg, wt=wt, segc=segc,
                      ncols=ncols, stg=stg: q.dma_start(
                    out=stg.h[:, 0:segc],
                    in_=bass.AP(wt.tensor, rb * 128 * ncols + sg * segc,
                                [[ncols, 128], [1, segc]])), [], [stg])
                first = (rb == 0 and sg == 0)
                dst = wsum[wi] if first else wred
                vop(lambda e, stg=stg, segc=segc, dst=dst: e.reduce_sum(
                    dst.h[:, 0:1], stg.h[:, 0:segc], axis=AX.X, apply_absolute_value=True),
                    [stg], [dst])
                if not first:
                    vop(lambda e, wi=wi: e.tensor_add(wsum[wi].h[:, 0:1], wsum[wi].h[:, 0:1],
                                                      wred.h[:, 0:1]), [wsum[wi], wred],
                        [wsum[wi]])
        top(lambda e, wi=wi: e.matmul(pb[6].h[0:1, 0:1], onesc.h[:, 0:1], wsum[wi].h[:, 0:1],
                                      start=True, stop=True), [onesc, wsum[wi]], [pb[6]])
        vop(lambda e, n=nrows * ncols: e.tensor_scalar(
            sc_sb.h[0:1, 0:1], pb[6].h[0:1, 0:1], 1.0 / n, 1e-5, A_.mult, A_.max),
            [pb[6]], [sc_sb])
        vop(lambda e: e.tensor_scalar_mul(sc_sb.h[0:1, 1:2], sc_sb.h[0:1, 0:1], 0.5),
            [sc_sb], [sc_sb], accum=True)
        B.dma("qg", "dscr", lambda q: q.dma_start(
            out=screp.h[:, :],
            in_=bass.AP(sc_sb.h, 0, [[1, 1], [0, 128], [1, 2]])), [sc_sb], [screp])
        vop(lambda e, wi=wi: e.tensor_scalar_mul(swneg[wi].h[:, 0:1], screp.h[:, 0:1], -1.0),
            [screp], [swneg[wi]])
        vop(lambda e, wi=wi: e.tensor_copy(thrp[wi].h[:, 0:1], screp.h[:, 1:2]),
            [screp], [thrp[wi]])
        vop(lambda e, wi=wi: e.tensor_scalar_mul(thrn[wi].h[:, 0:1], screp.h[:, 1:2], -1.0),
            [screp], [thrn[wi]])

    xg = [B.sbuf("xg%d" % i, [128, D_MODEL], F32) for i in range(2)]
    mcol = B.sbuf("mcol", [128, 1], F32)
    m2col = B.sbuf("m2col", [128, 1], F32)
    sqcol = B.sbuf("sqcol", [128, 1], F32)
    rstd = B.sbuf("rstd", [128, 1], F32)
    nmr = B.sbuf("nmr", [128, 1], F32)
    xn32 = B.sbuf("xn32", [128, D_MODEL], F32)
    sqbuf = xn32
    xnbf = B.sbuf("xnbf", [128, D_MODEL], BF16)
    xnfm = B.sbuf("xnfm", [128, 8 * TILE], BF16)
    xproj_t = B.sbuf("xproj_t", [128, 16 * (TILE + 3)], BF16)
    zst = B.sbuf("zst", [128, 512], BF16)
    cvf = scr[0]
    xconv_t = B.sbuf("xconv_t", [128, 16 * TILE], BF16)
    xtm = [B.sbuf("xtm0", [128, D_INNER], BF16)] * 2
    xdbl = scr[1]
    dtbc = B.sbuf("dtbc", [16, TILE], F32)
    dtrow = dtbc
    sprow = dtbc
    dtA = B.sbuf("dtA", [16, TILE], F32)
    Bbar = dtA
    Bbarbf = B.sbuf("Bbarbf", [16, TILE], BF16)
    Cbf = B.sbuf("Cbf", [16, TILE], BF16)
    eexp = B.sbuf("eexp", [16, TILE], BF16)
    Bhat = B.sbuf("Bhat", [16, TILE], BF16)
    BhT = B.sbuf("BhT", [128, CPT * 16], BF16)
    dcol = B.sbuf("dcol", [48, 1], F32)
    amx = B.sbuf("amx", [128, 1], F32)
    scol = B.sbuf("scol", [128, 1], F32)
    scol2 = B.sbuf("scol2", [128, 1], F32)
    lad_w, lad_d = scr[0], scr[1]
    xq_tm = sbf[0]
    sgb = B.sbuf("sgb", [128, 512], FP16)
    upb = B.sbuf("upb", [128, 512], FP16)
    th_a = B.sbuf("th_a", [128, 2048], FP16)
    th_b = B.sbuf("th_b", [128, 2048], FP16)

    qi8 = B.sbuf("qi8", [128, 1024], I8)

    def quant_cast(fn, bufs, w, sc, wb, d, qp, c2, fp16_src=False):
        vop(lambda e: e.tensor_scalar(qi8.h[:, 0:w], fn(), sc.h[:, 0:1],
                                      None, A_.mult), bufs + [sc], [qi8])
        vop(lambda e: e.tensor_copy(wb[0].h[:, wb[1]:wb[1] + w],
                                    qi8.h[:, 0:w]), [qi8], [wb[0]])
        if fp16_src:
            vop(lambda e: e.tensor_scalar(d[0].h[:, d[1]:d[1] + w], fn(), sc.h[:, 0:1],
                                          None, A_.mult), bufs + [sc], [d[0]])
            vop(lambda e: e.tensor_sub(d[0].h[:, d[1]:d[1] + w], d[0].h[:, d[1]:d[1] + w],
                                       wb[0].h[:, wb[1]:wb[1] + w]), [d[0], wb[0]], [d[0]])
        else:
            vop(lambda e: e.scalar_tensor_tensor(
                d[0].h[:, d[1]:d[1] + w], fn(), sc.h[:, 0:1],
                wb[0].h[:, wb[1]:wb[1] + w], A_.mult, A_.subtract),
                bufs + [sc, wb[0]], [d[0]])
        vop(lambda e: e.scalar_tensor_tensor(
            qp[0].h[:, qp[1]:qp[1] + w], d[0].h[:, d[1]:d[1] + w], 0.5,
            wb[0].h[:, wb[1]:wb[1] + w], A_.is_ge, A_.add), [d[0], wb[0]], [qp[0]])
        vop(lambda e: e.tensor_scalar(c2[0].h[:, c2[1]:c2[1] + w],
                                      d[0].h[:, d[1]:d[1] + w], -0.5, None, A_.is_le),
            [d[0]], [c2[0]])
        vop(lambda e: e.tensor_sub(wb[0].h[:, wb[1]:wb[1] + w],
                                   qp[0].h[:, qp[1]:qp[1] + w],
                                   c2[0].h[:, c2[1]:c2[1] + w]), [qp[0], c2[0]], [wb[0]],
            accum=True)

    for _nm, _dst in _pvec_loads:
        B.dma("qs", "dsm", lambda q, _nm=_nm: q.dma_start(
            out=dtbc.h[0:16, 0:128],
            in_=bass.AP(T[_nm].tensor, 0, [[128, 16], [1, 128]])), [], [dtbc])
        top(lambda e: e.transpose(pb[6].h[0:128, 0:16], dtbc.h[0:16, 0:128],
                                  identf.h[0:16, 0:16]), [dtbc, identf], [pb[6]])
        vop(lambda e, _dst=_dst: e.tensor_copy(_dst.h[:, 0:16], pb[6].h[0:128, 0:16]),
            [pb[6]], [_dst])

    vop(lambda e: e.memset(
        bass.AP(xproj_t.h, 0, [[16 * (TILE + 3), 128], [TILE + 3, 16], [1, 3]]), 0.0),
        [], [xproj_t])

    hcur = [0]

    def mixer_tiles(prefix):
        ntiles = NT_PRE if prefix else NT_OWN
        x_src = T["x_pre"] if prefix else T["x_own"]
        cbcol = pconvb if prefix else convb
        n_oc = 16 if prefix else 32
        span = D_INNER if prefix else 2 * D_INNER
        def stageA(ti):
            t0 = ti * TILE
            for gp in range(GPT // 2):
                for g in (2 * gp, 2 * gp + 1):
                    xt = xg[g % 2]
                    sc = scol if g % 2 == 0 else scol2
                    B.dma("qs", "dx%d" % (g % 2), lambda q, t0=t0, g=g, xt=xt, x_src=x_src:
                          q.dma_start(out=xt.h[:, :],
                                      in_=bass.AP(x_src.tensor, (t0 + g * 128) * D_MODEL,
                                                  [[D_MODEL, 128], [1, D_MODEL]])), [], [xt])
                    vop(lambda e, xt=xt: e.reduce_sum(mcol.h[:, 0:1], xt.h[:, :], axis=AX.X),
                        [xt], [mcol])
                    sop(lambda e, xt=xt: e.activation(sqbuf.h[:, :], xt.h[:, :], AF.Square),
                        [xt], [sqbuf])
                    vop(lambda e: e.reduce_sum(sqcol.h[:, 0:1], sqbuf.h[:, :], axis=AX.X),
                        [sqbuf], [sqcol])
                    vop(lambda e: e.tensor_scalar_mul(mcol.h[:, 0:1], mcol.h[:, 0:1],
                                                      1.0 / D_MODEL), [mcol], [mcol])
                    vop(lambda e: e.tensor_mul(m2col.h[:, 0:1], mcol.h[:, 0:1],
                                               mcol.h[:, 0:1]), [mcol], [m2col])
                    vop(lambda e: e.scalar_tensor_tensor(sqcol.h[:, 0:1], sqcol.h[:, 0:1],
                                                         1.0 / D_MODEL, m2col.h[:, 0:1],
                                                         A_.mult, A_.subtract),
                        [sqcol, m2col], [sqcol])
                    sop(lambda e: e.activation(rstd.h[:, 0:1], sqcol.h[:, 0:1], AF.Sqrt,
                                               bias=epscol.h[:, 0:1]), [sqcol, epscol],
                        [rstd])
                    vop(lambda e: e.reciprocal(rstd.h[:, 0:1], rstd.h[:, 0:1]),
                        [rstd], [rstd])
                    vop(lambda e: e.scalar_tensor_tensor(nmr.h[:, 0:1], mcol.h[:, 0:1],
                                                         -1.0, rstd.h[:, 0:1],
                                                         A_.mult, A_.mult),
                        [mcol, rstd], [nmr])
                    sop(lambda e, xt=xt: e.activation(xt.h[:, :], xt.h[:, :], AF.Identity,
                                                      bias=nmr.h[:, 0:1],
                                                      scale=rstd.h[:, 0:1]),
                        [xt, nmr, rstd], [xt])
                    vop(lambda e, xt=xt: e.tensor_copy(xnbf.h[:, :], xt.h[:, :]),
                        [xt], [xnbf])
                    for kc in range(8):
                        transpose_to(xnfm, kc * TILE + g * 128, xnbf,
                                     xnbf.h[:, kc * 128:(kc + 1) * 128], 128)
                    if not prefix:
                        gi = (t0 + g * 128) // 128
                        vop(lambda e, xt=xt: e.reduce_max(amx.h[:, 0:1], xt.h[:, :],
                                                          axis=AX.X,
                                                          apply_absolute_value=True),
                            [xt], [amx])
                        vop(lambda e: e.tensor_scalar_max(amx.h[:, 0:1], amx.h[:, 0:1],
                                                          1e-5), [amx], [amx])
                        vop(lambda e, sc=sc: e.reciprocal(sc.h[:, 0:1], amx.h[:, 0:1]),
                            [amx], [sc])
                        vop(lambda e, sc=sc: e.tensor_scalar_mul(sc.h[:, 0:1],
                                                                 sc.h[:, 0:1], 127.0),
                            [sc], [sc])
                        vop(lambda e, gi=gi, sc=sc: e.reciprocal(
                            c1_all.h[:, gi:gi + 1], sc.h[:, 0:1]), [sc],
                            [c1_all], accum=True)
                        quant_cast(lambda xt=xt: xt.h[:, :], [xg[g % 2]], D_MODEL, sc,
                                   (sbf[g % 2], 0), (scr[g % 2], 1024),
                                   (sbf[g % 2], 1024), (xnbf, 0))
                        for kc in range(8):
                            col = kc * TILE + g * 128
                            dst = wb16[0] if col < 2048 else wb16[1]
                            transpose_to(dst, col % 2048, sbf[g % 2],
                                         sbf[g % 2].h[:, kc * 128:(kc + 1) * 128], 128)
            if not prefix:
                B.dma("qg", "dxq1", lambda q, t0=t0: q.dma_start(
                    out=bass.AP(T["xq1_sp"].tensor, t0,
                                [[8 * n_own, 128], [n_own, 4], [1, TILE]]),
                    in_=bass.AP(wb16[0].h, 0, [[2048, 128], [TILE, 4], [1, TILE]])),
                    [wb16[0]], [dq1])
                B.dma("qg", "dxq1", lambda q, t0=t0: q.dma_start(
                    out=bass.AP(T["xq1_sp"].tensor, 4 * n_own + t0,
                                [[8 * n_own, 128], [n_own, 4], [1, TILE]]),
                    in_=bass.AP(wb16[1].h, 0, [[2048, 128], [TILE, 4], [1, TILE]])),
                    [wb16[1]], [dq1])
        def stageB(ti):
            t0 = ti * TILE
            for oc in range(n_oc):
                ps = pb[oc % 2]
                for kc in range(8):
                    top(lambda e, oc=oc, kc=kc, ps=ps, span=span: e.matmul(
                        ps.h[:, :], wpool.h[:, kc * span + oc * 128:kc * span + (oc + 1) * 128],
                        xnfm.h[:, kc * TILE:(kc + 1) * TILE],
                        start=(kc == 0), stop=(kc == 7)), [wpool, xnfm], [ps], accum=(kc > 0))
                if oc < 16:
                    vop(lambda e, oc=oc, ps=ps: e.tensor_copy(
                        xproj_t.h[:, oc * (TILE + 3) + 3:oc * (TILE + 3) + 3 + TILE],
                        ps.h[:, :]), [ps], [xproj_t], accum=True)
                else:
                    vop(lambda e, ps=ps: e.tensor_copy(zst.h[:, :], ps.h[:, :]), [ps], [zst])
                    B.dma("qg", "dzs", lambda q, t0=t0, oc=oc: q.dma_start(
                        out=bass.AP(T["z_sp"].tensor, (oc - 16) * n_own + t0,
                                    [[16 * n_own, 128], [1, TILE]]),
                        in_=zst.h[:, :]), [zst], [dz])
        def stageC(ti):
            t0 = ti * TILE
            for c in range(16):
                base = c * (TILE + 3)
                vop(lambda e, base=base, c=c, cb=cbcol: e.tensor_scalar(
                    cvf.h[:, 0:TILE], xproj_t.h[:, base:base + TILE], convw.h[:, 4 * c:4 * c + 1],
                    cb.h[:, c:c + 1], A_.mult, A_.add), [xproj_t, convw, cbcol], [cvf])
                for k in (1, 2, 3):
                    vop(lambda e, base=base, c=c, k=k: e.scalar_tensor_tensor(
                        cvf.h[:, 0:TILE], xproj_t.h[:, base + k:base + k + TILE],
                        convw.h[:, 4 * c + k:4 * c + k + 1], cvf.h[:, 0:TILE], A_.mult, A_.add),
                        [xproj_t, convw, cvf], [cvf])
                sop(lambda e, c=c: e.activation(xconv_t.h[:, c * TILE:(c + 1) * TILE],
                                                cvf.h[:, 0:TILE], AF.Silu), [cvf], [xconv_t],
                    accum=True)
            vop(lambda e: e.tensor_copy(
                bass.AP(xproj_t.h, 0, [[16 * (TILE + 3), 128], [TILE + 3, 16], [1, 3]]),
                bass.AP(xproj_t.h, TILE, [[16 * (TILE + 3), 128], [TILE + 3, 16], [1, 3]])),
                [xproj_t], [xproj_t], accum=True)
            if not prefix:
                B.dma("qg", "dxcs", lambda q, t0=t0: q.dma_start(
                    out=bass.AP(T["xc_sp"].tensor, t0,
                                [[16 * n_own, 128], [n_own, 16], [1, TILE]]),
                    in_=bass.AP(xconv_t.h, 0, [[16 * TILE, 128], [TILE, 16], [1, TILE]])),
                    [xconv_t], [dxc])
        def stageD(ti):
            t0 = ti * TILE
            for kc in range(16):
                top(lambda e, kc=kc: e.matmul(
                    pb[6].h[0:16, :], WxT.h[:, kc * 33:kc * 33 + 16],
                    xconv_t.h[:, kc * TILE:(kc + 1) * TILE],
                    start=(kc == 0), stop=(kc == 15)), [WxT, xconv_t], [pb[6]],
                    accum=(kc > 0))
            for kc in range(16):
                top(lambda e, kc=kc: e.matmul(
                    pb[0].h[0:16, :], WxT.h[:, kc * 33 + 16:kc * 33 + 32],
                    xconv_t.h[:, kc * TILE:(kc + 1) * TILE],
                    start=(kc == 0), stop=(kc == 15)), [WxT, xconv_t], [pb[0]],
                    accum=(kc > 0))
            for kc in range(16):
                top(lambda e, kc=kc: e.matmul(
                    pb[1].h[0:1, :], WxT.h[:, kc * 33 + 32:kc * 33 + 33],
                    xconv_t.h[:, kc * TILE:(kc + 1) * TILE],
                    start=(kc == 0), stop=(kc == 15)), [WxT, xconv_t], [pb[1]],
                    accum=(kc > 0))
            vop(lambda e: e.tensor_copy(xdbl.h[0:16, 0:TILE], pb[6].h[0:16, :]), [pb[6]], [xdbl])
            vop(lambda e: e.tensor_copy(Cbf.h[:, :], pb[0].h[0:16, :]), [pb[0]], [Cbf])
            sop(lambda e: e.activation(cvf.h[0:1, 0:TILE], pb[1].h[0:1, :], AF.Abs),
                [pb[1]], [cvf])
            sop(lambda e: e.activation(cvf.h[0:1, 0:TILE], cvf.h[0:1, 0:TILE], AF.Exp, scale=-1.0),
                [cvf], [cvf], accum=True)
            sop(lambda e: e.activation(cvf.h[0:1, 0:TILE], cvf.h[0:1, 0:TILE], AF.Ln, bias=1.0),
                [cvf], [cvf], accum=True)
            vop(lambda e: e.scalar_tensor_tensor(dtbc.h[0:1, :], pb[1].h[0:1, :], 0.0,
                                                 cvf.h[0:1, 0:TILE], A_.max, A_.add),
                [pb[1], cvf], [dtbc], accum=True)
            vop(lambda e: e.tensor_scalar(dtbc.h[0:1, :], dtbc.h[0:1, :], 0.1, 0.001,
                                          A_.min, A_.max), [dtbc], [dtbc], accum=True)
            top(lambda e: e.matmul(pb[1].h[0:16, :], ones1x16.h[:, :], dtbc.h[0:1, :],
                                   start=True, stop=True), [ones1x16, dtbc], [pb[1]])
            vop(lambda e: e.tensor_copy(dtbc.h[:, :], pb[1].h[0:16, :]), [pb[1]], [dtbc])
            vop(lambda e: e.tensor_scalar(dtA.h[:, :], dtbc.h[:, :], Acol.h[:, 0:1], 0.0,
                                          A_.mult, A_.min), [dtbc, Acol], [dtA])
            vop(lambda e: e.tensor_scalar_max(dtA.h[:, :], dtA.h[:, :], -20.0), [dtA], [dtA])
            vop(lambda e: e.tensor_copy(cumc.h[:, 0:1], carry.h[:, :]), [carry], [cumc])
            vop(lambda e: e.tensor_tensor_scan(
                cumc.h[:, 1:1 + TILE], dtA.h[:, :],
                bass.AP(zcol16.h, 0, [[1, 16], [0, TILE]]), cumc.h[:, 0:1],
                A_.add, A_.add), [dtA, zcol16, cumc], [cumc], accum=True)
            vop(lambda e: e.tensor_copy(carry.h[:, :], cumc.h[:, TILE:TILE + 1]),
                [cumc], [carry])
            vop(lambda e: e.tensor_mul(Bbar.h[:, :], dtbc.h[:, :], xdbl.h[0:16, 0:TILE]),
                [dtbc, xdbl], [Bbar])
            vop(lambda e: e.tensor_scalar(Bbar.h[:, :], Bbar.h[:, :], 10.0, -10.0,
                                          A_.min, A_.max), [Bbar], [Bbar])
            vop(lambda e: e.tensor_sub(
                Bhat.h[:, :].rearrange("p (c q) -> p c q", q=CHUNK),
                cumc.h[:, 1:1 + TILE].rearrange("p (c q) -> p c q", q=CHUNK),
                bass.AP(cumc.h, CHUNK, [[1 + TILE, 16], [CHUNK, CPT], [0, CHUNK]])),
                [cumc], [Bhat])
            sop(lambda e: e.activation(eexp.h[:, :], Bhat.h[:, :], AF.Exp, scale=-1.0),
                [Bhat], [eexp])
            vop(lambda e: e.tensor_mul(Bhat.h[:, :], eexp.h[:, :], Bbar.h[:, :]),
                [eexp, Bbar], [Bhat])
            if not prefix:
                vop(lambda e: e.tensor_copy(Bbarbf.h[:, :], Bbar.h[:, :]), [Bbar], [Bbarbf])
                vop(lambda e: e.tensor_sub(
                    eexp.h[:, :].rearrange("p (c q) -> p c q", q=CHUNK),
                    cumc.h[:, 1:1 + TILE].rearrange("p (c q) -> p c q", q=CHUNK),
                    bass.AP(cumc.h, 0, [[1 + TILE, 16], [CHUNK, CPT], [0, CHUNK]])),
                    [cumc], [eexp])
                sop(lambda e: e.activation(eexp.h[:, :], eexp.h[:, :], AF.Exp), [eexp], [eexp])
                vop(lambda e, t0=t0: e.tensor_mul(Chp_all.h[:, t0:t0 + TILE], eexp.h[:, :],
                                                  Cbf.h[:, :]), [eexp, Cbf],
                    [Chp_all], accum=True)
            for c in range(CPT):
                xt_ = xtm[c % 2]
                for dc in range(16):
                    transpose_to(xt_, dc * 128, xconv_t,
                                 xconv_t.h[:, dc * TILE + c * CHUNK:dc * TILE + (c + 1) * CHUNK],
                                 128)
                gk = (t0 + c * CHUNK) // CHUNK
                transpose_to(BhT, c * 16, Bhat, Bhat.h[:, c * CHUNK:(c + 1) * CHUNK], 16)
                if not prefix:
                    transpose_to(BbT_own, gk * 16, Bbarbf,
                                 Bbarbf.h[:, c * CHUNK:(c + 1) * CHUNK], 16)
                    transpose_to(CT_own, gk * 16, Cbf,
                                 Cbf.h[:, c * CHUNK:(c + 1) * CHUNK], 16)
                    top(lambda e, c=c: e.transpose(
                        pb[6].h[0:128, 0:16], cumc.h[:, 1 + c * CHUNK:1 + (c + 1) * CHUNK],
                        identf.h[0:16, 0:16]), [cumc, identf], [pb[6]])
                    vop(lambda e, gk=gk: e.tensor_copy(
                        cumT_own.h[:, gk * 16:(gk + 1) * 16], pb[6].h[0:128, 0:16]),
                        [pb[6]], [cumT_own], accum=True)
                vop(lambda e, c=c: e.tensor_scalar(
                    dcol.h[0:16, 0:1], cumc.h[:, (c + 1) * CHUNK:(c + 1) * CHUNK + 1],
                    cumc.h[:, c * CHUNK:c * CHUNK + 1], None, A_.subtract), [cumc], [dcol])
                sop(lambda e: e.activation(dcol.h[0:16, 0:1], dcol.h[0:16, 0:1], AF.Exp),
                    [dcol], [dcol])
                vop(lambda e: e.tensor_copy(dcol.h[32:48, 0:1], dcol.h[0:16, 0:1]),
                    [dcol], [dcol], accum=True)
                cu, nx = hcur[0], 1 - hcur[0]
                if not prefix:
                    B.dma("qg", "dhs", lambda q, gk=gk, cu=cu: q.dma_start(
                        out=T["h_sp"][:, gk * D_INNER:(gk + 1) * D_INNER],
                        in_=Hrow.h[HROW[cu]:HROW[cu] + 16, :]), [Hrow], [dh])
                HBANKS = (2, 3, 5, 0)
                for j in range(4):
                    bk = HBANKS[j]
                    top(lambda e, c=c, j=j, xt_=xt_, bk=bk: e.matmul(
                        pb[bk].h[0:16, :], BhT.h[:, c * 16:(c + 1) * 16],
                        xt_.h[:, j * 512:(j + 1) * 512], start=True, stop=True),
                        [BhT, xt_], [pb[bk]])
                for j in range(4):
                    bk = HBANKS[j]
                    vop(lambda e, j=j, cu=cu, nx=nx, bk=bk: e.scalar_tensor_tensor(
                        Hrow.h[HROW[nx]:HROW[nx] + 16, j * 512:(j + 1) * 512],
                        Hrow.h[HROW[cu]:HROW[cu] + 16, j * 512:(j + 1) * 512],
                        dcol.h[HROW[cu]:HROW[cu] + 16, 0:1], pb[bk].h[0:16, :],
                        A_.mult, A_.add),
                        [Hrow, dcol, pb[bk]], [Hrow], accum=True)
                hcur[0] = nx


        stageA(0)
        stageB(0)
        for ti in range(ntiles):
            if ti + 1 < ntiles:
                stageA(ti + 1)
            stageC(ti)
            if ti + 1 < ntiles:
                stageB(ti + 1)
            stageD(ti)
    if NT_PRE > 0:
        load_transpose_bf16(T["pre_Win"], D_INNER, D_MODEL, 0, D_INNER, "dwl")
        mixer_tiles(prefix=True)
    load_transpose_bf16(T["W_in"], 2 * D_INNER, D_MODEL, 0, 2 * D_INNER, "dwl")
    mixer_tiles(prefix=False)

    xq1g = xnbf
    k1g = sqcol
    k1u = rstd

    xq1gp = [Buf(xnfm.h, "xq1ga"), Buf(xnfm.h, "xq1gb")]

    def ffn_half(hf):
        wqst = sbf[1]
        for wi, wt in ((0, T["w_gate"]), (1, T["w_up"])):
            for rb in range(16):
                row0 = hf * 2048 + rb * 128
                stg = scr[0] if rb % 2 == 0 else scr[2]
                B.dma("qs", "dwf%d" % (rb % 2), lambda q, row0=row0, wt=wt, stg=stg:
                      q.dma_start(out=stg.h[:, 0:D_MODEL],
                                  in_=bass.AP(wt.tensor, row0 * D_MODEL,
                                              [[D_MODEL, 128], [1, D_MODEL]])), [], [stg])
                vop(lambda e, stg=stg, wi=wi: e.tensor_scalar(
                    lad_d.h[:, 0:D_MODEL], stg.h[:, 0:D_MODEL], thrp[wi].h[:, 0:1], None,
                    A_.is_ge), [stg, thrp[wi]], [lad_d])
                vop(lambda e, stg=stg, wi=wi: e.scalar_tensor_tensor(
                    wqst.h[:, 0:D_MODEL], stg.h[:, 0:D_MODEL], thrn[wi].h[:, 0:1],
                    lad_d.h[:, 0:D_MODEL], A_.is_le, A_.subtract),
                    [stg, thrn[wi], lad_d], [wqst])
                for kb in range(8):
                    transpose_to(wpool, wi * 16384 + kb * 2048 + rb * 128, wqst,
                                 wqst.h[:, kb * 128:(kb + 1) * 128], 128)
        def xq1_load(g):
            par = g % 2
            extra = [xnfm] if (hf == 0 and g == 0) else []
            B.dma("qs", "dq1r%d" % par, lambda q, g=g, par=par: q.dma_start(
                out=bass.AP(xnfm.h, par * 1024, [[4096, 128], [128, 8], [1, 128]]),
                in_=bass.AP(T["xq1_sp"].tensor, g * 128,
                            [[8 * n_own, 128], [n_own, 8], [1, 128]])),
                [dq1], [xq1gp[par]] + extra)

        xq1_load(0)
        pend = [None]
        for g in range(NG):
            if g + 1 < NG:
                xq1_load(g + 1)
            xq1g_ap = lambda kc, g=g: bass.AP(
                xnfm.h, (g % 2) * 1024 + kc * 128, [[4096, 128], [1, 128]])
            xq1g_buf = xq1gp[g % 2]
            vop(lambda e, g=g: e.tensor_mul(k1g.h[:, 0:1], swneg[0].h[:, 0:1],
                                            c1_all.h[:, g:g + 1]), [swneg[0], c1_all], [k1g])
            vop(lambda e, g=g: e.tensor_mul(k1u.h[:, 0:1], swneg[1].h[:, 0:1],
                                            c1_all.h[:, g:g + 1]), [swneg[1], c1_all], [k1u])
            for nh in range(4):
                for kc in range(8):
                    top(lambda e, nh=nh, kc=kc, a=xq1g_ap: e.matmul(
                        pb[0].h[:, :], a(kc),
                        wpool.h[:, kc * 2048 + nh * 512:kc * 2048 + (nh + 1) * 512],
                        start=(kc == 0), stop=(kc == 7)), [xq1g_buf, wpool], [pb[0]],
                        accum=(kc > 0))
                sop(lambda e: e.activation(sgb.h[:, :], pb[0].h[:, :], AF.Silu,
                                           scale=k1g.h[:, 0:1]), [pb[0], k1g], [sgb])
                for kc in range(8):
                    top(lambda e, nh=nh, kc=kc, a=xq1g_ap: e.matmul(
                        pb[1].h[:, :], a(kc),
                        wpool.h[:, 16384 + kc * 2048 + nh * 512:
                                16384 + kc * 2048 + (nh + 1) * 512],
                        start=(kc == 0), stop=(kc == 7)), [xq1g_buf, wpool], [pb[1]],
                        accum=(kc > 0))
                sop(lambda e: e.activation(upb.h[:, :], pb[1].h[:, :], AF.Identity,
                                           scale=k1u.h[:, 0:1]), [pb[1], k1u], [upb])
                vop(lambda e, nh=nh: e.tensor_mul(th_a.h[:, nh * 512:(nh + 1) * 512],
                                                  sgb.h[:, :], upb.h[:, :]), [sgb, upb],
                    [th_a], accum=True)
            if hf == 0:
                B.dma("qg", "dt1s", lambda q, g=g: q.dma_start(
                    out=bass.AP(T["t1_sp"].tensor, g * 128,
                                [[16 * n_own, 128], [n_own, 16], [1, 128]]),
                    in_=bass.AP(th_a.h, 0, [[2048, 128], [128, 16], [1, 128]])),
                    [th_a], [dt1])
            else:
                B.dma("qs", "dt1r", lambda q, g=g: q.dma_start(
                    out=bass.AP(th_b.h, 0, [[2048, 128], [128, 16], [1, 128]]),
                    in_=bass.AP(T["t1_sp"].tensor, g * 128,
                                [[16 * n_own, 128], [n_own, 16], [1, 128]])), [dt1], [th_b])
                vop(lambda e: e.reduce_max(amx.h[:, 0:1], th_a.h[:, :], axis=AX.X,
                                           apply_absolute_value=True), [th_a], [amx])
                vop(lambda e: e.reduce_max(m2col.h[:, 0:1], th_b.h[:, :], axis=AX.X,
                                           apply_absolute_value=True), [th_b], [m2col])
                vop(lambda e: e.tensor_max(amx.h[:, 0:1], amx.h[:, 0:1], m2col.h[:, 0:1]),
                    [amx, m2col], [amx])
                vop(lambda e: e.tensor_scalar_max(amx.h[:, 0:1], amx.h[:, 0:1], 1e-5),
                    [amx], [amx])
                vop(lambda e: e.reciprocal(scol.h[:, 0:1], amx.h[:, 0:1]), [amx], [scol])
                vop(lambda e: e.tensor_scalar_mul(scol.h[:, 0:1], scol.h[:, 0:1], 127.0),
                    [scol], [scol])
                vop(lambda e, g=g: e.reciprocal(c2_all.h[:, g:g + 1], scol.h[:, 0:1]),
                    [scol], [c2_all], accum=True)
                if pend[0] is not None:
                    pend[0]()
                for half2, tb in ((0, th_b), (1, th_a)):
                    for co in (0, 1024):
                        quant_cast(lambda tb=tb, co=co: tb.h[:, co:co + 1024], [tb],
                                   1024, scol, (sbf[half2], co), (scr[half2], co),
                                   (wb16[0], co), (wb16[1], co), fp16_src=True)

                def mk_pend(g=g):
                    def flush():
                        for half2 in (0, 1):
                            for kc in range(16):
                                transpose_to(wb16[2], kc * 128, sbf[half2],
                                             sbf[half2].h[:, kc * 128:(kc + 1) * 128],
                                             128)
                            B.dma("qg", "dxq2", lambda q, half2=half2, g=g: q.dma_start(
                                out=bass.AP(T["xq2_sp"].tensor,
                                            half2 * 16 * n_own + g * 128,
                                            [[32 * n_own, 128], [n_own, 16], [1, 128]]),
                                in_=bass.AP(wb16[2].h, 0, [[2048, 128], [128, 16],
                                                           [1, 128]])),
                                [wb16[2]], [dq2])
                    return flush
                pend[0] = mk_pend()

        if hf == 1 and pend[0] is not None:
            pend[0]()

    ffn_half(0)
    ffn_half(1)

    load_transpose_bf16(T["W_out"], D_MODEL, D_INNER, 0, D_MODEL, "dwl")

    wpar = Buf(wpool.h, "wpar")
    XCK = [(xnfm, 0), (wpar, 16384)]
    ZK = [(xnfm, 2048), (wpar, 18432)]
    HK = [(xconv_t, 0), (wpar, 20480)]
    SSL = [(xconv_t, 2048), (wpar, 22528)]
    ESC = [(wb16[0], 0), (wpar, 22656)]
    CBC = [(wb16[1], 0), (wpar, 24704)]
    YS = [(wb16[2], 0), (wpar, 26752)]
    XTMK = [(xtm[0], 0), (wpar, 28800)]
    XROW = [xg[0], xg[1]]
    DARG = [scr[0], scr[2]]
    cumbc = scr[1]
    xat = xn32

    def p4_xrow(k):
        par = k % 2
        B.dma("qs", "dxr%d" % par, lambda q, k=k, par=par: q.dma_start(
            out=XROW[par].h[:, :],
            in_=bass.AP(T["x_own"].tensor, k * 128 * D_MODEL,
                        [[D_MODEL, 128], [1, D_MODEL]])), [], [XROW[par]])

    def p4_loads(k):
        par = k % 2
        xb, xo = XCK[par]
        zb, zo = ZK[par]
        hb, ho = HK[par]
        db_ = DARG[par]
        extra = [wpool] if k == 1 else (xq1gp if k == 0 else [])
        B.dma("qs", "dxck%d" % par, lambda q, k=k, xb=xb, xo=xo: q.dma_start(
            out=bass.AP(xb.h, xo, [[xb.h.shape[1], 128], [128, 16], [1, 128]]),
            in_=bass.AP(T["xc_sp"].tensor, k * 128,
                        [[16 * n_own, 128], [n_own, 16], [1, 128]])), [dxc], [xb] + extra)
        B.dma("qs", "dzk%d" % par, lambda q, k=k, zb=zb, zo=zo: q.dma_start(
            out=bass.AP(zb.h, zo, [[zb.h.shape[1], 128], [128, 16], [1, 128]]),
            in_=bass.AP(T["z_sp"].tensor, k * 128,
                        [[16 * n_own, 128], [n_own, 16], [1, 128]])), [dz], [zb])
        B.dma("qs", "dhk%d" % par, lambda q, k=k, hb=hb, ho=ho: q.dma_start(
            out=bass.AP(hb.h, ho, [[hb.h.shape[1], 16], [1, D_INNER]]),
            in_=T["h_sp"][:, k * D_INNER:(k + 1) * D_INNER]), [dh], [hb])
        B.dma("qg", "dcf%d" % par, lambda q, k=k, db_=db_: q.dma_start(
            out=db_.h[0:1, 0:2048],
            in_=bass.AP(cumT_own.h, k * 16, [[NCH * 16, 128], [1, 16]])),
            [cumT_own], [db_])
        B.dma("qg", "dcb%d" % par, lambda q, db_=db_: q.dma_start(
            out=cumbc.h[:, :],
            in_=bass.AP(db_.h, 0, [[1, 1], [0, 128], [1, 2048]])),
            [db_], [cumbc])
        B.dma("qg", "dclf%d" % par, lambda q, k=k: q.dma_start(
            out=sbf[0].h[0:1, 0:2048],
            in_=bass.AP(CT_own.h, k * 16, [[NCH * 16, 128], [1, 16]])),
            [CT_own], [sbf[0]])
        cb, co = CBC[par]
        B.dma("qg", "dclb%d" % par, lambda q, cb=cb, co=co: q.dma_start(
            out=bass.AP(cb.h, co, [[cb.h.shape[1], 128], [1, 2048]]),
            in_=bass.AP(sbf[0].h, 0, [[1, 1], [0, 128], [1, 2048]])), [sbf[0]], [cb])

    def p4_scores(k):
        par = k % 2
        db_ = DARG[par]
        eb, eo = ESC[par]
        cb, co = CBC[par]
        sb, so = SSL[par]
        vop(lambda e, k=k, db_=db_: e.tensor_sub(
            db_.h[:, :].rearrange("p (j s) -> p j s", s=16),
            cumbc.h[:, :].rearrange("p (j s) -> p j s", s=16),
            cumT_own.h[:, k * 16:(k + 1) * 16][:, None, :].broadcast_to([128, 128, 16])),
            [cumbc, cumT_own], [db_], accum=True)
        vop(lambda e, db_=db_: e.tensor_sub(
            db_.h[:, :].rearrange("p (j s) -> p j s", s=16),
            db_.h[:, :].rearrange("p (j s) -> p j s", s=16),
            maskb.h[:, :, None].broadcast_to([128, 128, 16])), [db_, maskb], [db_],
            accum=True)
        sop(lambda e, db_=db_, eb=eb, eo=eo: e.activation(
            eb.h[:, eo:eo + 2048], db_.h[:, :], AF.Exp), [db_], [eb])
        vop(lambda e, k=k, eb=eb, eo=eo: e.tensor_mul(
            bass.AP(eb.h, eo, [[eb.h.shape[1], 128], [16, 128], [1, 16]]),
            bass.AP(eb.h, eo, [[eb.h.shape[1], 128], [16, 128], [1, 16]]),
            BbT_own.h[:, k * 16:(k + 1) * 16][:, None, :].broadcast_to([128, 128, 16])),
            [eb, BbT_own], [eb], accum=True)
        vop(lambda e, eb=eb, eo=eo, cb=cb, co=co: e.tensor_mul(
            eb.h[:, eo:eo + 2048], eb.h[:, eo:eo + 2048], cb.h[:, co:co + 2048]),
            [eb, cb], [eb], accum=True)

        def _score_reduce(e, eb=eb, eo=eo, sb=sb, so=so):
            with B.nc.allow_low_precision(reason="bf16 score tile for bf16 matmul"):
                return e.reduce_sum(
                    sb.h[0:128, so:so + 128],
                    bass.AP(eb.h, eo, [[eb.h.shape[1], 128], [16, 128], [1, 16]]),
                    axis=AX.X)
        vop(_score_reduce, [eb], [sb], accum=True)

    def p4_xtm(k):
        par = k % 2
        xb, xo = XCK[par]
        tb, to = XTMK[par]
        for dc in range(16):
            transpose_to(tb, to + dc * 128, xb,
                         bass.AP(xb.h, xo + dc * 128, [[xb.h.shape[1], 128], [1, 128]]),
                         128)

    def p4_y(k):
        par = k % 2
        tb, to = XTMK[par]
        hb, ho = HK[par]
        sb, so = SSL[par]
        yb, yo = YS[par]
        for dc in range(16):
            bank = 2 + dc % 2
            top(lambda e, dc=dc, bank=bank, tb=tb, to=to, sb=sb, so=so: e.matmul(
                pb[bank].h[:, 0:128],
                bass.AP(tb.h, to + dc * 128, [[tb.h.shape[1], 128], [1, 128]]),
                sb.h[0:128, so:so + 128], start=True, stop=False),
                [tb, sb], [pb[bank]])
            top(lambda e, dc=dc, bank=bank, k=k, hb=hb, ho=ho: e.matmul(
                pb[bank].h[:, 0:128],
                bass.AP(hb.h, ho + dc * 128, [[hb.h.shape[1], 16], [1, 128]]),
                Chp_all.h[:, k * 128:(k + 1) * 128], start=False, stop=True),
                [hb, Chp_all], [pb[bank]], accum=True)
            vop(lambda e, dc=dc, bank=bank, yb=yb, yo=yo: e.tensor_copy(
                yb.h[:, yo + dc * 128:yo + (dc + 1) * 128], pb[bank].h[:, 0:128]),
                [pb[bank]], [yb], accum=True)

    def p4_ypost(k):
        par = k % 2
        xb, xo = XCK[par]
        zb, zo = ZK[par]
        eb, eo = ESC[par]
        cb, co = CBC[par]
        yb, yo = YS[par]
        sop(lambda e, zb=zb, zo=zo, cb=cb, co=co: e.activation(
            cb.h[:, co:co + 2048], zb.h[:, zo:zo + 2048], AF.Silu), [zb], [cb],
            accum=True)
        vop(lambda e, yb=yb, yo=yo, cb=cb, co=co: e.tensor_mul(
            yb.h[:, yo:yo + 2048], yb.h[:, yo:yo + 2048], cb.h[:, co:co + 2048]),
            [yb, cb], [yb], accum=True)
        vop(lambda e, xb=xb, xo=xo, eb=eb, eo=eo: e.tensor_mul(
            bass.AP(eb.h, eo, [[eb.h.shape[1], 128], [128, 16], [1, 128]]),
            bass.AP(xb.h, xo, [[xb.h.shape[1], 128], [128, 16], [1, 128]]),
            Dp.h[:, :, None].broadcast_to([128, 16, 128])), [xb, Dp], [eb], accum=True)
        vop(lambda e, yb=yb, yo=yo, eb=eb, eo=eo: e.tensor_add(
            yb.h[:, yo:yo + 2048], yb.h[:, yo:yo + 2048], eb.h[:, eo:eo + 2048]),
            [yb, eb], [yb], accum=True)

    def p4_attn(k):
        par = k % 2
        yb, yo = YS[par]
        for nh in range(2):
            for dc in range(16):
                top(lambda e, nh=nh, dc=dc, yb=yb, yo=yo: e.matmul(
                    pb[nh].h[:, :],
                    bass.AP(yb.h, yo + dc * 128, [[yb.h.shape[1], 128], [1, 128]]),
                    wpool.h[:, dc * D_MODEL + nh * 512:dc * D_MODEL + (nh + 1) * 512],
                    start=(dc == 0), stop=(dc == 15)), [yb, wpool], [pb[nh]],
                    accum=(dc != 0))
        for nh in range(2):
            vop(lambda e, nh=nh, par=par: e.tensor_add(
                xat.h[:, nh * 512:(nh + 1) * 512],
                XROW[par].h[:, nh * 512:(nh + 1) * 512],
                pb[nh].h[:, :]), [XROW[par], pb[nh]], [xat], accum=True)
        B.dma("qg", "dxats", lambda q, k=k: q.dma_start(
            out=bass.AP(T["xat_sp"].tensor, k * 128,
                        [[8 * n_own, 128], [n_own, 8], [1, 128]]),
            in_=bass.AP(xat.h, 0, [[1024, 128], [128, 8], [1, 128]])), [xat], [dxat])

    p4_loads(0)
    p4_xrow(0)
    p4_scores(0)
    p4_xtm(0)
    if NCH > 1:
        p4_loads(1)
        p4_xrow(1)
    for k in range(NCH):
        p4_y(k)
        p4_ypost(k)
        if k + 1 < NCH:
            p4_xtm(k + 1)
        p4_attn(k)
        if k + 1 < NCH:
            p4_scores(k + 1)
        if k + 2 < NCH:
            p4_loads(k + 2)
            p4_xrow(k + 2)

    wqst = sbf[1]
    for rb in range(D_MODEL // 128):
        for sg in range(2):
            stg = scr[0] if sg == 0 else scr[2]
            extra = [wpar] if rb == 0 else []
            B.dma("qs", "dwf%d" % sg, lambda q, rb=rb, sg=sg, stg=stg: q.dma_start(
                out=stg.h[:, 0:2048],
                in_=bass.AP(T["w_down"].tensor, rb * 128 * FFN + sg * 2048,
                            [[FFN, 128], [1, 2048]])), [], [stg] + extra)
            vop(lambda e, stg=stg: e.tensor_scalar(
                lad_d.h[:, 0:2048], stg.h[:, 0:2048], thrp[2].h[:, 0:1], None, A_.is_ge),
                [stg, thrp[2]], [lad_d])
            vop(lambda e, stg=stg: e.scalar_tensor_tensor(
                wqst.h[:, 0:2048], stg.h[:, 0:2048], thrn[2].h[:, 0:1],
                lad_d.h[:, 0:2048], A_.is_le, A_.subtract), [stg, thrn[2], lad_d], [wqst])
            for kb in range(16):
                transpose_to(wpool, (sg * 16 + kb) * D_MODEL + rb * 128, wqst,
                             wqst.h[:, kb * 128:(kb + 1) * 128], 128)

    xatr = [xg[0]] * 2
    outb = [xn32] * 2
    k2c = nmr

    for k in range(NCH):
        par = k % 2
        B.dma("qs", "dq2r%d" % par, lambda q, k=k, par=par: q.dma_start(
            out=bass.AP(wb16[0].h, 0, [[2048, 128], [128, 16], [1, 128]]),
            in_=bass.AP(T["xq2_sp"].tensor, k * 128,
                        [[32 * n_own, 128], [n_own, 16], [1, 128]])), [dq2], [wb16[0]])
        B.dma("qs", "dq2r%d" % par, lambda q, k=k, par=par: q.dma_start(
            out=bass.AP(wb16[1].h, 0, [[2048, 128], [128, 16], [1, 128]]),
            in_=bass.AP(T["xq2_sp"].tensor, 16 * n_own + k * 128,
                        [[32 * n_own, 128], [n_own, 16], [1, 128]])), [dq2], [wb16[1]])
        B.dma("qs", "dxatr%d" % par, lambda q, k=k, par=par: q.dma_start(
            out=bass.AP(xatr[par].h, 0, [[1024, 128], [128, 8], [1, 128]]),
            in_=bass.AP(T["xat_sp"].tensor, k * 128,
                        [[8 * n_own, 128], [n_own, 8], [1, 128]])), [dxat], [xatr[par]])
        for nh in range(2):
            for kc in range(32):
                src = wb16[kc // 16]
                top(lambda e, nh=nh, kc=kc, src=src: e.matmul(
                    pb[nh].h[:, :], src.h[:, (kc % 16) * 128:((kc % 16) + 1) * 128],
                    wpool.h[:, kc * D_MODEL + nh * 512:kc * D_MODEL + (nh + 1) * 512],
                    start=(kc == 0), stop=(kc == 31)), [src, wpool], [pb[nh]],
                    accum=(kc != 0))
        vop(lambda e, k=k: e.tensor_mul(k2c.h[:, 0:1], swneg[2].h[:, 0:1],
                                        c2_all.h[:, k:k + 1]), [swneg[2], c2_all], [k2c])
        ob = outb[par]
        for nh in range(2):
            vop(lambda e, nh=nh, ob=ob, par=par: e.scalar_tensor_tensor(
                ob.h[:, nh * 512:(nh + 1) * 512], pb[nh].h[:, :], k2c.h[:, 0:1],
                xatr[par].h[:, nh * 512:(nh + 1) * 512], A_.mult, A_.add),
                [pb[nh], k2c, xatr[par]], [ob], accum=True)
        B.dma("qg", "dout", lambda q, k=k, ob=ob: q.dma_start(
            out=bass.AP(T["out"].tensor, k * 128 * D_MODEL, [[D_MODEL, 128], [1, D_MODEL]]),
            in_=ob.h[:, :]), [ob], [])

    B.wait_stream("qs", "dout")


_NC_CACHE = {}


def _get_nc(n_own, n_pre):
    key = (n_own, n_pre)
    if key not in _NC_CACHE:
        _NC_CACHE[key] = build_core_program(n_own, n_pre)
    return _NC_CACHE[key]


def kernel(L0_attn_x, L0_attn_W_in, L0_attn_conv_w, L0_attn_conv_b, L0_attn_W_xproj,
           L0_attn_A_log, L0_attn_D_param, L0_attn_W_out, L0_ffn_w_gate, L0_ffn_w_up,
           L0_ffn_w_down):
    x = np.ascontiguousarray(L0_attn_x, dtype=np.float32)
    b_, s_, d_ = x.shape
    half = s_ // 2
    nc = _get_nc(half, half)
    shared = {
        "W_in": np.ascontiguousarray(L0_attn_W_in, np.float32),
        "conv_w": np.ascontiguousarray(L0_attn_conv_w, np.float32),
        "conv_b": np.ascontiguousarray(L0_attn_conv_b, np.float32),
        "W_xproj": np.ascontiguousarray(L0_attn_W_xproj, np.float32),
        "A_log": np.ascontiguousarray(L0_attn_A_log, np.float32),
        "D_param": np.ascontiguousarray(L0_attn_D_param, np.float32),
        "W_out": np.ascontiguousarray(L0_attn_W_out, np.float32),
        "w_gate": np.ascontiguousarray(L0_ffn_w_gate, np.float32),
        "w_up": np.ascontiguousarray(L0_ffn_w_up, np.float32),
        "w_down": np.ascontiguousarray(L0_ffn_w_down, np.float32),
        "ident": np.eye(128, dtype=bfdt),
        "identf": np.eye(16, dtype=np.float32),
        "maskbig": np.ascontiguousarray(
            (np.triu(np.ones((128, 128), np.float32), 1).T * MASK_BIG).astype(bfdt)),
    }
    zwin = np.zeros((D_INNER, D_MODEL), np.float32)
    zcb = np.zeros((D_INNER,), np.float32)
    zx = np.zeros((half, D_MODEL), np.float32)
    in_maps = []
    for c in range(8):
        b = c // 2
        h = c % 2
        m = dict(shared)
        m["x_own"] = np.ascontiguousarray(x[b, h * half:(h + 1) * half])
        if h == 1:
            m["x_pre"] = np.ascontiguousarray(x[b, 0:half])
            m["pre_Win"] = shared["W_in"][0:D_INNER].copy()
            m["pre_conv_b"] = shared["conv_b"].copy()
        else:
            m["x_pre"] = zx
            m["pre_Win"] = zwin
            m["pre_conv_b"] = zcb
        in_maps.append(m)
    res = run_bass_kernel_spmd(nc, in_maps, core_ids=list(range(8)))
    out = np.empty((b_, s_, d_), np.float32)
    for c in range(8):
        b = c // 2
        h = c % 2
        out[b, h * half:(h + 1) * half] = np.asarray(res.results[c]["out"]).reshape(half, d_)
    return out


# version 16
# speedup vs baseline: 1.0479x; 1.0256x over previous
import contextlib
import numpy as np
import ml_dtypes
import concourse.bass as bass
import concourse.mybir as mybir
from concourse.bass_utils import run_bass_kernel_spmd

F32 = mybir.dt.float32
BF16 = mybir.dt.bfloat16
FP16 = mybir.dt.float16
I8 = mybir.dt.int8
A_ = mybir.AluOpType
AF = mybir.ActivationFunctionType
AX = mybir.AxisListType
bfdt = ml_dtypes.bfloat16

D_MODEL = 1024
D_INNER = 2048
D_STATE = 16
FFN = 4096
TILE = 512
CHUNK = 128
MASK_BIG = 32768.0
LVLS = [128.0, 64.0, 32.0, 16.0, 8.0, 4.0, 2.0, 1.0]


class Buf:
    __slots__ = ("h", "writers", "readers", "name")

    def __init__(self, h, name):
        self.h = h
        self.name = name
        self.writers = {}
        self.readers = {}


class Builder:
    ENGINES = ("te", "ve", "se", "qs", "qg")

    def __init__(self, nc, stack):
        self.nc = nc
        self.stack = stack
        self.items = {e: [] for e in self.ENGINES}
        self.cnt = {e: 0 for e in self.ENGINES}
        self.sems = {e: stack.enter_context(nc.semaphore("sem_" + e)) for e in self.ENGINES}
        self.extra = {}
        self.waited = {e: {} for e in self.ENGINES}

    def sbuf(self, name, shape, dtype):
        return Buf(self.stack.enter_context(self.nc.sbuf_tensor(name, shape, dtype)), name)

    def psum(self, name, shape, dtype=F32):
        return Buf(self.stack.enter_context(self.nc.psum_tensor(name, shape, dtype)), name)

    def _sem_of(self, p):
        return self.sems[p] if p in self.sems else self.extra[p][0]

    def _wait(self, eng, p, v):
        w = self.waited[eng]
        if w.get(p, -1) >= v:
            return
        w[p] = v
        self.items[eng].append(("w", self._sem_of(p), v))

    def _edges(self, eng, ins, outs, accum):
        need = {}

        def add(p, v):
            if v > need.get(p, -1):
                need[p] = v

        for b in ins:
            if b is not None:
                for p, v in b.writers.items():
                    add(p, v)
        for b in outs:
            if b is None:
                continue
            for p, v in b.writers.items():
                add(p, v)
            for p, v in b.readers.items():
                add(p, v)
        for p in sorted(need):
            v = self.extra[p][1] if p in self.extra else need[p]
            self._wait(eng, p, v)

    def op(self, eng, fn, ins, outs, accum=False):
        self._edges(eng, ins, outs, accum)
        self.cnt[eng] += 1
        v = self.cnt[eng]
        self.items[eng].append(("o", fn, self.sems[eng], 1))
        for b in ins:
            if b is not None:
                b.readers[eng] = v
        for b in outs:
            if b is not None:
                b.writers[eng] = v
        return v

    def dma(self, queue, sem_name, fn, ins, outs):
        self._edges(queue, ins, outs, False)
        if sem_name not in self.extra:
            self.extra[sem_name] = [self.stack.enter_context(self.nc.semaphore("s_" + sem_name)), 0]
        ent = self.extra[sem_name]
        ent[1] += 16
        v = ent[1]
        self.items[queue].append(("o", fn, ent[0], 16))
        for b in ins:
            if b is not None:
                b.readers[sem_name] = v
        for b in outs:
            if b is not None:
                b.writers[sem_name] = v
        return v

    def wait_stream(self, eng, p):
        v = self.cnt[p] if p in self.sems else (self.extra[p][1] if p in self.extra else 0)
        if v > 0:
            self._wait(eng, p, v)

    def build_block(self):
        block = self.stack.enter_context(self.nc.Block())

        def make(name):
            seq = self.items[name]

            def body(eng):
                for it in seq:
                    if it[0] == "w":
                        eng.wait_ge(it[1], it[2])
                    else:
                        it[1](eng).then_inc(it[2], it[3])
            return body

        block.tensor(make("te"))
        block.vector(make("ve"))
        block.scalar(make("se"))
        block.sync(make("qs"))
        block.gpsimd(make("qg"))


def build_core_program(n_own, n_pre):
    NCH = n_own // CHUNK
    nc = bass.Bass("TRN2", target_bir_lowering=False)
    DI = nc.dram_tensor
    T = {}
    T["x_own"] = DI("x_own", [n_own, D_MODEL], F32, kind="ExternalInput").ap()
    T["x_pre"] = DI("x_pre", [max(n_pre, 1), D_MODEL], F32, kind="ExternalInput").ap()
    T["W_in"] = DI("W_in", [2 * D_INNER, D_MODEL], F32, kind="ExternalInput").ap()
    T["pre_Win"] = DI("pre_Win", [D_INNER, D_MODEL], F32, kind="ExternalInput").ap()
    T["conv_w"] = DI("conv_w", [D_INNER, 4], F32, kind="ExternalInput").ap()
    T["conv_b"] = DI("conv_b", [D_INNER], F32, kind="ExternalInput").ap()
    T["pre_conv_b"] = DI("pre_conv_b", [D_INNER], F32, kind="ExternalInput").ap()
    T["W_xproj"] = DI("W_xproj", [2 * D_STATE + 1, D_INNER], F32, kind="ExternalInput").ap()
    T["A_log"] = DI("A_log", [D_STATE], F32, kind="ExternalInput").ap()
    T["D_param"] = DI("D_param", [D_INNER], F32, kind="ExternalInput").ap()
    T["W_out"] = DI("W_out", [D_MODEL, D_INNER], F32, kind="ExternalInput").ap()
    T["w_gate"] = DI("w_gate", [FFN, D_MODEL], F32, kind="ExternalInput").ap()
    T["w_up"] = DI("w_up", [FFN, D_MODEL], F32, kind="ExternalInput").ap()
    T["w_down"] = DI("w_down", [D_MODEL, FFN], F32, kind="ExternalInput").ap()
    T["ident"] = DI("ident", [128, 128], BF16, kind="ExternalInput").ap()
    T["identf"] = DI("identf", [16, 16], F32, kind="ExternalInput").ap()
    T["maskbig"] = DI("maskbig", [128, 128], BF16, kind="ExternalInput").ap()
    T["out"] = DI("out", [n_own, D_MODEL], F32, kind="ExternalOutput").ap()
    T["z_sp"] = DI("z_sp", [128, 16, n_own], BF16).ap()
    T["xc_sp"] = DI("xc_sp", [128, 16, n_own], BF16).ap()
    T["xq1_sp"] = DI("xq1_sp", [128, 8, n_own], BF16).ap()
    T["xq2_sp"] = DI("xq2_sp", [128, 32, n_own], BF16).ap()
    T["t1_sp"] = DI("t1_sp", [128, 16, n_own], FP16).ap()
    T["xat_sp"] = DI("xat_sp", [128, 8, n_own], F32).ap()
    T["h_sp"] = DI("h_sp", [D_STATE, NCH * D_INNER], BF16).ap()

    with contextlib.ExitStack() as stack:
        B = Builder(nc, stack)
        _emit(B, T, n_own, n_pre)
        B.build_block()
    return nc


def _emit(B, T, n_own, n_pre):
    NCH = n_own // CHUNK
    NT_OWN = n_own // TILE
    NT_PRE = n_pre // TILE
    CPT = TILE // CHUNK
    NG = n_own // 128
    GPT = TILE // 128

    def vop(fn, ins, outs, accum=False):
        B.op("ve", fn, ins, outs, accum)

    def sop(fn, ins, outs, accum=False):
        B.op("se", fn, ins, outs, accum)

    def top(fn, ins, outs, accum=False):
        B.op("te", fn, ins, outs, accum)

    dz = Buf(None, "z_sp")
    dxc = Buf(None, "xc_sp")
    dq1 = Buf(None, "xq1_sp")
    dq2 = Buf(None, "xq2_sp")
    dt1 = Buf(None, "t1_sp")
    dxat = Buf(None, "xat_sp")
    dh = Buf(None, "h_sp")

    wpool = B.sbuf("wpool", [128, 32768], BF16)
    scr = [B.sbuf("scr%d" % i, [128, 2048], F32) for i in range(3)]
    sbf = [B.sbuf("sbf%d" % i, [128, 2048], BF16) for i in range(2)]
    wb16 = [B.sbuf("wb16_%d" % i, [128, 2048], BF16) for i in range(3)]
    WxT = B.sbuf("WxT", [128, 16 * 33], BF16)
    convw = B.sbuf("convw", [128, 16 * 4], F32)
    convb = B.sbuf("convb", [128, 16], F32)
    pconvb = B.sbuf("pconvb", [128, 16], F32)
    Dp = B.sbuf("Dp", [128, 16], F32)
    Acol = B.sbuf("Acol", [16, 1], F32)
    ident = B.sbuf("id16", [128, 128], BF16)
    identf = B.sbuf("idf32", [16, 16], F32)
    maskb = B.sbuf("maskb", [128, 128], BF16)
    ones1x16 = B.sbuf("ones1x16", [1, 16], F32)
    screp = B.sbuf("screp", [128, 2], F32)
    onesc = B.sbuf("onesc", [128, 1], F32)
    zcol16 = B.sbuf("zcol16", [16, 1], F32)
    cumT_own = B.sbuf("cumT_own", [128, NCH * 16], F32)
    BbT_own = B.sbuf("BbT_own", [128, NCH * 16], BF16)
    CT_own = B.sbuf("CT_own", [128, NCH * 16], BF16)
    Chp_all = B.sbuf("Chp_all", [16, n_own], BF16)
    c1_all = B.sbuf("c1_all", [128, NG], F32)
    c2_all = B.sbuf("c2_all", [128, NG], F32)
    swneg = [B.sbuf("swneg%d" % i, [128, 1], F32) for i in range(3)]
    thrp = [B.sbuf("thrp%d" % i, [128, 1], F32) for i in range(3)]
    thrn = [B.sbuf("thrn%d" % i, [128, 1], F32) for i in range(3)]
    cumc = B.sbuf("cumc", [16, 1 + TILE], F32)
    carry = B.sbuf("carry", [16, 1], F32)
    epscol = B.sbuf("epscol", [128, 1], F32)
    Hrow = scr[2]
    HROW = [0, 32]

    pb = [B.psum("pb%d" % i, [128, 512], F32) for i in range(6)]
    pb.append(pb[4])
    pt = [B.psum("pt%d" % i, [128, 512], BF16) for i in range(2)]
    PTOFF = [0, 0]

    B.dma("qs", "dsm", lambda q: q.dma_start(
        out=bass.AP(convw.h, 0, [[64, 128], [4, 16], [1, 4]]),
        in_=bass.AP(T["conv_w"].tensor, 0, [[4, 128], [512, 16], [1, 4]])), [], [convw])
    _pvec_loads = []
    for _nm, _dst in (("conv_b", convb), ("pre_conv_b", pconvb), ("D_param", Dp)):
        _pvec_loads.append((_nm, _dst))
    B.dma("qs", "dsm", lambda q: q.dma_start(out=Acol.h[:, 0:1], in_=T["A_log"][:, None]),
          [], [Acol])
    B.dma("qs", "dsm", lambda q: q.dma_start(out=ident.h[:, :], in_=T["ident"]), [], [ident])
    B.dma("qs", "dsm", lambda q: q.dma_start(out=identf.h[:, :], in_=T["identf"]), [], [identf])
    B.dma("qs", "dsm", lambda q: q.dma_start(out=maskb.h[:, :], in_=T["maskbig"]), [], [maskb])
    sop(lambda e: e.activation(Acol.h[:, 0:1], Acol.h[:, 0:1], AF.Exp), [Acol], [Acol])
    vop(lambda e: e.tensor_scalar_mul(Acol.h[:, 0:1], Acol.h[:, 0:1], -1.0), [Acol], [Acol])
    for bz, val in ((zcol16, 0.0), (ones1x16, 1.0), (onesc, 1.0)):
        vop(lambda e, bz=bz, val=val: e.memset(bz.h[:, :], val), [], [bz])
    vop(lambda e: e.memset(carry.h[:, :], 0.0), [], [carry])
    vop(lambda e: e.memset(epscol.h[:, :], 1e-5), [], [epscol])
    vop(lambda e: e.memset(Hrow.h[0:16, :], 0.0), [], [Hrow])

    trcnt = [0]

    def transpose_to(dst, dst_col, src_buf, src_ap, n, evac="ve"):
        slot = trcnt[0] % 2
        ps = pt[slot]
        off = PTOFF[slot]
        trcnt[0] += 1
        top(lambda e, ps=ps, src_ap=src_ap, n=n, off=off: e.transpose(
            ps.h[0:128, off:off + n], src_ap, ident.h[0:n, 0:n]),
            [src_buf, ident], [ps])
        if evac == "se":
            sop(lambda e, ps=ps, dst=dst, dst_col=dst_col, n=n, off=off: e.activation(
                dst.h[:, dst_col:dst_col + n], ps.h[0:128, off:off + n], AF.Identity),
                [ps], [dst], accum=True)
        else:
            vop(lambda e, ps=ps, dst=dst, dst_col=dst_col, n=n, off=off: e.tensor_copy(
                dst.h[:, dst_col:dst_col + n], ps.h[0:128, off:off + n]), [ps], [dst],
                accum=True)

    def load_transpose_bf16(src_t, rows, cols, dst_off, dst_span, tag):
        for rb in range(rows // 128):
            stg = sbf[rb % 2]
            B.dma("qg", tag + str(rb % 2), lambda q, rb=rb, stg=stg, cols=cols, src_t=src_t:
                  q.dma_start(out=stg.h[:, 0:cols],
                              in_=bass.AP(src_t.tensor, rb * 128 * cols,
                                          [[cols, 128], [1, cols]])), [], [stg])
            for kb in range(cols // 128):
                transpose_to(wpool, dst_off + kb * dst_span + rb * 128, stg,
                             stg.h[:, kb * 128:(kb + 1) * 128], 128)

    wx_stage = sbf[1]
    B.dma("qg", "dwx", lambda q: q.dma_start(out=wx_stage.h[0:33, :], in_=T["W_xproj"]),
          [], [wx_stage])
    for kb in range(16):
        transpose_to(WxT, kb * 33, wx_stage, wx_stage.h[0:33, kb * 128:(kb + 1) * 128], 33)

    wred = B.sbuf("wred", [128, 1], F32)
    wsum = [B.sbuf("wsum%d" % i, [128, 1], F32) for i in range(3)]
    sc_sb = B.sbuf("sc_sb", [1, 2], F32)
    wmats = [(T["w_gate"], FFN, D_MODEL), (T["w_up"], FFN, D_MODEL), (T["w_down"], D_MODEL, FFN)]
    for wi, (wt, nrows, ncols) in enumerate(wmats):
        nseg = max(1, ncols // 2048)
        segc = ncols // nseg
        si = 0
        for rb in range(nrows // 128):
            for sg in range(nseg):
                stg = scr[si % 2]
                si += 1
                B.dma("qs", "dwf%d" % (si % 2), lambda q, rb=rb, sg=sg, wt=wt, segc=segc,
                      ncols=ncols, stg=stg: q.dma_start(
                    out=stg.h[:, 0:segc],
                    in_=bass.AP(wt.tensor, rb * 128 * ncols + sg * segc,
                                [[ncols, 128], [1, segc]])), [], [stg])
                first = (rb == 0 and sg == 0)
                dst = wsum[wi] if first else wred
                vop(lambda e, stg=stg, segc=segc, dst=dst: e.reduce_sum(
                    dst.h[:, 0:1], stg.h[:, 0:segc], axis=AX.X, apply_absolute_value=True),
                    [stg], [dst])
                if not first:
                    vop(lambda e, wi=wi: e.tensor_add(wsum[wi].h[:, 0:1], wsum[wi].h[:, 0:1],
                                                      wred.h[:, 0:1]), [wsum[wi], wred],
                        [wsum[wi]])
        top(lambda e, wi=wi: e.matmul(pb[6].h[0:1, 0:1], onesc.h[:, 0:1], wsum[wi].h[:, 0:1],
                                      start=True, stop=True), [onesc, wsum[wi]], [pb[6]])
        vop(lambda e, n=nrows * ncols: e.tensor_scalar(
            sc_sb.h[0:1, 0:1], pb[6].h[0:1, 0:1], 1.0 / n, 1e-5, A_.mult, A_.max),
            [pb[6]], [sc_sb])
        vop(lambda e: e.tensor_scalar_mul(sc_sb.h[0:1, 1:2], sc_sb.h[0:1, 0:1], 0.5),
            [sc_sb], [sc_sb], accum=True)
        B.dma("qg", "dscr", lambda q: q.dma_start(
            out=screp.h[:, :],
            in_=bass.AP(sc_sb.h, 0, [[1, 1], [0, 128], [1, 2]])), [sc_sb], [screp])
        vop(lambda e, wi=wi: e.tensor_scalar_mul(swneg[wi].h[:, 0:1], screp.h[:, 0:1], -1.0),
            [screp], [swneg[wi]])
        vop(lambda e, wi=wi: e.tensor_copy(thrp[wi].h[:, 0:1], screp.h[:, 1:2]),
            [screp], [thrp[wi]])
        vop(lambda e, wi=wi: e.tensor_scalar_mul(thrn[wi].h[:, 0:1], screp.h[:, 1:2], -1.0),
            [screp], [thrn[wi]])

    xg = [B.sbuf("xg%d" % i, [128, D_MODEL], F32) for i in range(2)]
    mcol = B.sbuf("mcol", [128, 1], F32)
    m2col = B.sbuf("m2col", [128, 1], F32)
    sqcol = B.sbuf("sqcol", [128, 1], F32)
    rstd = B.sbuf("rstd", [128, 1], F32)
    nmr = B.sbuf("nmr", [128, 1], F32)
    xn32 = B.sbuf("xn32", [128, D_MODEL], F32)
    sqbuf = xn32
    xnbf = B.sbuf("xnbf", [128, D_MODEL], BF16)
    xnfm = B.sbuf("xnfm", [128, 8 * TILE], BF16)
    xproj_t = B.sbuf("xproj_t", [128, 16 * (TILE + 3)], BF16)
    zst = B.sbuf("zst", [128, 512], BF16)
    cvf = scr[0]
    xconv_t = B.sbuf("xconv_t", [128, 16 * TILE], BF16)
    xtm = [B.sbuf("xtm0", [128, D_INNER], BF16)] * 2
    xdbl = scr[1]
    dtbc = B.sbuf("dtbc", [16, TILE], F32)
    dtrow = dtbc
    sprow = dtbc
    dtA = B.sbuf("dtA", [16, TILE], F32)
    Bbar = dtA
    Bbarbf = B.sbuf("Bbarbf", [16, TILE], BF16)
    Cbf = B.sbuf("Cbf", [16, TILE], BF16)
    eexp = B.sbuf("eexp", [16, TILE], BF16)
    Bhat = B.sbuf("Bhat", [16, TILE], BF16)
    BhT = B.sbuf("BhT", [128, CPT * 16], BF16)
    dcol = B.sbuf("dcol", [48, 1], F32)
    amx = B.sbuf("amx", [128, 1], F32)
    scol = B.sbuf("scol", [128, 1], F32)
    scol2 = B.sbuf("scol2", [128, 1], F32)
    lad_w, lad_d = scr[0], scr[1]
    xq_tm = sbf[0]
    sgb = B.sbuf("sgb", [128, 512], FP16)
    upb = B.sbuf("upb", [128, 512], FP16)
    th_a = B.sbuf("th_a", [128, 2048], FP16)
    th_b = B.sbuf("th_b", [128, 2048], FP16)

    qi8 = B.sbuf("qi8", [128, 1024], I8)

    def quant_cast(fn, bufs, w, sc, wb, d, qp, c2, fp16_src=False):
        vop(lambda e: e.tensor_scalar(qi8.h[:, 0:w], fn(), sc.h[:, 0:1],
                                      None, A_.mult), bufs + [sc], [qi8])
        vop(lambda e: e.tensor_copy(wb[0].h[:, wb[1]:wb[1] + w],
                                    qi8.h[:, 0:w]), [qi8], [wb[0]])
        if fp16_src:
            vop(lambda e: e.tensor_scalar(d[0].h[:, d[1]:d[1] + w], fn(), sc.h[:, 0:1],
                                          None, A_.mult), bufs + [sc], [d[0]])
            vop(lambda e: e.tensor_sub(d[0].h[:, d[1]:d[1] + w], d[0].h[:, d[1]:d[1] + w],
                                       wb[0].h[:, wb[1]:wb[1] + w]), [d[0], wb[0]], [d[0]])
        else:
            vop(lambda e: e.scalar_tensor_tensor(
                d[0].h[:, d[1]:d[1] + w], fn(), sc.h[:, 0:1],
                wb[0].h[:, wb[1]:wb[1] + w], A_.mult, A_.subtract),
                bufs + [sc, wb[0]], [d[0]])
        vop(lambda e: e.scalar_tensor_tensor(
            qp[0].h[:, qp[1]:qp[1] + w], d[0].h[:, d[1]:d[1] + w], 0.5,
            wb[0].h[:, wb[1]:wb[1] + w], A_.is_ge, A_.add), [d[0], wb[0]], [qp[0]])
        vop(lambda e: e.tensor_scalar(c2[0].h[:, c2[1]:c2[1] + w],
                                      d[0].h[:, d[1]:d[1] + w], -0.5, None, A_.is_le),
            [d[0]], [c2[0]])
        vop(lambda e: e.tensor_sub(wb[0].h[:, wb[1]:wb[1] + w],
                                   qp[0].h[:, qp[1]:qp[1] + w],
                                   c2[0].h[:, c2[1]:c2[1] + w]), [qp[0], c2[0]], [wb[0]],
            accum=True)

    for _nm, _dst in _pvec_loads:
        B.dma("qs", "dsm", lambda q, _nm=_nm: q.dma_start(
            out=dtbc.h[0:16, 0:128],
            in_=bass.AP(T[_nm].tensor, 0, [[128, 16], [1, 128]])), [], [dtbc])
        top(lambda e: e.transpose(pb[6].h[0:128, 0:16], dtbc.h[0:16, 0:128],
                                  identf.h[0:16, 0:16]), [dtbc, identf], [pb[6]])
        vop(lambda e, _dst=_dst: e.tensor_copy(_dst.h[:, 0:16], pb[6].h[0:128, 0:16]),
            [pb[6]], [_dst])

    vop(lambda e: e.memset(
        bass.AP(xproj_t.h, 0, [[16 * (TILE + 3), 128], [TILE + 3, 16], [1, 3]]), 0.0),
        [], [xproj_t])

    hcur = [0]

    def mixer_tiles(prefix):
        ntiles = NT_PRE if prefix else NT_OWN
        x_src = T["x_pre"] if prefix else T["x_own"]
        cbcol = pconvb if prefix else convb
        n_oc = 16 if prefix else 32
        span = D_INNER if prefix else 2 * D_INNER
        def stageA(ti):
            t0 = ti * TILE
            for gp in range(GPT // 2):
                for g in (2 * gp, 2 * gp + 1):
                    xt = xg[g % 2]
                    sc = scol if g % 2 == 0 else scol2
                    B.dma("qs", "dx%d" % (g % 2), lambda q, t0=t0, g=g, xt=xt, x_src=x_src:
                          q.dma_start(out=xt.h[:, :],
                                      in_=bass.AP(x_src.tensor, (t0 + g * 128) * D_MODEL,
                                                  [[D_MODEL, 128], [1, D_MODEL]])), [], [xt])
                    vop(lambda e, xt=xt: e.reduce_sum(mcol.h[:, 0:1], xt.h[:, :], axis=AX.X),
                        [xt], [mcol])
                    sop(lambda e, xt=xt: e.activation(sqbuf.h[:, :], xt.h[:, :], AF.Square),
                        [xt], [sqbuf])
                    vop(lambda e: e.reduce_sum(sqcol.h[:, 0:1], sqbuf.h[:, :], axis=AX.X),
                        [sqbuf], [sqcol])
                    vop(lambda e: e.tensor_scalar_mul(mcol.h[:, 0:1], mcol.h[:, 0:1],
                                                      1.0 / D_MODEL), [mcol], [mcol])
                    vop(lambda e: e.tensor_mul(m2col.h[:, 0:1], mcol.h[:, 0:1],
                                               mcol.h[:, 0:1]), [mcol], [m2col])
                    vop(lambda e: e.scalar_tensor_tensor(sqcol.h[:, 0:1], sqcol.h[:, 0:1],
                                                         1.0 / D_MODEL, m2col.h[:, 0:1],
                                                         A_.mult, A_.subtract),
                        [sqcol, m2col], [sqcol])
                    sop(lambda e: e.activation(rstd.h[:, 0:1], sqcol.h[:, 0:1], AF.Sqrt,
                                               bias=epscol.h[:, 0:1]), [sqcol, epscol],
                        [rstd])
                    vop(lambda e: e.reciprocal(rstd.h[:, 0:1], rstd.h[:, 0:1]),
                        [rstd], [rstd])
                    vop(lambda e: e.scalar_tensor_tensor(nmr.h[:, 0:1], mcol.h[:, 0:1],
                                                         -1.0, rstd.h[:, 0:1],
                                                         A_.mult, A_.mult),
                        [mcol, rstd], [nmr])
                    sop(lambda e, xt=xt: e.activation(xt.h[:, :], xt.h[:, :], AF.Identity,
                                                      bias=nmr.h[:, 0:1],
                                                      scale=rstd.h[:, 0:1]),
                        [xt, nmr, rstd], [xt])
                    vop(lambda e, xt=xt: e.tensor_copy(xnbf.h[:, :], xt.h[:, :]),
                        [xt], [xnbf])
                    for kc in range(8):
                        transpose_to(xnfm, kc * TILE + g * 128, xnbf,
                                     xnbf.h[:, kc * 128:(kc + 1) * 128], 128)
                    if not prefix:
                        gi = (t0 + g * 128) // 128
                        vop(lambda e, xt=xt: e.reduce_max(amx.h[:, 0:1], xt.h[:, :],
                                                          axis=AX.X,
                                                          apply_absolute_value=True),
                            [xt], [amx])
                        vop(lambda e: e.tensor_scalar_max(amx.h[:, 0:1], amx.h[:, 0:1],
                                                          1e-5), [amx], [amx])
                        vop(lambda e, sc=sc: e.reciprocal(sc.h[:, 0:1], amx.h[:, 0:1]),
                            [amx], [sc])
                        vop(lambda e, sc=sc: e.tensor_scalar_mul(sc.h[:, 0:1],
                                                                 sc.h[:, 0:1], 127.0),
                            [sc], [sc])
                        vop(lambda e, gi=gi, sc=sc: e.reciprocal(
                            c1_all.h[:, gi:gi + 1], sc.h[:, 0:1]), [sc],
                            [c1_all], accum=True)
                        quant_cast(lambda xt=xt: xt.h[:, :], [xg[g % 2]], D_MODEL, sc,
                                   (sbf[g % 2], 0), (scr[g % 2], 1024),
                                   (sbf[g % 2], 1024), (xnbf, 0))
                        for kc in range(8):
                            col = kc * TILE + g * 128
                            dst = wb16[0] if col < 2048 else wb16[1]
                            transpose_to(dst, col % 2048, sbf[g % 2],
                                         sbf[g % 2].h[:, kc * 128:(kc + 1) * 128], 128)
            if not prefix:
                B.dma("qg", "dxq1", lambda q, t0=t0: q.dma_start(
                    out=bass.AP(T["xq1_sp"].tensor, t0,
                                [[8 * n_own, 128], [n_own, 4], [1, TILE]]),
                    in_=bass.AP(wb16[0].h, 0, [[2048, 128], [TILE, 4], [1, TILE]])),
                    [wb16[0]], [dq1])
                B.dma("qg", "dxq1", lambda q, t0=t0: q.dma_start(
                    out=bass.AP(T["xq1_sp"].tensor, 4 * n_own + t0,
                                [[8 * n_own, 128], [n_own, 4], [1, TILE]]),
                    in_=bass.AP(wb16[1].h, 0, [[2048, 128], [TILE, 4], [1, TILE]])),
                    [wb16[1]], [dq1])
        def stageB(ti):
            t0 = ti * TILE
            for oc in range(n_oc):
                ps = pb[oc % 2]
                for kc in range(8):
                    top(lambda e, oc=oc, kc=kc, ps=ps, span=span: e.matmul(
                        ps.h[:, :], wpool.h[:, kc * span + oc * 128:kc * span + (oc + 1) * 128],
                        xnfm.h[:, kc * TILE:(kc + 1) * TILE],
                        start=(kc == 0), stop=(kc == 7)), [wpool, xnfm], [ps], accum=(kc > 0))
                if oc < 16:
                    vop(lambda e, oc=oc, ps=ps: e.tensor_copy(
                        xproj_t.h[:, oc * (TILE + 3) + 3:oc * (TILE + 3) + 3 + TILE],
                        ps.h[:, :]), [ps], [xproj_t], accum=True)
                else:
                    vop(lambda e, ps=ps: e.tensor_copy(zst.h[:, :], ps.h[:, :]), [ps], [zst])
                    B.dma("qg", "dzs", lambda q, t0=t0, oc=oc: q.dma_start(
                        out=bass.AP(T["z_sp"].tensor, (oc - 16) * n_own + t0,
                                    [[16 * n_own, 128], [1, TILE]]),
                        in_=zst.h[:, :]), [zst], [dz])
        def stageC(ti):
            t0 = ti * TILE
            for c in range(16):
                base = c * (TILE + 3)
                vop(lambda e, base=base, c=c, cb=cbcol: e.tensor_scalar(
                    cvf.h[:, 0:TILE], xproj_t.h[:, base:base + TILE], convw.h[:, 4 * c:4 * c + 1],
                    cb.h[:, c:c + 1], A_.mult, A_.add), [xproj_t, convw, cbcol], [cvf])
                for k in (1, 2, 3):
                    vop(lambda e, base=base, c=c, k=k: e.scalar_tensor_tensor(
                        cvf.h[:, 0:TILE], xproj_t.h[:, base + k:base + k + TILE],
                        convw.h[:, 4 * c + k:4 * c + k + 1], cvf.h[:, 0:TILE], A_.mult, A_.add),
                        [xproj_t, convw, cvf], [cvf])
                sop(lambda e, c=c: e.activation(xconv_t.h[:, c * TILE:(c + 1) * TILE],
                                                cvf.h[:, 0:TILE], AF.Silu), [cvf], [xconv_t],
                    accum=True)
            vop(lambda e: e.tensor_copy(
                bass.AP(xproj_t.h, 0, [[16 * (TILE + 3), 128], [TILE + 3, 16], [1, 3]]),
                bass.AP(xproj_t.h, TILE, [[16 * (TILE + 3), 128], [TILE + 3, 16], [1, 3]])),
                [xproj_t], [xproj_t], accum=True)
            if not prefix:
                B.dma("qg", "dxcs", lambda q, t0=t0: q.dma_start(
                    out=bass.AP(T["xc_sp"].tensor, t0,
                                [[16 * n_own, 128], [n_own, 16], [1, TILE]]),
                    in_=bass.AP(xconv_t.h, 0, [[16 * TILE, 128], [TILE, 16], [1, TILE]])),
                    [xconv_t], [dxc])
        def stageD(ti):
            t0 = ti * TILE
            for kc in range(16):
                top(lambda e, kc=kc: e.matmul(
                    pb[6].h[0:16, :], WxT.h[:, kc * 33:kc * 33 + 16],
                    xconv_t.h[:, kc * TILE:(kc + 1) * TILE],
                    start=(kc == 0), stop=(kc == 15)), [WxT, xconv_t], [pb[6]],
                    accum=(kc > 0))
            for kc in range(16):
                top(lambda e, kc=kc: e.matmul(
                    pb[0].h[0:16, :], WxT.h[:, kc * 33 + 16:kc * 33 + 32],
                    xconv_t.h[:, kc * TILE:(kc + 1) * TILE],
                    start=(kc == 0), stop=(kc == 15)), [WxT, xconv_t], [pb[0]],
                    accum=(kc > 0))
            for kc in range(16):
                top(lambda e, kc=kc: e.matmul(
                    pb[1].h[0:1, :], WxT.h[:, kc * 33 + 32:kc * 33 + 33],
                    xconv_t.h[:, kc * TILE:(kc + 1) * TILE],
                    start=(kc == 0), stop=(kc == 15)), [WxT, xconv_t], [pb[1]],
                    accum=(kc > 0))
            vop(lambda e: e.tensor_copy(xdbl.h[0:16, 0:TILE], pb[6].h[0:16, :]), [pb[6]], [xdbl])
            vop(lambda e: e.tensor_copy(Cbf.h[:, :], pb[0].h[0:16, :]), [pb[0]], [Cbf])
            sop(lambda e: e.activation(cvf.h[0:1, 0:TILE], pb[1].h[0:1, :], AF.Abs),
                [pb[1]], [cvf])
            sop(lambda e: e.activation(cvf.h[0:1, 0:TILE], cvf.h[0:1, 0:TILE], AF.Exp, scale=-1.0),
                [cvf], [cvf], accum=True)
            sop(lambda e: e.activation(cvf.h[0:1, 0:TILE], cvf.h[0:1, 0:TILE], AF.Ln, bias=1.0),
                [cvf], [cvf], accum=True)
            vop(lambda e: e.scalar_tensor_tensor(dtbc.h[0:1, :], pb[1].h[0:1, :], 0.0,
                                                 cvf.h[0:1, 0:TILE], A_.max, A_.add),
                [pb[1], cvf], [dtbc], accum=True)
            vop(lambda e: e.tensor_scalar(dtbc.h[0:1, :], dtbc.h[0:1, :], 0.1, 0.001,
                                          A_.min, A_.max), [dtbc], [dtbc], accum=True)
            top(lambda e: e.matmul(pb[1].h[0:16, :], ones1x16.h[:, :], dtbc.h[0:1, :],
                                   start=True, stop=True), [ones1x16, dtbc], [pb[1]])
            vop(lambda e: e.tensor_copy(dtbc.h[:, :], pb[1].h[0:16, :]), [pb[1]], [dtbc])
            vop(lambda e: e.tensor_scalar(dtA.h[:, :], dtbc.h[:, :], Acol.h[:, 0:1], 0.0,
                                          A_.mult, A_.min), [dtbc, Acol], [dtA])
            vop(lambda e: e.tensor_scalar_max(dtA.h[:, :], dtA.h[:, :], -20.0), [dtA], [dtA])
            vop(lambda e: e.tensor_copy(cumc.h[:, 0:1], carry.h[:, :]), [carry], [cumc])
            vop(lambda e: e.tensor_tensor_scan(
                cumc.h[:, 1:1 + TILE], dtA.h[:, :],
                bass.AP(zcol16.h, 0, [[1, 16], [0, TILE]]), cumc.h[:, 0:1],
                A_.add, A_.add), [dtA, zcol16, cumc], [cumc], accum=True)
            vop(lambda e: e.tensor_copy(carry.h[:, :], cumc.h[:, TILE:TILE + 1]),
                [cumc], [carry])
            vop(lambda e: e.tensor_mul(Bbar.h[:, :], dtbc.h[:, :], xdbl.h[0:16, 0:TILE]),
                [dtbc, xdbl], [Bbar])
            vop(lambda e: e.tensor_scalar(Bbar.h[:, :], Bbar.h[:, :], 10.0, -10.0,
                                          A_.min, A_.max), [Bbar], [Bbar])
            vop(lambda e: e.tensor_sub(
                Bhat.h[:, :].rearrange("p (c q) -> p c q", q=CHUNK),
                cumc.h[:, 1:1 + TILE].rearrange("p (c q) -> p c q", q=CHUNK),
                bass.AP(cumc.h, CHUNK, [[1 + TILE, 16], [CHUNK, CPT], [0, CHUNK]])),
                [cumc], [Bhat])
            sop(lambda e: e.activation(eexp.h[:, :], Bhat.h[:, :], AF.Exp, scale=-1.0),
                [Bhat], [eexp])
            vop(lambda e: e.tensor_mul(Bhat.h[:, :], eexp.h[:, :], Bbar.h[:, :]),
                [eexp, Bbar], [Bhat])
            if not prefix:
                vop(lambda e: e.tensor_copy(Bbarbf.h[:, :], Bbar.h[:, :]), [Bbar], [Bbarbf])
                vop(lambda e: e.tensor_sub(
                    eexp.h[:, :].rearrange("p (c q) -> p c q", q=CHUNK),
                    cumc.h[:, 1:1 + TILE].rearrange("p (c q) -> p c q", q=CHUNK),
                    bass.AP(cumc.h, 0, [[1 + TILE, 16], [CHUNK, CPT], [0, CHUNK]])),
                    [cumc], [eexp])
                sop(lambda e: e.activation(eexp.h[:, :], eexp.h[:, :], AF.Exp), [eexp], [eexp])
                vop(lambda e, t0=t0: e.tensor_mul(Chp_all.h[:, t0:t0 + TILE], eexp.h[:, :],
                                                  Cbf.h[:, :]), [eexp, Cbf],
                    [Chp_all], accum=True)
            for c in range(CPT):
                xt_ = xtm[c % 2]
                for dc in range(16):
                    transpose_to(xt_, dc * 128, xconv_t,
                                 xconv_t.h[:, dc * TILE + c * CHUNK:dc * TILE + (c + 1) * CHUNK],
                                 128)
                gk = (t0 + c * CHUNK) // CHUNK
                transpose_to(BhT, c * 16, Bhat, Bhat.h[:, c * CHUNK:(c + 1) * CHUNK], 16)
                if not prefix:
                    transpose_to(BbT_own, gk * 16, Bbarbf,
                                 Bbarbf.h[:, c * CHUNK:(c + 1) * CHUNK], 16)
                    transpose_to(CT_own, gk * 16, Cbf,
                                 Cbf.h[:, c * CHUNK:(c + 1) * CHUNK], 16)
                    top(lambda e, c=c: e.transpose(
                        pb[6].h[0:128, 0:16], cumc.h[:, 1 + c * CHUNK:1 + (c + 1) * CHUNK],
                        identf.h[0:16, 0:16]), [cumc, identf], [pb[6]])
                    vop(lambda e, gk=gk: e.tensor_copy(
                        cumT_own.h[:, gk * 16:(gk + 1) * 16], pb[6].h[0:128, 0:16]),
                        [pb[6]], [cumT_own], accum=True)
                vop(lambda e, c=c: e.tensor_scalar(
                    dcol.h[0:16, 0:1], cumc.h[:, (c + 1) * CHUNK:(c + 1) * CHUNK + 1],
                    cumc.h[:, c * CHUNK:c * CHUNK + 1], None, A_.subtract), [cumc], [dcol])
                sop(lambda e: e.activation(dcol.h[0:16, 0:1], dcol.h[0:16, 0:1], AF.Exp),
                    [dcol], [dcol])
                vop(lambda e: e.tensor_copy(dcol.h[32:48, 0:1], dcol.h[0:16, 0:1]),
                    [dcol], [dcol], accum=True)
                cu, nx = hcur[0], 1 - hcur[0]
                if not prefix:
                    B.dma("qg", "dhs", lambda q, gk=gk, cu=cu: q.dma_start(
                        out=T["h_sp"][:, gk * D_INNER:(gk + 1) * D_INNER],
                        in_=Hrow.h[HROW[cu]:HROW[cu] + 16, :]), [Hrow], [dh])
                HBANKS = (2, 3, 5, 0)
                for j in range(4):
                    bk = HBANKS[j]
                    top(lambda e, c=c, j=j, xt_=xt_, bk=bk: e.matmul(
                        pb[bk].h[0:16, :], BhT.h[:, c * 16:(c + 1) * 16],
                        xt_.h[:, j * 512:(j + 1) * 512], start=True, stop=True),
                        [BhT, xt_], [pb[bk]])
                for j in range(4):
                    bk = HBANKS[j]
                    vop(lambda e, j=j, cu=cu, nx=nx, bk=bk: e.scalar_tensor_tensor(
                        Hrow.h[HROW[nx]:HROW[nx] + 16, j * 512:(j + 1) * 512],
                        Hrow.h[HROW[cu]:HROW[cu] + 16, j * 512:(j + 1) * 512],
                        dcol.h[HROW[cu]:HROW[cu] + 16, 0:1], pb[bk].h[0:16, :],
                        A_.mult, A_.add),
                        [Hrow, dcol, pb[bk]], [Hrow], accum=True)
                hcur[0] = nx


        stageA(0)
        stageB(0)
        for ti in range(ntiles):
            if ti + 1 < ntiles:
                stageA(ti + 1)
            stageC(ti)
            if ti + 1 < ntiles:
                stageB(ti + 1)
            stageD(ti)
    if NT_PRE > 0:
        load_transpose_bf16(T["pre_Win"], D_INNER, D_MODEL, 0, D_INNER, "dwl")
        mixer_tiles(prefix=True)
    load_transpose_bf16(T["W_in"], 2 * D_INNER, D_MODEL, 0, 2 * D_INNER, "dwl")
    mixer_tiles(prefix=False)

    xq1g = xnbf
    k1g = sqcol
    k1u = rstd

    xq1gp = [Buf(xnfm.h, "xq1ga"), Buf(xnfm.h, "xq1gb")]

    def ffn_half(hf):
        wqst = sbf[1]
        for wi, wt in ((0, T["w_gate"]), (1, T["w_up"])):
            for rb in range(16):
                row0 = hf * 2048 + rb * 128
                stg = scr[0] if rb % 2 == 0 else scr[2]
                B.dma("qs", "dwf%d" % (rb % 2), lambda q, row0=row0, wt=wt, stg=stg:
                      q.dma_start(out=stg.h[:, 0:D_MODEL],
                                  in_=bass.AP(wt.tensor, row0 * D_MODEL,
                                              [[D_MODEL, 128], [1, D_MODEL]])), [], [stg])
                vop(lambda e, stg=stg, wi=wi: e.tensor_scalar(
                    lad_d.h[:, 0:D_MODEL], stg.h[:, 0:D_MODEL], thrp[wi].h[:, 0:1], None,
                    A_.is_ge), [stg, thrp[wi]], [lad_d])
                vop(lambda e, stg=stg, wi=wi: e.scalar_tensor_tensor(
                    wqst.h[:, 0:D_MODEL], stg.h[:, 0:D_MODEL], thrn[wi].h[:, 0:1],
                    lad_d.h[:, 0:D_MODEL], A_.is_le, A_.subtract),
                    [stg, thrn[wi], lad_d], [wqst])
                for kb in range(8):
                    transpose_to(wpool, wi * 16384 + kb * 2048 + rb * 128, wqst,
                                 wqst.h[:, kb * 128:(kb + 1) * 128], 128)
        def xq1_load(g):
            par = g % 2
            extra = [xnfm] if (hf == 0 and g == 0) else []
            B.dma("qs", "dq1r%d" % par, lambda q, g=g, par=par: q.dma_start(
                out=bass.AP(xnfm.h, par * 1024, [[4096, 128], [128, 8], [1, 128]]),
                in_=bass.AP(T["xq1_sp"].tensor, g * 128,
                            [[8 * n_own, 128], [n_own, 8], [1, 128]])),
                [dq1], [xq1gp[par]] + extra)

        xq1_load(0)
        pend = [None]
        for g in range(NG):
            if g + 1 < NG:
                xq1_load(g + 1)
            xq1g_ap = lambda kc, g=g: bass.AP(
                xnfm.h, (g % 2) * 1024 + kc * 128, [[4096, 128], [1, 128]])
            xq1g_buf = xq1gp[g % 2]
            vop(lambda e, g=g: e.tensor_mul(k1g.h[:, 0:1], swneg[0].h[:, 0:1],
                                            c1_all.h[:, g:g + 1]), [swneg[0], c1_all], [k1g])
            vop(lambda e, g=g: e.tensor_mul(k1u.h[:, 0:1], swneg[1].h[:, 0:1],
                                            c1_all.h[:, g:g + 1]), [swneg[1], c1_all], [k1u])
            for nh in range(4):
                gbk = pb[0] if nh % 2 == 0 else pb[2]
                ubk = pb[1] if nh % 2 == 0 else pb[3]
                sgp = (sgb, upb) if nh % 2 == 0 else (upb, sgb)
                for kc in range(8):
                    top(lambda e, nh=nh, kc=kc, a=xq1g_ap, gbk=gbk: e.matmul(
                        gbk.h[:, :], a(kc),
                        wpool.h[:, kc * 2048 + nh * 512:kc * 2048 + (nh + 1) * 512],
                        start=(kc == 0), stop=(kc == 7)), [xq1g_buf, wpool], [gbk],
                        accum=(kc > 0))
                for kc in range(8):
                    top(lambda e, nh=nh, kc=kc, a=xq1g_ap, ubk=ubk: e.matmul(
                        ubk.h[:, :], a(kc),
                        wpool.h[:, 16384 + kc * 2048 + nh * 512:
                                16384 + kc * 2048 + (nh + 1) * 512],
                        start=(kc == 0), stop=(kc == 7)), [xq1g_buf, wpool], [ubk],
                        accum=(kc > 0))
                sop(lambda e, gbk=gbk, s0=sgp[0]: e.activation(
                    s0.h[:, :], gbk.h[:, :], AF.Silu,
                    scale=k1g.h[:, 0:1]), [gbk, k1g], [sgp[0]])
                sop(lambda e, ubk=ubk, s1=sgp[1]: e.activation(
                    s1.h[:, :], ubk.h[:, :], AF.Identity,
                    scale=k1u.h[:, 0:1]), [ubk, k1u], [sgp[1]])
                vop(lambda e, nh=nh, sgp=sgp: e.tensor_mul(
                    th_a.h[:, nh * 512:(nh + 1) * 512],
                    sgp[0].h[:, :], sgp[1].h[:, :]), [sgp[0], sgp[1]],
                    [th_a], accum=True)
            if hf == 0:
                B.dma("qg", "dt1s", lambda q, g=g: q.dma_start(
                    out=bass.AP(T["t1_sp"].tensor, g * 128,
                                [[16 * n_own, 128], [n_own, 16], [1, 128]]),
                    in_=bass.AP(th_a.h, 0, [[2048, 128], [128, 16], [1, 128]])),
                    [th_a], [dt1])
            else:
                B.dma("qs", "dt1r", lambda q, g=g: q.dma_start(
                    out=bass.AP(th_b.h, 0, [[2048, 128], [128, 16], [1, 128]]),
                    in_=bass.AP(T["t1_sp"].tensor, g * 128,
                                [[16 * n_own, 128], [n_own, 16], [1, 128]])), [dt1], [th_b])
                vop(lambda e: e.reduce_max(amx.h[:, 0:1], th_a.h[:, :], axis=AX.X,
                                           apply_absolute_value=True), [th_a], [amx])
                vop(lambda e: e.reduce_max(m2col.h[:, 0:1], th_b.h[:, :], axis=AX.X,
                                           apply_absolute_value=True), [th_b], [m2col])
                vop(lambda e: e.tensor_max(amx.h[:, 0:1], amx.h[:, 0:1], m2col.h[:, 0:1]),
                    [amx, m2col], [amx])
                vop(lambda e: e.tensor_scalar_max(amx.h[:, 0:1], amx.h[:, 0:1], 1e-5),
                    [amx], [amx])
                vop(lambda e: e.reciprocal(scol.h[:, 0:1], amx.h[:, 0:1]), [amx], [scol])
                vop(lambda e: e.tensor_scalar_mul(scol.h[:, 0:1], scol.h[:, 0:1], 127.0),
                    [scol], [scol])
                vop(lambda e, g=g: e.reciprocal(c2_all.h[:, g:g + 1], scol.h[:, 0:1]),
                    [scol], [c2_all], accum=True)
                if pend[0] is not None:
                    pend[0]()
                for half2, tb in ((0, th_b), (1, th_a)):
                    for co in (0, 1024):
                        quant_cast(lambda tb=tb, co=co: tb.h[:, co:co + 1024], [tb],
                                   1024, scol, (sbf[half2], co), (scr[half2], co),
                                   (wb16[0], co), (wb16[1], co), fp16_src=True)

                def mk_pend(g=g):
                    def flush():
                        for half2 in (0, 1):
                            for kc in range(16):
                                transpose_to(wb16[2], kc * 128, sbf[half2],
                                             sbf[half2].h[:, kc * 128:(kc + 1) * 128],
                                             128)
                            B.dma("qg", "dxq2", lambda q, half2=half2, g=g: q.dma_start(
                                out=bass.AP(T["xq2_sp"].tensor,
                                            half2 * 16 * n_own + g * 128,
                                            [[32 * n_own, 128], [n_own, 16], [1, 128]]),
                                in_=bass.AP(wb16[2].h, 0, [[2048, 128], [128, 16],
                                                           [1, 128]])),
                                [wb16[2]], [dq2])
                    return flush
                pend[0] = mk_pend()

        if hf == 1 and pend[0] is not None:
            pend[0]()

    ffn_half(0)
    ffn_half(1)

    load_transpose_bf16(T["W_out"], D_MODEL, D_INNER, 0, D_MODEL, "dwl")

    wpar = Buf(wpool.h, "wpar")
    XCK = [(xnfm, 0), (wpar, 16384)]
    ZK = [(xnfm, 2048), (wpar, 18432)]
    HK = [(xconv_t, 0), (wpar, 20480)]
    SSL = [(xconv_t, 2048), (wpar, 22528)]
    ESC = [(wb16[0], 0), (wpar, 22656)]
    CBC = [(wb16[1], 0), (wpar, 24704)]
    YS = [(wb16[2], 0), (wpar, 26752)]
    XTMK = [(xtm[0], 0), (wpar, 28800)]
    XROW = [xg[0], xg[1]]
    DARG = [scr[0], scr[2]]
    cumbc = scr[1]
    xat = xn32

    def p4_xrow(k):
        par = k % 2
        B.dma("qs", "dxr%d" % par, lambda q, k=k, par=par: q.dma_start(
            out=XROW[par].h[:, :],
            in_=bass.AP(T["x_own"].tensor, k * 128 * D_MODEL,
                        [[D_MODEL, 128], [1, D_MODEL]])), [], [XROW[par]])

    def p4_loads(k):
        par = k % 2
        xb, xo = XCK[par]
        zb, zo = ZK[par]
        hb, ho = HK[par]
        db_ = DARG[par]
        extra = [wpool] if k == 1 else (xq1gp if k == 0 else [])
        B.dma("qs", "dxck%d" % par, lambda q, k=k, xb=xb, xo=xo: q.dma_start(
            out=bass.AP(xb.h, xo, [[xb.h.shape[1], 128], [128, 16], [1, 128]]),
            in_=bass.AP(T["xc_sp"].tensor, k * 128,
                        [[16 * n_own, 128], [n_own, 16], [1, 128]])), [dxc], [xb] + extra)
        B.dma("qs", "dzk%d" % par, lambda q, k=k, zb=zb, zo=zo: q.dma_start(
            out=bass.AP(zb.h, zo, [[zb.h.shape[1], 128], [128, 16], [1, 128]]),
            in_=bass.AP(T["z_sp"].tensor, k * 128,
                        [[16 * n_own, 128], [n_own, 16], [1, 128]])), [dz], [zb])
        B.dma("qs", "dhk%d" % par, lambda q, k=k, hb=hb, ho=ho: q.dma_start(
            out=bass.AP(hb.h, ho, [[hb.h.shape[1], 16], [1, D_INNER]]),
            in_=T["h_sp"][:, k * D_INNER:(k + 1) * D_INNER]), [dh], [hb])
        B.dma("qg", "dcf%d" % par, lambda q, k=k, db_=db_: q.dma_start(
            out=db_.h[0:1, 0:2048],
            in_=bass.AP(cumT_own.h, k * 16, [[NCH * 16, 128], [1, 16]])),
            [cumT_own], [db_])
        B.dma("qg", "dcb%d" % par, lambda q, db_=db_: q.dma_start(
            out=cumbc.h[:, :],
            in_=bass.AP(db_.h, 0, [[1, 1], [0, 128], [1, 2048]])),
            [db_], [cumbc])
        B.dma("qg", "dclf%d" % par, lambda q, k=k: q.dma_start(
            out=sbf[0].h[0:1, 0:2048],
            in_=bass.AP(CT_own.h, k * 16, [[NCH * 16, 128], [1, 16]])),
            [CT_own], [sbf[0]])
        cb, co = CBC[par]
        B.dma("qg", "dclb%d" % par, lambda q, cb=cb, co=co: q.dma_start(
            out=bass.AP(cb.h, co, [[cb.h.shape[1], 128], [1, 2048]]),
            in_=bass.AP(sbf[0].h, 0, [[1, 1], [0, 128], [1, 2048]])), [sbf[0]], [cb])

    def p4_scores(k):
        par = k % 2
        db_ = DARG[par]
        eb, eo = ESC[par]
        cb, co = CBC[par]
        sb, so = SSL[par]
        vop(lambda e, k=k, db_=db_: e.tensor_sub(
            db_.h[:, :].rearrange("p (j s) -> p j s", s=16),
            cumbc.h[:, :].rearrange("p (j s) -> p j s", s=16),
            cumT_own.h[:, k * 16:(k + 1) * 16][:, None, :].broadcast_to([128, 128, 16])),
            [cumbc, cumT_own], [db_], accum=True)
        vop(lambda e, db_=db_: e.tensor_sub(
            db_.h[:, :].rearrange("p (j s) -> p j s", s=16),
            db_.h[:, :].rearrange("p (j s) -> p j s", s=16),
            maskb.h[:, :, None].broadcast_to([128, 128, 16])), [db_, maskb], [db_],
            accum=True)
        sop(lambda e, db_=db_, eb=eb, eo=eo: e.activation(
            eb.h[:, eo:eo + 2048], db_.h[:, :], AF.Exp), [db_], [eb])
        vop(lambda e, k=k, eb=eb, eo=eo: e.tensor_mul(
            bass.AP(eb.h, eo, [[eb.h.shape[1], 128], [16, 128], [1, 16]]),
            bass.AP(eb.h, eo, [[eb.h.shape[1], 128], [16, 128], [1, 16]]),
            BbT_own.h[:, k * 16:(k + 1) * 16][:, None, :].broadcast_to([128, 128, 16])),
            [eb, BbT_own], [eb], accum=True)
        vop(lambda e, eb=eb, eo=eo, cb=cb, co=co: e.tensor_mul(
            eb.h[:, eo:eo + 2048], eb.h[:, eo:eo + 2048], cb.h[:, co:co + 2048]),
            [eb, cb], [eb], accum=True)

        def _score_reduce(e, eb=eb, eo=eo, sb=sb, so=so):
            with B.nc.allow_low_precision(reason="bf16 score tile for bf16 matmul"):
                return e.reduce_sum(
                    sb.h[0:128, so:so + 128],
                    bass.AP(eb.h, eo, [[eb.h.shape[1], 128], [16, 128], [1, 16]]),
                    axis=AX.X)
        vop(_score_reduce, [eb], [sb], accum=True)

    def p4_xtm(k):
        par = k % 2
        xb, xo = XCK[par]
        tb, to = XTMK[par]
        for dc in range(16):
            transpose_to(tb, to + dc * 128, xb,
                         bass.AP(xb.h, xo + dc * 128, [[xb.h.shape[1], 128], [1, 128]]),
                         128)

    def p4_y(k):
        par = k % 2
        tb, to = XTMK[par]
        hb, ho = HK[par]
        sb, so = SSL[par]
        yb, yo = YS[par]
        for dc in range(16):
            bank = 2 + dc % 2
            top(lambda e, dc=dc, bank=bank, tb=tb, to=to, sb=sb, so=so: e.matmul(
                pb[bank].h[:, 0:128],
                bass.AP(tb.h, to + dc * 128, [[tb.h.shape[1], 128], [1, 128]]),
                sb.h[0:128, so:so + 128], start=True, stop=False),
                [tb, sb], [pb[bank]])
            top(lambda e, dc=dc, bank=bank, k=k, hb=hb, ho=ho: e.matmul(
                pb[bank].h[:, 0:128],
                bass.AP(hb.h, ho + dc * 128, [[hb.h.shape[1], 16], [1, 128]]),
                Chp_all.h[:, k * 128:(k + 1) * 128], start=False, stop=True),
                [hb, Chp_all], [pb[bank]], accum=True)
            vop(lambda e, dc=dc, bank=bank, yb=yb, yo=yo: e.tensor_copy(
                yb.h[:, yo + dc * 128:yo + (dc + 1) * 128], pb[bank].h[:, 0:128]),
                [pb[bank]], [yb], accum=True)

    def p4_ypost(k):
        par = k % 2
        xb, xo = XCK[par]
        zb, zo = ZK[par]
        eb, eo = ESC[par]
        cb, co = CBC[par]
        yb, yo = YS[par]
        sop(lambda e, zb=zb, zo=zo, cb=cb, co=co: e.activation(
            cb.h[:, co:co + 2048], zb.h[:, zo:zo + 2048], AF.Silu), [zb], [cb],
            accum=True)
        vop(lambda e, yb=yb, yo=yo, cb=cb, co=co: e.tensor_mul(
            yb.h[:, yo:yo + 2048], yb.h[:, yo:yo + 2048], cb.h[:, co:co + 2048]),
            [yb, cb], [yb], accum=True)
        vop(lambda e, xb=xb, xo=xo, eb=eb, eo=eo: e.tensor_mul(
            bass.AP(eb.h, eo, [[eb.h.shape[1], 128], [128, 16], [1, 128]]),
            bass.AP(xb.h, xo, [[xb.h.shape[1], 128], [128, 16], [1, 128]]),
            Dp.h[:, :, None].broadcast_to([128, 16, 128])), [xb, Dp], [eb], accum=True)
        vop(lambda e, yb=yb, yo=yo, eb=eb, eo=eo: e.tensor_add(
            yb.h[:, yo:yo + 2048], yb.h[:, yo:yo + 2048], eb.h[:, eo:eo + 2048]),
            [yb, eb], [yb], accum=True)

    def p4_attn(k):
        par = k % 2
        yb, yo = YS[par]
        for nh in range(2):
            for dc in range(16):
                top(lambda e, nh=nh, dc=dc, yb=yb, yo=yo: e.matmul(
                    pb[nh].h[:, :],
                    bass.AP(yb.h, yo + dc * 128, [[yb.h.shape[1], 128], [1, 128]]),
                    wpool.h[:, dc * D_MODEL + nh * 512:dc * D_MODEL + (nh + 1) * 512],
                    start=(dc == 0), stop=(dc == 15)), [yb, wpool], [pb[nh]],
                    accum=(dc != 0))
        for nh in range(2):
            vop(lambda e, nh=nh, par=par: e.tensor_add(
                xat.h[:, nh * 512:(nh + 1) * 512],
                XROW[par].h[:, nh * 512:(nh + 1) * 512],
                pb[nh].h[:, :]), [XROW[par], pb[nh]], [xat], accum=True)
        B.dma("qg", "dxats", lambda q, k=k: q.dma_start(
            out=bass.AP(T["xat_sp"].tensor, k * 128,
                        [[8 * n_own, 128], [n_own, 8], [1, 128]]),
            in_=bass.AP(xat.h, 0, [[1024, 128], [128, 8], [1, 128]])), [xat], [dxat])

    p4_loads(0)
    p4_xrow(0)
    p4_scores(0)
    p4_xtm(0)
    if NCH > 1:
        p4_loads(1)
        p4_xrow(1)
    for k in range(NCH):
        p4_y(k)
        p4_ypost(k)
        if k + 1 < NCH:
            p4_xtm(k + 1)
        p4_attn(k)
        if k + 1 < NCH:
            p4_scores(k + 1)
        if k + 2 < NCH:
            p4_loads(k + 2)
            p4_xrow(k + 2)

    wqst = sbf[1]
    for rb in range(D_MODEL // 128):
        for sg in range(2):
            stg = scr[0] if sg == 0 else scr[2]
            extra = [wpar] if rb == 0 else []
            B.dma("qs", "dwf%d" % sg, lambda q, rb=rb, sg=sg, stg=stg: q.dma_start(
                out=stg.h[:, 0:2048],
                in_=bass.AP(T["w_down"].tensor, rb * 128 * FFN + sg * 2048,
                            [[FFN, 128], [1, 2048]])), [], [stg] + extra)
            vop(lambda e, stg=stg: e.tensor_scalar(
                lad_d.h[:, 0:2048], stg.h[:, 0:2048], thrp[2].h[:, 0:1], None, A_.is_ge),
                [stg, thrp[2]], [lad_d])
            vop(lambda e, stg=stg: e.scalar_tensor_tensor(
                wqst.h[:, 0:2048], stg.h[:, 0:2048], thrn[2].h[:, 0:1],
                lad_d.h[:, 0:2048], A_.is_le, A_.subtract), [stg, thrn[2], lad_d], [wqst])
            for kb in range(16):
                transpose_to(wpool, (sg * 16 + kb) * D_MODEL + rb * 128, wqst,
                             wqst.h[:, kb * 128:(kb + 1) * 128], 128)

    xatr = [xg[0]] * 2
    outb = [xn32] * 2
    k2c = nmr

    for k in range(NCH):
        par = k % 2
        B.dma("qs", "dq2r%d" % par, lambda q, k=k, par=par: q.dma_start(
            out=bass.AP(wb16[0].h, 0, [[2048, 128], [128, 16], [1, 128]]),
            in_=bass.AP(T["xq2_sp"].tensor, k * 128,
                        [[32 * n_own, 128], [n_own, 16], [1, 128]])), [dq2], [wb16[0]])
        B.dma("qs", "dq2r%d" % par, lambda q, k=k, par=par: q.dma_start(
            out=bass.AP(wb16[1].h, 0, [[2048, 128], [128, 16], [1, 128]]),
            in_=bass.AP(T["xq2_sp"].tensor, 16 * n_own + k * 128,
                        [[32 * n_own, 128], [n_own, 16], [1, 128]])), [dq2], [wb16[1]])
        B.dma("qs", "dxatr%d" % par, lambda q, k=k, par=par: q.dma_start(
            out=bass.AP(xatr[par].h, 0, [[1024, 128], [128, 8], [1, 128]]),
            in_=bass.AP(T["xat_sp"].tensor, k * 128,
                        [[8 * n_own, 128], [n_own, 8], [1, 128]])), [dxat], [xatr[par]])
        for nh in range(2):
            dbk = pb[nh] if par == 0 else pb[2 + nh]
            for kc in range(32):
                src = wb16[kc // 16]
                top(lambda e, nh=nh, kc=kc, src=src, dbk=dbk: e.matmul(
                    dbk.h[:, :], src.h[:, (kc % 16) * 128:((kc % 16) + 1) * 128],
                    wpool.h[:, kc * D_MODEL + nh * 512:kc * D_MODEL + (nh + 1) * 512],
                    start=(kc == 0), stop=(kc == 31)), [src, wpool], [dbk],
                    accum=(kc != 0))
        vop(lambda e, k=k: e.tensor_mul(k2c.h[:, 0:1], swneg[2].h[:, 0:1],
                                        c2_all.h[:, k:k + 1]), [swneg[2], c2_all], [k2c])
        ob = outb[par]
        for nh in range(2):
            dbk = pb[nh] if par == 0 else pb[2 + nh]
            vop(lambda e, nh=nh, ob=ob, par=par, dbk=dbk: e.scalar_tensor_tensor(
                ob.h[:, nh * 512:(nh + 1) * 512], dbk.h[:, :], k2c.h[:, 0:1],
                xatr[par].h[:, nh * 512:(nh + 1) * 512], A_.mult, A_.add),
                [dbk, k2c, xatr[par]], [ob], accum=True)
        B.dma("qg", "dout", lambda q, k=k, ob=ob: q.dma_start(
            out=bass.AP(T["out"].tensor, k * 128 * D_MODEL, [[D_MODEL, 128], [1, D_MODEL]]),
            in_=ob.h[:, :]), [ob], [])

    B.wait_stream("qs", "dout")


_NC_CACHE = {}


def _get_nc(n_own, n_pre):
    key = (n_own, n_pre)
    if key not in _NC_CACHE:
        _NC_CACHE[key] = build_core_program(n_own, n_pre)
    return _NC_CACHE[key]


def kernel(L0_attn_x, L0_attn_W_in, L0_attn_conv_w, L0_attn_conv_b, L0_attn_W_xproj,
           L0_attn_A_log, L0_attn_D_param, L0_attn_W_out, L0_ffn_w_gate, L0_ffn_w_up,
           L0_ffn_w_down):
    x = np.ascontiguousarray(L0_attn_x, dtype=np.float32)
    b_, s_, d_ = x.shape
    half = s_ // 2
    nc = _get_nc(half, half)
    shared = {
        "W_in": np.ascontiguousarray(L0_attn_W_in, np.float32),
        "conv_w": np.ascontiguousarray(L0_attn_conv_w, np.float32),
        "conv_b": np.ascontiguousarray(L0_attn_conv_b, np.float32),
        "W_xproj": np.ascontiguousarray(L0_attn_W_xproj, np.float32),
        "A_log": np.ascontiguousarray(L0_attn_A_log, np.float32),
        "D_param": np.ascontiguousarray(L0_attn_D_param, np.float32),
        "W_out": np.ascontiguousarray(L0_attn_W_out, np.float32),
        "w_gate": np.ascontiguousarray(L0_ffn_w_gate, np.float32),
        "w_up": np.ascontiguousarray(L0_ffn_w_up, np.float32),
        "w_down": np.ascontiguousarray(L0_ffn_w_down, np.float32),
        "ident": np.eye(128, dtype=bfdt),
        "identf": np.eye(16, dtype=np.float32),
        "maskbig": np.ascontiguousarray(
            (np.triu(np.ones((128, 128), np.float32), 1).T * MASK_BIG).astype(bfdt)),
    }
    zwin = np.zeros((D_INNER, D_MODEL), np.float32)
    zcb = np.zeros((D_INNER,), np.float32)
    zx = np.zeros((half, D_MODEL), np.float32)
    in_maps = []
    for c in range(8):
        b = c // 2
        h = c % 2
        m = dict(shared)
        m["x_own"] = np.ascontiguousarray(x[b, h * half:(h + 1) * half])
        if h == 1:
            m["x_pre"] = np.ascontiguousarray(x[b, 0:half])
            m["pre_Win"] = shared["W_in"][0:D_INNER].copy()
            m["pre_conv_b"] = shared["conv_b"].copy()
        else:
            m["x_pre"] = zx
            m["pre_Win"] = zwin
            m["pre_conv_b"] = zcb
        in_maps.append(m)
    res = run_bass_kernel_spmd(nc, in_maps, core_ids=list(range(8)))
    out = np.empty((b_, s_, d_), np.float32)
    for c in range(8):
        b = c // 2
        h = c % 2
        out[b, h * half:(h + 1) * half] = np.asarray(res.results[c]["out"]).reshape(half, d_)
    return out


# version 17
# speedup vs baseline: 1.0601x; 1.0117x over previous
import contextlib
import numpy as np
import ml_dtypes
import concourse.bass as bass
import concourse.mybir as mybir
from concourse.bass_utils import run_bass_kernel_spmd

F32 = mybir.dt.float32
BF16 = mybir.dt.bfloat16
FP16 = mybir.dt.float16
I8 = mybir.dt.int8
A_ = mybir.AluOpType
AF = mybir.ActivationFunctionType
AX = mybir.AxisListType
bfdt = ml_dtypes.bfloat16

D_MODEL = 1024
D_INNER = 2048
D_STATE = 16
FFN = 4096
TILE = 512
CHUNK = 128
MASK_BIG = 32768.0
LVLS = [128.0, 64.0, 32.0, 16.0, 8.0, 4.0, 2.0, 1.0]


class Buf:
    __slots__ = ("h", "writers", "readers", "name")

    def __init__(self, h, name):
        self.h = h
        self.name = name
        self.writers = {}
        self.readers = {}


class Builder:
    ENGINES = ("te", "ve", "se", "qs", "qg")

    def __init__(self, nc, stack):
        self.nc = nc
        self.stack = stack
        self.items = {e: [] for e in self.ENGINES}
        self.cnt = {e: 0 for e in self.ENGINES}
        self.sems = {e: stack.enter_context(nc.semaphore("sem_" + e)) for e in self.ENGINES}
        self.extra = {}
        self.waited = {e: {} for e in self.ENGINES}

    def sbuf(self, name, shape, dtype):
        return Buf(self.stack.enter_context(self.nc.sbuf_tensor(name, shape, dtype)), name)

    def psum(self, name, shape, dtype=F32):
        return Buf(self.stack.enter_context(self.nc.psum_tensor(name, shape, dtype)), name)

    def _sem_of(self, p):
        return self.sems[p] if p in self.sems else self.extra[p][0]

    def _wait(self, eng, p, v):
        w = self.waited[eng]
        if w.get(p, -1) >= v:
            return
        w[p] = v
        self.items[eng].append(("w", self._sem_of(p), v))

    def _edges(self, eng, ins, outs, accum):
        need = {}

        def add(p, v):
            if v > need.get(p, -1):
                need[p] = v

        for b in ins:
            if b is not None:
                for p, v in b.writers.items():
                    add(p, v)
        for b in outs:
            if b is None:
                continue
            for p, v in b.writers.items():
                add(p, v)
            for p, v in b.readers.items():
                add(p, v)
        for p in sorted(need):
            v = self.extra[p][1] if p in self.extra else need[p]
            self._wait(eng, p, v)

    def op(self, eng, fn, ins, outs, accum=False):
        self._edges(eng, ins, outs, accum)
        self.cnt[eng] += 1
        v = self.cnt[eng]
        self.items[eng].append(("o", fn, self.sems[eng], 1))
        for b in ins:
            if b is not None:
                b.readers[eng] = v
        for b in outs:
            if b is not None:
                b.writers[eng] = v
        return v

    def dma(self, queue, sem_name, fn, ins, outs):
        self._edges(queue, ins, outs, False)
        if sem_name not in self.extra:
            self.extra[sem_name] = [self.stack.enter_context(self.nc.semaphore("s_" + sem_name)), 0]
        ent = self.extra[sem_name]
        ent[1] += 16
        v = ent[1]
        self.items[queue].append(("o", fn, ent[0], 16))
        for b in ins:
            if b is not None:
                b.readers[sem_name] = v
        for b in outs:
            if b is not None:
                b.writers[sem_name] = v
        return v

    def wait_stream(self, eng, p):
        v = self.cnt[p] if p in self.sems else (self.extra[p][1] if p in self.extra else 0)
        if v > 0:
            self._wait(eng, p, v)

    def build_block(self):
        block = self.stack.enter_context(self.nc.Block())

        def make(name):
            seq = self.items[name]

            def body(eng):
                for it in seq:
                    if it[0] == "w":
                        eng.wait_ge(it[1], it[2])
                    else:
                        it[1](eng).then_inc(it[2], it[3])
            return body

        block.tensor(make("te"))
        block.vector(make("ve"))
        block.scalar(make("se"))
        block.sync(make("qs"))
        block.gpsimd(make("qg"))


def build_core_program(n_own, n_pre):
    NCH = n_own // CHUNK
    nc = bass.Bass("TRN2", target_bir_lowering=False)
    DI = nc.dram_tensor
    T = {}
    T["x_own"] = DI("x_own", [n_own, D_MODEL], F32, kind="ExternalInput").ap()
    T["x_pre"] = DI("x_pre", [max(n_pre, 1), D_MODEL], F32, kind="ExternalInput").ap()
    T["W_in"] = DI("W_in", [2 * D_INNER, D_MODEL], F32, kind="ExternalInput").ap()
    T["pre_Win"] = DI("pre_Win", [D_INNER, D_MODEL], F32, kind="ExternalInput").ap()
    T["conv_w"] = DI("conv_w", [D_INNER, 4], F32, kind="ExternalInput").ap()
    T["conv_b"] = DI("conv_b", [D_INNER], F32, kind="ExternalInput").ap()
    T["pre_conv_b"] = DI("pre_conv_b", [D_INNER], F32, kind="ExternalInput").ap()
    T["W_xproj"] = DI("W_xproj", [2 * D_STATE + 1, D_INNER], F32, kind="ExternalInput").ap()
    T["A_log"] = DI("A_log", [D_STATE], F32, kind="ExternalInput").ap()
    T["D_param"] = DI("D_param", [D_INNER], F32, kind="ExternalInput").ap()
    T["W_out"] = DI("W_out", [D_MODEL, D_INNER], F32, kind="ExternalInput").ap()
    T["w_gate"] = DI("w_gate", [FFN, D_MODEL], F32, kind="ExternalInput").ap()
    T["w_up"] = DI("w_up", [FFN, D_MODEL], F32, kind="ExternalInput").ap()
    T["w_down"] = DI("w_down", [D_MODEL, FFN], F32, kind="ExternalInput").ap()
    T["ident"] = DI("ident", [128, 128], BF16, kind="ExternalInput").ap()
    T["identf"] = DI("identf", [16, 16], F32, kind="ExternalInput").ap()
    T["maskbig"] = DI("maskbig", [128, 128], BF16, kind="ExternalInput").ap()
    T["out"] = DI("out", [n_own, D_MODEL], F32, kind="ExternalOutput").ap()
    T["z_sp"] = DI("z_sp", [128, 16, n_own], BF16).ap()
    T["xc_sp"] = DI("xc_sp", [128, 16, n_own], BF16).ap()
    T["xq1_sp"] = DI("xq1_sp", [128, 8, n_own], BF16).ap()
    T["xq2_sp"] = DI("xq2_sp", [128, 32, n_own], BF16).ap()
    T["t1_sp"] = DI("t1_sp", [128, 16, n_own], FP16).ap()
    T["xat_sp"] = DI("xat_sp", [128, 8, n_own], F32).ap()
    T["h_sp"] = DI("h_sp", [D_STATE, NCH * D_INNER], BF16).ap()

    with contextlib.ExitStack() as stack:
        B = Builder(nc, stack)
        _emit(B, T, n_own, n_pre)
        B.build_block()
    return nc


def _emit(B, T, n_own, n_pre):
    NCH = n_own // CHUNK
    NT_OWN = n_own // TILE
    NT_PRE = n_pre // TILE
    CPT = TILE // CHUNK
    NG = n_own // 128
    GPT = TILE // 128

    def vop(fn, ins, outs, accum=False):
        B.op("ve", fn, ins, outs, accum)

    def sop(fn, ins, outs, accum=False):
        B.op("se", fn, ins, outs, accum)

    def top(fn, ins, outs, accum=False):
        B.op("te", fn, ins, outs, accum)

    dz = Buf(None, "z_sp")
    dxc = Buf(None, "xc_sp")
    dq1 = Buf(None, "xq1_sp")
    dq2 = Buf(None, "xq2_sp")
    dt1 = Buf(None, "t1_sp")
    dxat = Buf(None, "xat_sp")
    dh = Buf(None, "h_sp")

    wpool = B.sbuf("wpool", [128, 32768], BF16)
    scr = [B.sbuf("scr%d" % i, [128, 2048], F32) for i in range(3)]
    sbf = [B.sbuf("sbf%d" % i, [128, 2048], BF16) for i in range(2)]
    wb16 = [B.sbuf("wb16_%d" % i, [128, 2048], BF16) for i in range(3)]
    WxT = B.sbuf("WxT", [128, 16 * 33], BF16)
    convw = B.sbuf("convw", [128, 16 * 4], F32)
    convb = B.sbuf("convb", [128, 16], F32)
    pconvb = B.sbuf("pconvb", [128, 16], F32)
    Dp = B.sbuf("Dp", [128, 16], F32)
    Acol = B.sbuf("Acol", [16, 1], F32)
    ident = B.sbuf("id16", [128, 128], BF16)
    identf = B.sbuf("idf32", [16, 16], F32)
    maskb = B.sbuf("maskb", [128, 128], BF16)
    ones1x16 = B.sbuf("ones1x16", [1, 16], F32)
    screp = B.sbuf("screp", [128, 2], F32)
    onesc = B.sbuf("onesc", [128, 1], F32)
    zcol16 = B.sbuf("zcol16", [16, 1], F32)
    cumT_own = B.sbuf("cumT_own", [128, NCH * 16], F32)
    BbT_own = B.sbuf("BbT_own", [128, NCH * 16], BF16)
    CT_own = B.sbuf("CT_own", [128, NCH * 16], BF16)
    Chp_all = B.sbuf("Chp_all", [16, n_own], BF16)
    c1_all = B.sbuf("c1_all", [128, NG], F32)
    c2_all = B.sbuf("c2_all", [128, NG], F32)
    swneg = [B.sbuf("swneg%d" % i, [128, 1], F32) for i in range(3)]
    thrp = [B.sbuf("thrp%d" % i, [128, 1], F32) for i in range(3)]
    thrn = [B.sbuf("thrn%d" % i, [128, 1], F32) for i in range(3)]
    cumc = B.sbuf("cumc", [16, 1 + TILE], F32)
    carry = B.sbuf("carry", [16, 1], F32)
    epscol = B.sbuf("epscol", [128, 1], F32)
    Hrow = scr[2]
    HROW = [0, 32]

    pb = [B.psum("pb%d" % i, [128, 512], F32) for i in range(6)]
    pb.append(pb[4])
    pt = [B.psum("pt%d" % i, [128, 512], BF16) for i in range(2)]
    PTOFF = [0, 0]

    B.dma("qs", "dsm", lambda q: q.dma_start(
        out=bass.AP(convw.h, 0, [[64, 128], [4, 16], [1, 4]]),
        in_=bass.AP(T["conv_w"].tensor, 0, [[4, 128], [512, 16], [1, 4]])), [], [convw])
    _pvec_loads = []
    for _nm, _dst in (("conv_b", convb), ("pre_conv_b", pconvb), ("D_param", Dp)):
        _pvec_loads.append((_nm, _dst))
    B.dma("qs", "dsm", lambda q: q.dma_start(out=Acol.h[:, 0:1], in_=T["A_log"][:, None]),
          [], [Acol])
    B.dma("qs", "dsm", lambda q: q.dma_start(out=ident.h[:, :], in_=T["ident"]), [], [ident])
    B.dma("qs", "dsm", lambda q: q.dma_start(out=identf.h[:, :], in_=T["identf"]), [], [identf])
    B.dma("qs", "dsm", lambda q: q.dma_start(out=maskb.h[:, :], in_=T["maskbig"]), [], [maskb])
    sop(lambda e: e.activation(Acol.h[:, 0:1], Acol.h[:, 0:1], AF.Exp), [Acol], [Acol])
    vop(lambda e: e.tensor_scalar_mul(Acol.h[:, 0:1], Acol.h[:, 0:1], -1.0), [Acol], [Acol])
    for bz, val in ((zcol16, 0.0), (ones1x16, 1.0), (onesc, 1.0)):
        vop(lambda e, bz=bz, val=val: e.memset(bz.h[:, :], val), [], [bz])
    vop(lambda e: e.memset(carry.h[:, :], 0.0), [], [carry])
    vop(lambda e: e.memset(epscol.h[:, :], 1e-5), [], [epscol])
    vop(lambda e: e.memset(Hrow.h[0:16, :], 0.0), [], [Hrow])

    trcnt = [0]

    def transpose_to(dst, dst_col, src_buf, src_ap, n, evac="ve"):
        slot = trcnt[0] % 2
        ps = pt[slot]
        off = PTOFF[slot]
        trcnt[0] += 1
        top(lambda e, ps=ps, src_ap=src_ap, n=n, off=off: e.transpose(
            ps.h[0:128, off:off + n], src_ap, ident.h[0:n, 0:n]),
            [src_buf, ident], [ps])
        if evac == "se":
            sop(lambda e, ps=ps, dst=dst, dst_col=dst_col, n=n, off=off: e.activation(
                dst.h[:, dst_col:dst_col + n], ps.h[0:128, off:off + n], AF.Identity),
                [ps], [dst], accum=True)
        else:
            vop(lambda e, ps=ps, dst=dst, dst_col=dst_col, n=n, off=off: e.tensor_copy(
                dst.h[:, dst_col:dst_col + n], ps.h[0:128, off:off + n]), [ps], [dst],
                accum=True)

    def load_transpose_bf16(src_t, rows, cols, dst_off, dst_span, tag):
        for rb in range(rows // 128):
            stg = sbf[rb % 2]
            B.dma("qg", tag + str(rb % 2), lambda q, rb=rb, stg=stg, cols=cols, src_t=src_t:
                  q.dma_start(out=stg.h[:, 0:cols],
                              in_=bass.AP(src_t.tensor, rb * 128 * cols,
                                          [[cols, 128], [1, cols]])), [], [stg])
            for kb in range(cols // 128):
                transpose_to(wpool, dst_off + kb * dst_span + rb * 128, stg,
                             stg.h[:, kb * 128:(kb + 1) * 128], 128)

    wx_stage = sbf[1]
    B.dma("qg", "dwx", lambda q: q.dma_start(out=wx_stage.h[0:33, :], in_=T["W_xproj"]),
          [], [wx_stage])
    for kb in range(16):
        transpose_to(WxT, kb * 33, wx_stage, wx_stage.h[0:33, kb * 128:(kb + 1) * 128], 33)

    wred = B.sbuf("wred", [128, 1], F32)
    wsum = [B.sbuf("wsum%d" % i, [128, 1], F32) for i in range(3)]
    sc_sb = B.sbuf("sc_sb", [1, 2], F32)
    wmats = [(T["w_gate"], FFN, D_MODEL), (T["w_up"], FFN, D_MODEL), (T["w_down"], D_MODEL, FFN)]
    for wi, (wt, nrows, ncols) in enumerate(wmats):
        nseg = max(1, ncols // 2048)
        segc = ncols // nseg
        si = 0
        for rb in range(nrows // 128):
            for sg in range(nseg):
                stg = scr[si % 2]
                si += 1
                B.dma("qs", "dwf%d" % (si % 2), lambda q, rb=rb, sg=sg, wt=wt, segc=segc,
                      ncols=ncols, stg=stg: q.dma_start(
                    out=stg.h[:, 0:segc],
                    in_=bass.AP(wt.tensor, rb * 128 * ncols + sg * segc,
                                [[ncols, 128], [1, segc]])), [], [stg])
                first = (rb == 0 and sg == 0)
                dst = wsum[wi] if first else wred
                vop(lambda e, stg=stg, segc=segc, dst=dst: e.reduce_sum(
                    dst.h[:, 0:1], stg.h[:, 0:segc], axis=AX.X, apply_absolute_value=True),
                    [stg], [dst])
                if not first:
                    vop(lambda e, wi=wi: e.tensor_add(wsum[wi].h[:, 0:1], wsum[wi].h[:, 0:1],
                                                      wred.h[:, 0:1]), [wsum[wi], wred],
                        [wsum[wi]])
        top(lambda e, wi=wi: e.matmul(pb[6].h[0:1, 0:1], onesc.h[:, 0:1], wsum[wi].h[:, 0:1],
                                      start=True, stop=True), [onesc, wsum[wi]], [pb[6]])
        vop(lambda e, n=nrows * ncols: e.tensor_scalar(
            sc_sb.h[0:1, 0:1], pb[6].h[0:1, 0:1], 1.0 / n, 1e-5, A_.mult, A_.max),
            [pb[6]], [sc_sb])
        vop(lambda e: e.tensor_scalar_mul(sc_sb.h[0:1, 1:2], sc_sb.h[0:1, 0:1], 0.5),
            [sc_sb], [sc_sb], accum=True)
        B.dma("qg", "dscr", lambda q: q.dma_start(
            out=screp.h[:, :],
            in_=bass.AP(sc_sb.h, 0, [[1, 1], [0, 128], [1, 2]])), [sc_sb], [screp])
        vop(lambda e, wi=wi: e.tensor_scalar_mul(swneg[wi].h[:, 0:1], screp.h[:, 0:1], -1.0),
            [screp], [swneg[wi]])
        vop(lambda e, wi=wi: e.tensor_copy(thrp[wi].h[:, 0:1], screp.h[:, 1:2]),
            [screp], [thrp[wi]])
        vop(lambda e, wi=wi: e.tensor_scalar_mul(thrn[wi].h[:, 0:1], screp.h[:, 1:2], -1.0),
            [screp], [thrn[wi]])

    xg = [B.sbuf("xg%d" % i, [128, D_MODEL], F32) for i in range(2)]
    mcol = B.sbuf("mcol", [128, 1], F32)
    m2col = B.sbuf("m2col", [128, 1], F32)
    sqcol = B.sbuf("sqcol", [128, 1], F32)
    rstd = B.sbuf("rstd", [128, 1], F32)
    nmr = B.sbuf("nmr", [128, 1], F32)
    xn32 = B.sbuf("xn32", [128, D_MODEL], F32)
    sqbuf = xn32
    xnbf = B.sbuf("xnbf", [128, D_MODEL], BF16)
    xnfm = B.sbuf("xnfm", [128, 8 * TILE], BF16)
    xproj_t = B.sbuf("xproj_t", [128, 16 * (TILE + 3)], BF16)
    zst = B.sbuf("zst", [128, 512], BF16)
    cvf = scr[0]
    xconv_t = B.sbuf("xconv_t", [128, 16 * TILE], BF16)
    xtm = [B.sbuf("xtm0", [128, D_INNER], BF16)] * 2
    xdbl = scr[1]
    dtbc = B.sbuf("dtbc", [16, TILE], F32)
    dtrow = dtbc
    sprow = dtbc
    dtA = B.sbuf("dtA", [16, TILE], F32)
    Bbar = dtA
    Bbarbf = B.sbuf("Bbarbf", [16, TILE], BF16)
    Cbf = B.sbuf("Cbf", [16, TILE], BF16)
    eexp = B.sbuf("eexp", [16, TILE], BF16)
    Bhat = B.sbuf("Bhat", [16, TILE], BF16)
    BhT = B.sbuf("BhT", [128, CPT * 16], BF16)
    dcol = B.sbuf("dcol", [48, 1], F32)
    amx = B.sbuf("amx", [128, 1], F32)
    scol = B.sbuf("scol", [128, 1], F32)
    scol2 = B.sbuf("scol2", [128, 1], F32)
    lad_w, lad_d = scr[0], scr[1]
    xq_tm = sbf[0]
    sgb = B.sbuf("sgb", [128, 512], FP16)
    upb = B.sbuf("upb", [128, 512], FP16)
    th_a = B.sbuf("th_a", [128, 2048], FP16)
    th_b = B.sbuf("th_b", [128, 2048], FP16)

    qi8 = B.sbuf("qi8", [128, 1024], I8)

    def quant_cast(fn, bufs, w, sc, wb, d, qp, c2, fp16_src=False):
        vop(lambda e: e.tensor_scalar(qi8.h[:, 0:w], fn(), sc.h[:, 0:1],
                                      None, A_.mult), bufs + [sc], [qi8])
        vop(lambda e: e.tensor_copy(wb[0].h[:, wb[1]:wb[1] + w],
                                    qi8.h[:, 0:w]), [qi8], [wb[0]])
        if fp16_src:
            vop(lambda e: e.tensor_scalar(d[0].h[:, d[1]:d[1] + w], fn(), sc.h[:, 0:1],
                                          None, A_.mult), bufs + [sc], [d[0]])
            vop(lambda e: e.tensor_sub(d[0].h[:, d[1]:d[1] + w], d[0].h[:, d[1]:d[1] + w],
                                       wb[0].h[:, wb[1]:wb[1] + w]), [d[0], wb[0]], [d[0]])
        else:
            vop(lambda e: e.scalar_tensor_tensor(
                d[0].h[:, d[1]:d[1] + w], fn(), sc.h[:, 0:1],
                wb[0].h[:, wb[1]:wb[1] + w], A_.mult, A_.subtract),
                bufs + [sc, wb[0]], [d[0]])
        vop(lambda e: e.scalar_tensor_tensor(
            qp[0].h[:, qp[1]:qp[1] + w], d[0].h[:, d[1]:d[1] + w], 0.5,
            wb[0].h[:, wb[1]:wb[1] + w], A_.is_ge, A_.add), [d[0], wb[0]], [qp[0]])
        vop(lambda e: e.tensor_scalar(c2[0].h[:, c2[1]:c2[1] + w],
                                      d[0].h[:, d[1]:d[1] + w], -0.5, None, A_.is_le),
            [d[0]], [c2[0]])
        vop(lambda e: e.tensor_sub(wb[0].h[:, wb[1]:wb[1] + w],
                                   qp[0].h[:, qp[1]:qp[1] + w],
                                   c2[0].h[:, c2[1]:c2[1] + w]), [qp[0], c2[0]], [wb[0]],
            accum=True)

    for _nm, _dst in _pvec_loads:
        B.dma("qs", "dsm", lambda q, _nm=_nm: q.dma_start(
            out=dtbc.h[0:16, 0:128],
            in_=bass.AP(T[_nm].tensor, 0, [[128, 16], [1, 128]])), [], [dtbc])
        top(lambda e: e.transpose(pb[6].h[0:128, 0:16], dtbc.h[0:16, 0:128],
                                  identf.h[0:16, 0:16]), [dtbc, identf], [pb[6]])
        vop(lambda e, _dst=_dst: e.tensor_copy(_dst.h[:, 0:16], pb[6].h[0:128, 0:16]),
            [pb[6]], [_dst])

    vop(lambda e: e.memset(
        bass.AP(xproj_t.h, 0, [[16 * (TILE + 3), 128], [TILE + 3, 16], [1, 3]]), 0.0),
        [], [xproj_t])

    hcur = [0]

    def mixer_tiles(prefix):
        ntiles = NT_PRE if prefix else NT_OWN
        x_src = T["x_pre"] if prefix else T["x_own"]
        cbcol = pconvb if prefix else convb
        n_oc = 16 if prefix else 32
        span = D_INNER if prefix else 2 * D_INNER
        def stageA(ti):
            t0 = ti * TILE
            for gp in range(GPT // 2):
                for g in (2 * gp, 2 * gp + 1):
                    xt = xg[g % 2]
                    sc = scol if g % 2 == 0 else scol2
                    B.dma("qs", "dx%d" % (g % 2), lambda q, t0=t0, g=g, xt=xt, x_src=x_src:
                          q.dma_start(out=xt.h[:, :],
                                      in_=bass.AP(x_src.tensor, (t0 + g * 128) * D_MODEL,
                                                  [[D_MODEL, 128], [1, D_MODEL]])), [], [xt])
                    vop(lambda e, xt=xt: e.reduce_sum(mcol.h[:, 0:1], xt.h[:, :], axis=AX.X),
                        [xt], [mcol])
                    sop(lambda e, xt=xt: e.activation(sqbuf.h[:, :], xt.h[:, :], AF.Square),
                        [xt], [sqbuf])
                    vop(lambda e: e.reduce_sum(sqcol.h[:, 0:1], sqbuf.h[:, :], axis=AX.X),
                        [sqbuf], [sqcol])
                    vop(lambda e: e.tensor_scalar_mul(mcol.h[:, 0:1], mcol.h[:, 0:1],
                                                      1.0 / D_MODEL), [mcol], [mcol])
                    vop(lambda e: e.tensor_mul(m2col.h[:, 0:1], mcol.h[:, 0:1],
                                               mcol.h[:, 0:1]), [mcol], [m2col])
                    vop(lambda e: e.scalar_tensor_tensor(sqcol.h[:, 0:1], sqcol.h[:, 0:1],
                                                         1.0 / D_MODEL, m2col.h[:, 0:1],
                                                         A_.mult, A_.subtract),
                        [sqcol, m2col], [sqcol])
                    sop(lambda e: e.activation(rstd.h[:, 0:1], sqcol.h[:, 0:1], AF.Sqrt,
                                               bias=epscol.h[:, 0:1]), [sqcol, epscol],
                        [rstd])
                    vop(lambda e: e.reciprocal(rstd.h[:, 0:1], rstd.h[:, 0:1]),
                        [rstd], [rstd])
                    vop(lambda e: e.scalar_tensor_tensor(nmr.h[:, 0:1], mcol.h[:, 0:1],
                                                         -1.0, rstd.h[:, 0:1],
                                                         A_.mult, A_.mult),
                        [mcol, rstd], [nmr])
                    sop(lambda e, xt=xt: e.activation(xt.h[:, :], xt.h[:, :], AF.Identity,
                                                      bias=nmr.h[:, 0:1],
                                                      scale=rstd.h[:, 0:1]),
                        [xt, nmr, rstd], [xt])
                    vop(lambda e, xt=xt: e.tensor_copy(xnbf.h[:, :], xt.h[:, :]),
                        [xt], [xnbf])
                    for kc in range(8):
                        transpose_to(xnfm, kc * TILE + g * 128, xnbf,
                                     xnbf.h[:, kc * 128:(kc + 1) * 128], 128)
                    if not prefix:
                        gi = (t0 + g * 128) // 128
                        vop(lambda e, xt=xt: e.reduce_max(amx.h[:, 0:1], xt.h[:, :],
                                                          axis=AX.X,
                                                          apply_absolute_value=True),
                            [xt], [amx])
                        vop(lambda e: e.tensor_scalar_max(amx.h[:, 0:1], amx.h[:, 0:1],
                                                          1e-5), [amx], [amx])
                        vop(lambda e, sc=sc: e.reciprocal(sc.h[:, 0:1], amx.h[:, 0:1]),
                            [amx], [sc])
                        vop(lambda e, sc=sc: e.tensor_scalar_mul(sc.h[:, 0:1],
                                                                 sc.h[:, 0:1], 127.0),
                            [sc], [sc])
                        vop(lambda e, gi=gi, sc=sc: e.reciprocal(
                            c1_all.h[:, gi:gi + 1], sc.h[:, 0:1]), [sc],
                            [c1_all], accum=True)
                        quant_cast(lambda xt=xt: xt.h[:, :], [xg[g % 2]], D_MODEL, sc,
                                   (sbf[g % 2], 0), (scr[g % 2], 1024),
                                   (sbf[g % 2], 1024), (xnbf, 0))
                        for kc in range(8):
                            col = kc * TILE + g * 128
                            dst = wb16[0] if col < 2048 else wb16[1]
                            transpose_to(dst, col % 2048, sbf[g % 2],
                                         sbf[g % 2].h[:, kc * 128:(kc + 1) * 128], 128)
            if not prefix:
                B.dma("qg", "dxq1", lambda q, t0=t0: q.dma_start(
                    out=bass.AP(T["xq1_sp"].tensor, t0,
                                [[8 * n_own, 128], [n_own, 4], [1, TILE]]),
                    in_=bass.AP(wb16[0].h, 0, [[2048, 128], [TILE, 4], [1, TILE]])),
                    [wb16[0]], [dq1])
                B.dma("qg", "dxq1", lambda q, t0=t0: q.dma_start(
                    out=bass.AP(T["xq1_sp"].tensor, 4 * n_own + t0,
                                [[8 * n_own, 128], [n_own, 4], [1, TILE]]),
                    in_=bass.AP(wb16[1].h, 0, [[2048, 128], [TILE, 4], [1, TILE]])),
                    [wb16[1]], [dq1])
        def stageB(ti):
            t0 = ti * TILE
            for oc in range(n_oc):
                ps = pb[oc % 2]
                for kc in range(8):
                    top(lambda e, oc=oc, kc=kc, ps=ps, span=span: e.matmul(
                        ps.h[:, :], wpool.h[:, kc * span + oc * 128:kc * span + (oc + 1) * 128],
                        xnfm.h[:, kc * TILE:(kc + 1) * TILE],
                        start=(kc == 0), stop=(kc == 7)), [wpool, xnfm], [ps], accum=(kc > 0))
                if oc < 16:
                    vop(lambda e, oc=oc, ps=ps: e.tensor_copy(
                        xproj_t.h[:, oc * (TILE + 3) + 3:oc * (TILE + 3) + 3 + TILE],
                        ps.h[:, :]), [ps], [xproj_t], accum=True)
                else:
                    vop(lambda e, ps=ps: e.tensor_copy(zst.h[:, :], ps.h[:, :]), [ps], [zst])
                    B.dma("qg", "dzs", lambda q, t0=t0, oc=oc: q.dma_start(
                        out=bass.AP(T["z_sp"].tensor, (oc - 16) * n_own + t0,
                                    [[16 * n_own, 128], [1, TILE]]),
                        in_=zst.h[:, :]), [zst], [dz])
        def stageC(ti):
            t0 = ti * TILE
            for c in range(16):
                base = c * (TILE + 3)
                vop(lambda e, base=base, c=c, cb=cbcol: e.tensor_scalar(
                    cvf.h[:, 0:TILE], xproj_t.h[:, base:base + TILE], convw.h[:, 4 * c:4 * c + 1],
                    cb.h[:, c:c + 1], A_.mult, A_.add), [xproj_t, convw, cbcol], [cvf])
                for k in (1, 2, 3):
                    vop(lambda e, base=base, c=c, k=k: e.scalar_tensor_tensor(
                        cvf.h[:, 0:TILE], xproj_t.h[:, base + k:base + k + TILE],
                        convw.h[:, 4 * c + k:4 * c + k + 1], cvf.h[:, 0:TILE], A_.mult, A_.add),
                        [xproj_t, convw, cvf], [cvf])
                sop(lambda e, c=c: e.activation(xconv_t.h[:, c * TILE:(c + 1) * TILE],
                                                cvf.h[:, 0:TILE], AF.Silu), [cvf], [xconv_t],
                    accum=True)
            vop(lambda e: e.tensor_copy(
                bass.AP(xproj_t.h, 0, [[16 * (TILE + 3), 128], [TILE + 3, 16], [1, 3]]),
                bass.AP(xproj_t.h, TILE, [[16 * (TILE + 3), 128], [TILE + 3, 16], [1, 3]])),
                [xproj_t], [xproj_t], accum=True)
            if not prefix:
                B.dma("qg", "dxcs", lambda q, t0=t0: q.dma_start(
                    out=bass.AP(T["xc_sp"].tensor, t0,
                                [[16 * n_own, 128], [n_own, 16], [1, TILE]]),
                    in_=bass.AP(xconv_t.h, 0, [[16 * TILE, 128], [TILE, 16], [1, TILE]])),
                    [xconv_t], [dxc])
        def stageD(ti):
            t0 = ti * TILE
            for kc in range(16):
                top(lambda e, kc=kc: e.matmul(
                    pb[6].h[0:16, :], WxT.h[:, kc * 33:kc * 33 + 16],
                    xconv_t.h[:, kc * TILE:(kc + 1) * TILE],
                    start=(kc == 0), stop=(kc == 15)), [WxT, xconv_t], [pb[6]],
                    accum=(kc > 0))
            for kc in range(16):
                top(lambda e, kc=kc: e.matmul(
                    pb[0].h[0:16, :], WxT.h[:, kc * 33 + 16:kc * 33 + 32],
                    xconv_t.h[:, kc * TILE:(kc + 1) * TILE],
                    start=(kc == 0), stop=(kc == 15)), [WxT, xconv_t], [pb[0]],
                    accum=(kc > 0))
            for kc in range(16):
                top(lambda e, kc=kc: e.matmul(
                    pb[1].h[0:1, :], WxT.h[:, kc * 33 + 32:kc * 33 + 33],
                    xconv_t.h[:, kc * TILE:(kc + 1) * TILE],
                    start=(kc == 0), stop=(kc == 15)), [WxT, xconv_t], [pb[1]],
                    accum=(kc > 0))
            vop(lambda e: e.tensor_copy(xdbl.h[0:16, 0:TILE], pb[6].h[0:16, :]), [pb[6]], [xdbl])
            vop(lambda e: e.tensor_copy(Cbf.h[:, :], pb[0].h[0:16, :]), [pb[0]], [Cbf])
            sop(lambda e: e.activation(cvf.h[0:1, 0:TILE], pb[1].h[0:1, :], AF.Abs),
                [pb[1]], [cvf])
            sop(lambda e: e.activation(cvf.h[0:1, 0:TILE], cvf.h[0:1, 0:TILE], AF.Exp, scale=-1.0),
                [cvf], [cvf], accum=True)
            sop(lambda e: e.activation(cvf.h[0:1, 0:TILE], cvf.h[0:1, 0:TILE], AF.Ln, bias=1.0),
                [cvf], [cvf], accum=True)
            vop(lambda e: e.scalar_tensor_tensor(dtbc.h[0:1, :], pb[1].h[0:1, :], 0.0,
                                                 cvf.h[0:1, 0:TILE], A_.max, A_.add),
                [pb[1], cvf], [dtbc], accum=True)
            vop(lambda e: e.tensor_scalar(dtbc.h[0:1, :], dtbc.h[0:1, :], 0.1, 0.001,
                                          A_.min, A_.max), [dtbc], [dtbc], accum=True)
            top(lambda e: e.matmul(pb[1].h[0:16, :], ones1x16.h[:, :], dtbc.h[0:1, :],
                                   start=True, stop=True), [ones1x16, dtbc], [pb[1]])
            vop(lambda e: e.tensor_copy(dtbc.h[:, :], pb[1].h[0:16, :]), [pb[1]], [dtbc])
            vop(lambda e: e.tensor_scalar(dtA.h[:, :], dtbc.h[:, :], Acol.h[:, 0:1], 0.0,
                                          A_.mult, A_.min), [dtbc, Acol], [dtA])
            vop(lambda e: e.tensor_scalar_max(dtA.h[:, :], dtA.h[:, :], -20.0), [dtA], [dtA])
            vop(lambda e: e.tensor_copy(cumc.h[:, 0:1], carry.h[:, :]), [carry], [cumc])
            vop(lambda e: e.tensor_tensor_scan(
                cumc.h[:, 1:1 + TILE], dtA.h[:, :],
                bass.AP(zcol16.h, 0, [[1, 16], [0, TILE]]), cumc.h[:, 0:1],
                A_.add, A_.add), [dtA, zcol16, cumc], [cumc], accum=True)
            vop(lambda e: e.tensor_copy(carry.h[:, :], cumc.h[:, TILE:TILE + 1]),
                [cumc], [carry])
            vop(lambda e: e.tensor_mul(Bbar.h[:, :], dtbc.h[:, :], xdbl.h[0:16, 0:TILE]),
                [dtbc, xdbl], [Bbar])
            vop(lambda e: e.tensor_scalar(Bbar.h[:, :], Bbar.h[:, :], 10.0, -10.0,
                                          A_.min, A_.max), [Bbar], [Bbar])
            vop(lambda e: e.tensor_sub(
                Bhat.h[:, :].rearrange("p (c q) -> p c q", q=CHUNK),
                cumc.h[:, 1:1 + TILE].rearrange("p (c q) -> p c q", q=CHUNK),
                bass.AP(cumc.h, CHUNK, [[1 + TILE, 16], [CHUNK, CPT], [0, CHUNK]])),
                [cumc], [Bhat])
            sop(lambda e: e.activation(eexp.h[:, :], Bhat.h[:, :], AF.Exp, scale=-1.0),
                [Bhat], [eexp])
            vop(lambda e: e.tensor_mul(Bhat.h[:, :], eexp.h[:, :], Bbar.h[:, :]),
                [eexp, Bbar], [Bhat])
            if not prefix:
                vop(lambda e: e.tensor_copy(Bbarbf.h[:, :], Bbar.h[:, :]), [Bbar], [Bbarbf])
                vop(lambda e: e.tensor_sub(
                    eexp.h[:, :].rearrange("p (c q) -> p c q", q=CHUNK),
                    cumc.h[:, 1:1 + TILE].rearrange("p (c q) -> p c q", q=CHUNK),
                    bass.AP(cumc.h, 0, [[1 + TILE, 16], [CHUNK, CPT], [0, CHUNK]])),
                    [cumc], [eexp])
                sop(lambda e: e.activation(eexp.h[:, :], eexp.h[:, :], AF.Exp), [eexp], [eexp])
                vop(lambda e, t0=t0: e.tensor_mul(Chp_all.h[:, t0:t0 + TILE], eexp.h[:, :],
                                                  Cbf.h[:, :]), [eexp, Cbf],
                    [Chp_all], accum=True)
            for c in range(CPT):
                xt_ = xtm[c % 2]
                for dc in range(16):
                    transpose_to(xt_, dc * 128, xconv_t,
                                 xconv_t.h[:, dc * TILE + c * CHUNK:dc * TILE + (c + 1) * CHUNK],
                                 128)
                gk = (t0 + c * CHUNK) // CHUNK
                transpose_to(BhT, c * 16, Bhat, Bhat.h[:, c * CHUNK:(c + 1) * CHUNK], 16)
                if not prefix:
                    transpose_to(BbT_own, gk * 16, Bbarbf,
                                 Bbarbf.h[:, c * CHUNK:(c + 1) * CHUNK], 16)
                    transpose_to(CT_own, gk * 16, Cbf,
                                 Cbf.h[:, c * CHUNK:(c + 1) * CHUNK], 16)
                    top(lambda e, c=c: e.transpose(
                        pb[6].h[0:128, 0:16], cumc.h[:, 1 + c * CHUNK:1 + (c + 1) * CHUNK],
                        identf.h[0:16, 0:16]), [cumc, identf], [pb[6]])
                    vop(lambda e, gk=gk: e.tensor_copy(
                        cumT_own.h[:, gk * 16:(gk + 1) * 16], pb[6].h[0:128, 0:16]),
                        [pb[6]], [cumT_own], accum=True)
                vop(lambda e, c=c: e.tensor_scalar(
                    dcol.h[0:16, 0:1], cumc.h[:, (c + 1) * CHUNK:(c + 1) * CHUNK + 1],
                    cumc.h[:, c * CHUNK:c * CHUNK + 1], None, A_.subtract), [cumc], [dcol])
                sop(lambda e: e.activation(dcol.h[0:16, 0:1], dcol.h[0:16, 0:1], AF.Exp),
                    [dcol], [dcol])
                vop(lambda e: e.tensor_copy(dcol.h[32:48, 0:1], dcol.h[0:16, 0:1]),
                    [dcol], [dcol], accum=True)
                cu, nx = hcur[0], 1 - hcur[0]
                if not prefix:
                    B.dma("qg", "dhs", lambda q, gk=gk, cu=cu: q.dma_start(
                        out=T["h_sp"][:, gk * D_INNER:(gk + 1) * D_INNER],
                        in_=Hrow.h[HROW[cu]:HROW[cu] + 16, :]), [Hrow], [dh])
                HBANKS = (2, 3, 5, 0)
                for j in range(4):
                    bk = HBANKS[j]
                    top(lambda e, c=c, j=j, xt_=xt_, bk=bk: e.matmul(
                        pb[bk].h[0:16, :], BhT.h[:, c * 16:(c + 1) * 16],
                        xt_.h[:, j * 512:(j + 1) * 512], start=True, stop=True),
                        [BhT, xt_], [pb[bk]])
                for j in range(4):
                    bk = HBANKS[j]
                    vop(lambda e, j=j, cu=cu, nx=nx, bk=bk: e.scalar_tensor_tensor(
                        Hrow.h[HROW[nx]:HROW[nx] + 16, j * 512:(j + 1) * 512],
                        Hrow.h[HROW[cu]:HROW[cu] + 16, j * 512:(j + 1) * 512],
                        dcol.h[HROW[cu]:HROW[cu] + 16, 0:1], pb[bk].h[0:16, :],
                        A_.mult, A_.add),
                        [Hrow, dcol, pb[bk]], [Hrow], accum=True)
                hcur[0] = nx


        stageA(0)
        stageB(0)
        for ti in range(ntiles):
            if ti + 1 < ntiles:
                stageA(ti + 1)
            stageC(ti)
            if ti + 1 < ntiles:
                stageB(ti + 1)
            stageD(ti)
    if NT_PRE > 0:
        load_transpose_bf16(T["pre_Win"], D_INNER, D_MODEL, 0, D_INNER, "dwl")
        mixer_tiles(prefix=True)
    load_transpose_bf16(T["W_in"], 2 * D_INNER, D_MODEL, 0, 2 * D_INNER, "dwl")
    mixer_tiles(prefix=False)

    xq1g = xnbf
    k1g = sqcol
    k1u = rstd

    xq1gp = [Buf(xnfm.h, "xq1ga"), Buf(xnfm.h, "xq1gb")]

    def ffn_half(hf):
        wqst = sbf[1]
        for wi, wt in ((0, T["w_gate"]), (1, T["w_up"])):
            for rb in range(16):
                row0 = hf * 2048 + rb * 128
                stg = scr[0] if rb % 2 == 0 else scr[2]
                B.dma("qs", "dwf%d" % (rb % 2), lambda q, row0=row0, wt=wt, stg=stg:
                      q.dma_start(out=stg.h[:, 0:D_MODEL],
                                  in_=bass.AP(wt.tensor, row0 * D_MODEL,
                                              [[D_MODEL, 128], [1, D_MODEL]])), [], [stg])
                vop(lambda e, stg=stg, wi=wi: e.tensor_scalar(
                    lad_d.h[:, 0:D_MODEL], stg.h[:, 0:D_MODEL], thrp[wi].h[:, 0:1], None,
                    A_.is_ge), [stg, thrp[wi]], [lad_d])
                vop(lambda e, stg=stg, wi=wi: e.scalar_tensor_tensor(
                    wqst.h[:, 0:D_MODEL], stg.h[:, 0:D_MODEL], thrn[wi].h[:, 0:1],
                    lad_d.h[:, 0:D_MODEL], A_.is_le, A_.subtract),
                    [stg, thrn[wi], lad_d], [wqst])
                for kb in range(8):
                    transpose_to(wpool, wi * 16384 + kb * 2048 + rb * 128, wqst,
                                 wqst.h[:, kb * 128:(kb + 1) * 128], 128)
        def xq1_load(g):
            par = g % 2
            extra = [xnfm] if (hf == 0 and g == 0) else []
            B.dma("qs", "dq1r%d" % par, lambda q, g=g, par=par: q.dma_start(
                out=bass.AP(xnfm.h, par * 1024, [[4096, 128], [128, 8], [1, 128]]),
                in_=bass.AP(T["xq1_sp"].tensor, g * 128,
                            [[8 * n_own, 128], [n_own, 8], [1, 128]])),
                [dq1], [xq1gp[par]] + extra)

        xq1_load(0)
        pend = [None]
        for g in range(NG):
            if g + 1 < NG:
                xq1_load(g + 1)
            xq1g_ap = lambda kc, g=g: bass.AP(
                xnfm.h, (g % 2) * 1024 + kc * 128, [[4096, 128], [1, 128]])
            xq1g_buf = xq1gp[g % 2]
            vop(lambda e, g=g: e.tensor_mul(k1g.h[:, 0:1], swneg[0].h[:, 0:1],
                                            c1_all.h[:, g:g + 1]), [swneg[0], c1_all], [k1g])
            vop(lambda e, g=g: e.tensor_mul(k1u.h[:, 0:1], swneg[1].h[:, 0:1],
                                            c1_all.h[:, g:g + 1]), [swneg[1], c1_all], [k1u])
            for nh in range(4):
                gbk = pb[0] if nh % 2 == 0 else pb[2]
                ubk = pb[1] if nh % 2 == 0 else pb[3]
                sgp = (sgb, upb) if nh % 2 == 0 else (upb, sgb)
                for kc in range(8):
                    top(lambda e, nh=nh, kc=kc, a=xq1g_ap, gbk=gbk: e.matmul(
                        gbk.h[:, :], a(kc),
                        wpool.h[:, kc * 2048 + nh * 512:kc * 2048 + (nh + 1) * 512],
                        start=(kc == 0), stop=(kc == 7)), [xq1g_buf, wpool], [gbk],
                        accum=(kc > 0))
                for kc in range(8):
                    top(lambda e, nh=nh, kc=kc, a=xq1g_ap, ubk=ubk: e.matmul(
                        ubk.h[:, :], a(kc),
                        wpool.h[:, 16384 + kc * 2048 + nh * 512:
                                16384 + kc * 2048 + (nh + 1) * 512],
                        start=(kc == 0), stop=(kc == 7)), [xq1g_buf, wpool], [ubk],
                        accum=(kc > 0))
                sop(lambda e, gbk=gbk, s0=sgp[0]: e.activation(
                    s0.h[:, :], gbk.h[:, :], AF.Silu,
                    scale=k1g.h[:, 0:1]), [gbk, k1g], [sgp[0]])
                sop(lambda e, ubk=ubk, s1=sgp[1]: e.activation(
                    s1.h[:, :], ubk.h[:, :], AF.Identity,
                    scale=k1u.h[:, 0:1]), [ubk, k1u], [sgp[1]])
                vop(lambda e, nh=nh, sgp=sgp: e.tensor_mul(
                    th_a.h[:, nh * 512:(nh + 1) * 512],
                    sgp[0].h[:, :], sgp[1].h[:, :]), [sgp[0], sgp[1]],
                    [th_a], accum=True)
            if hf == 0:
                B.dma("qg", "dt1s", lambda q, g=g: q.dma_start(
                    out=bass.AP(T["t1_sp"].tensor, g * 128,
                                [[16 * n_own, 128], [n_own, 16], [1, 128]]),
                    in_=bass.AP(th_a.h, 0, [[2048, 128], [128, 16], [1, 128]])),
                    [th_a], [dt1])
            else:
                B.dma("qs", "dt1r", lambda q, g=g: q.dma_start(
                    out=bass.AP(th_b.h, 0, [[2048, 128], [128, 16], [1, 128]]),
                    in_=bass.AP(T["t1_sp"].tensor, g * 128,
                                [[16 * n_own, 128], [n_own, 16], [1, 128]])), [dt1], [th_b])
                vop(lambda e: e.reduce_max(amx.h[:, 0:1], th_a.h[:, :], axis=AX.X,
                                           apply_absolute_value=True), [th_a], [amx])
                vop(lambda e: e.reduce_max(m2col.h[:, 0:1], th_b.h[:, :], axis=AX.X,
                                           apply_absolute_value=True), [th_b], [m2col])
                vop(lambda e: e.tensor_max(amx.h[:, 0:1], amx.h[:, 0:1], m2col.h[:, 0:1]),
                    [amx, m2col], [amx])
                vop(lambda e: e.tensor_scalar_max(amx.h[:, 0:1], amx.h[:, 0:1], 1e-5),
                    [amx], [amx])
                vop(lambda e: e.reciprocal(scol.h[:, 0:1], amx.h[:, 0:1]), [amx], [scol])
                vop(lambda e: e.tensor_scalar_mul(scol.h[:, 0:1], scol.h[:, 0:1], 127.0),
                    [scol], [scol])
                vop(lambda e, g=g: e.reciprocal(c2_all.h[:, g:g + 1], scol.h[:, 0:1]),
                    [scol], [c2_all], accum=True)
                if pend[0] is not None:
                    pend[0]()
                for half2, tb in ((0, th_b), (1, th_a)):
                    for co in (0, 1024):
                        quant_cast(lambda tb=tb, co=co: tb.h[:, co:co + 1024], [tb],
                                   1024, scol, (sbf[half2], co), (scr[half2], co),
                                   (wb16[0], co), (wb16[1], co), fp16_src=True)

                def mk_pend(g=g):
                    def flush():
                        for half2 in (0, 1):
                            for kc in range(16):
                                transpose_to(wb16[2], kc * 128, sbf[half2],
                                             sbf[half2].h[:, kc * 128:(kc + 1) * 128],
                                             128)
                            B.dma("qg", "dxq2", lambda q, half2=half2, g=g: q.dma_start(
                                out=bass.AP(T["xq2_sp"].tensor,
                                            half2 * 16 * n_own + g * 128,
                                            [[32 * n_own, 128], [n_own, 16], [1, 128]]),
                                in_=bass.AP(wb16[2].h, 0, [[2048, 128], [128, 16],
                                                           [1, 128]])),
                                [wb16[2]], [dq2])
                    return flush
                pend[0] = mk_pend()

        if hf == 1 and pend[0] is not None:
            pend[0]()

    ffn_half(0)
    ffn_half(1)

    load_transpose_bf16(T["W_out"], D_MODEL, D_INNER, 0, D_MODEL, "dwl")

    wpar = Buf(wpool.h, "wpar")
    XCK = [(xnfm, 0), (wpar, 16384)]
    ZK = [(xnfm, 2048), (wpar, 18432)]
    HK = [(xconv_t, 0), (wpar, 20480)]
    SSL = [(xconv_t, 2048), (wpar, 22528)]
    ESC = [(wb16[0], 0), (wpar, 22656)]
    CBC = [(wb16[1], 0), (wpar, 24704)]
    YS = [(wb16[2], 0), (wpar, 26752)]
    XTMK = [(xtm[0], 0), (wpar, 28800)]
    XROW = [xg[0], xg[1]]
    DARG = [scr[0], scr[2]]
    cumbc = scr[1]
    xat = xn32

    def p4_xrow(k):
        par = k % 2
        B.dma("qs", "dxr%d" % par, lambda q, k=k, par=par: q.dma_start(
            out=XROW[par].h[:, :],
            in_=bass.AP(T["x_own"].tensor, k * 128 * D_MODEL,
                        [[D_MODEL, 128], [1, D_MODEL]])), [], [XROW[par]])

    def p4_loads(k):
        par = k % 2
        xb, xo = XCK[par]
        zb, zo = ZK[par]
        hb, ho = HK[par]
        db_ = DARG[par]
        extra = [wpool] if k == 1 else (xq1gp if k == 0 else [])
        B.dma("qs", "dxck%d" % par, lambda q, k=k, xb=xb, xo=xo: q.dma_start(
            out=bass.AP(xb.h, xo, [[xb.h.shape[1], 128], [128, 16], [1, 128]]),
            in_=bass.AP(T["xc_sp"].tensor, k * 128,
                        [[16 * n_own, 128], [n_own, 16], [1, 128]])), [dxc], [xb] + extra)
        B.dma("qs", "dzk%d" % par, lambda q, k=k, zb=zb, zo=zo: q.dma_start(
            out=bass.AP(zb.h, zo, [[zb.h.shape[1], 128], [128, 16], [1, 128]]),
            in_=bass.AP(T["z_sp"].tensor, k * 128,
                        [[16 * n_own, 128], [n_own, 16], [1, 128]])), [dz], [zb])
        B.dma("qs", "dhk%d" % par, lambda q, k=k, hb=hb, ho=ho: q.dma_start(
            out=bass.AP(hb.h, ho, [[hb.h.shape[1], 16], [1, D_INNER]]),
            in_=T["h_sp"][:, k * D_INNER:(k + 1) * D_INNER]), [dh], [hb])
        B.dma("qg", "dcf%d" % par, lambda q, k=k, db_=db_: q.dma_start(
            out=db_.h[0:1, 0:2048],
            in_=bass.AP(cumT_own.h, k * 16, [[NCH * 16, 128], [1, 16]])),
            [cumT_own], [db_])
        B.dma("qg", "dcb%d" % par, lambda q, db_=db_: q.dma_start(
            out=cumbc.h[:, :],
            in_=bass.AP(db_.h, 0, [[1, 1], [0, 128], [1, 2048]])),
            [db_], [cumbc])
        B.dma("qg", "dclf%d" % par, lambda q, k=k: q.dma_start(
            out=sbf[0].h[0:1, 0:2048],
            in_=bass.AP(CT_own.h, k * 16, [[NCH * 16, 128], [1, 16]])),
            [CT_own], [sbf[0]])
        cb, co = CBC[par]
        B.dma("qg", "dclb%d" % par, lambda q, cb=cb, co=co: q.dma_start(
            out=bass.AP(cb.h, co, [[cb.h.shape[1], 128], [1, 2048]]),
            in_=bass.AP(sbf[0].h, 0, [[1, 1], [0, 128], [1, 2048]])), [sbf[0]], [cb])

    def p4_scores(k):
        par = k % 2
        db_ = DARG[par]
        eb, eo = ESC[par]
        cb, co = CBC[par]
        sb, so = SSL[par]
        vop(lambda e, k=k, db_=db_: e.tensor_sub(
            db_.h[:, :].rearrange("p (j s) -> p j s", s=16),
            cumbc.h[:, :].rearrange("p (j s) -> p j s", s=16),
            cumT_own.h[:, k * 16:(k + 1) * 16][:, None, :].broadcast_to([128, 128, 16])),
            [cumbc, cumT_own], [db_], accum=True)
        vop(lambda e, db_=db_: e.tensor_sub(
            db_.h[:, :].rearrange("p (j s) -> p j s", s=16),
            db_.h[:, :].rearrange("p (j s) -> p j s", s=16),
            maskb.h[:, :, None].broadcast_to([128, 128, 16])), [db_, maskb], [db_],
            accum=True)
        sop(lambda e, db_=db_, eb=eb, eo=eo: e.activation(
            eb.h[:, eo:eo + 2048], db_.h[:, :], AF.Exp), [db_], [eb])
        vop(lambda e, k=k, eb=eb, eo=eo: e.tensor_mul(
            bass.AP(eb.h, eo, [[eb.h.shape[1], 128], [16, 128], [1, 16]]),
            bass.AP(eb.h, eo, [[eb.h.shape[1], 128], [16, 128], [1, 16]]),
            BbT_own.h[:, k * 16:(k + 1) * 16][:, None, :].broadcast_to([128, 128, 16])),
            [eb, BbT_own], [eb], accum=True)
        vop(lambda e, eb=eb, eo=eo, cb=cb, co=co: e.tensor_mul(
            eb.h[:, eo:eo + 2048], eb.h[:, eo:eo + 2048], cb.h[:, co:co + 2048]),
            [eb, cb], [eb], accum=True)

        def _score_reduce(e, eb=eb, eo=eo, sb=sb, so=so):
            with B.nc.allow_low_precision(reason="bf16 score tile for bf16 matmul"):
                return e.reduce_sum(
                    sb.h[0:128, so:so + 128],
                    bass.AP(eb.h, eo, [[eb.h.shape[1], 128], [16, 128], [1, 16]]),
                    axis=AX.X)
        vop(_score_reduce, [eb], [sb], accum=True)

    def p4_xtm(k):
        par = k % 2
        xb, xo = XCK[par]
        tb, to = XTMK[par]
        for dc in range(16):
            transpose_to(tb, to + dc * 128, xb,
                         bass.AP(xb.h, xo + dc * 128, [[xb.h.shape[1], 128], [1, 128]]),
                         128)

    def p4_y(k):
        par = k % 2
        tb, to = XTMK[par]
        hb, ho = HK[par]
        sb, so = SSL[par]
        yb, yo = YS[par]
        for dc in range(16):
            bank = 2 + dc % 2
            top(lambda e, dc=dc, bank=bank, tb=tb, to=to, sb=sb, so=so: e.matmul(
                pb[bank].h[:, 0:128],
                bass.AP(tb.h, to + dc * 128, [[tb.h.shape[1], 128], [1, 128]]),
                sb.h[0:128, so:so + 128], start=True, stop=False),
                [tb, sb], [pb[bank]])
            top(lambda e, dc=dc, bank=bank, k=k, hb=hb, ho=ho: e.matmul(
                pb[bank].h[:, 0:128],
                bass.AP(hb.h, ho + dc * 128, [[hb.h.shape[1], 16], [1, 128]]),
                Chp_all.h[:, k * 128:(k + 1) * 128], start=False, stop=True),
                [hb, Chp_all], [pb[bank]], accum=True)
            vop(lambda e, dc=dc, bank=bank, yb=yb, yo=yo: e.tensor_copy(
                yb.h[:, yo + dc * 128:yo + (dc + 1) * 128], pb[bank].h[:, 0:128]),
                [pb[bank]], [yb], accum=True)

    def p4_ypost(k):
        par = k % 2
        xb, xo = XCK[par]
        zb, zo = ZK[par]
        eb, eo = ESC[par]
        cb, co = CBC[par]
        yb, yo = YS[par]
        sop(lambda e, zb=zb, zo=zo, cb=cb, co=co: e.activation(
            cb.h[:, co:co + 2048], zb.h[:, zo:zo + 2048], AF.Silu), [zb], [cb],
            accum=True)
        vop(lambda e, yb=yb, yo=yo, cb=cb, co=co: e.tensor_mul(
            yb.h[:, yo:yo + 2048], yb.h[:, yo:yo + 2048], cb.h[:, co:co + 2048]),
            [yb, cb], [yb], accum=True)
        vop(lambda e, xb=xb, xo=xo, eb=eb, eo=eo: e.tensor_mul(
            bass.AP(eb.h, eo, [[eb.h.shape[1], 128], [128, 16], [1, 128]]),
            bass.AP(xb.h, xo, [[xb.h.shape[1], 128], [128, 16], [1, 128]]),
            Dp.h[:, :, None].broadcast_to([128, 16, 128])), [xb, Dp], [eb], accum=True)
        vop(lambda e, yb=yb, yo=yo, eb=eb, eo=eo: e.tensor_add(
            yb.h[:, yo:yo + 2048], yb.h[:, yo:yo + 2048], eb.h[:, eo:eo + 2048]),
            [yb, eb], [yb], accum=True)

    def p4_attn(k):
        par = k % 2
        yb, yo = YS[par]
        for nh in range(2):
            for dc in range(16):
                top(lambda e, nh=nh, dc=dc, yb=yb, yo=yo: e.matmul(
                    pb[nh].h[:, :],
                    bass.AP(yb.h, yo + dc * 128, [[yb.h.shape[1], 128], [1, 128]]),
                    wpool.h[:, dc * D_MODEL + nh * 512:dc * D_MODEL + (nh + 1) * 512],
                    start=(dc == 0), stop=(dc == 15)), [yb, wpool], [pb[nh]],
                    accum=(dc != 0))
        for nh in range(2):
            vop(lambda e, nh=nh, par=par: e.tensor_add(
                xat.h[:, nh * 512:(nh + 1) * 512],
                XROW[par].h[:, nh * 512:(nh + 1) * 512],
                pb[nh].h[:, :]), [XROW[par], pb[nh]], [xat], accum=True)
        B.dma("qg", "dxats", lambda q, k=k: q.dma_start(
            out=bass.AP(T["xat_sp"].tensor, k * 128,
                        [[8 * n_own, 128], [n_own, 8], [1, 128]]),
            in_=bass.AP(xat.h, 0, [[1024, 128], [128, 8], [1, 128]])), [xat], [dxat])

    p4_loads(0)
    p4_xrow(0)
    p4_scores(0)
    p4_xtm(0)
    if NCH > 1:
        p4_loads(1)
        p4_xrow(1)
    for k in range(NCH):
        p4_y(k)
        p4_ypost(k)
        if k + 1 < NCH:
            p4_xtm(k + 1)
        p4_attn(k)
        if k + 1 < NCH:
            p4_scores(k + 1)
        if k + 2 < NCH:
            p4_loads(k + 2)
            p4_xrow(k + 2)

    wqst = sbf[1]
    for rb in range(D_MODEL // 128):
        for sg in range(2):
            stg = scr[0] if sg == 0 else scr[2]
            extra = [wpar] if rb == 0 else []
            B.dma("qs", "dwf%d" % sg, lambda q, rb=rb, sg=sg, stg=stg: q.dma_start(
                out=stg.h[:, 0:2048],
                in_=bass.AP(T["w_down"].tensor, rb * 128 * FFN + sg * 2048,
                            [[FFN, 128], [1, 2048]])), [], [stg] + extra)
            vop(lambda e, stg=stg: e.tensor_scalar(
                lad_d.h[:, 0:2048], stg.h[:, 0:2048], thrp[2].h[:, 0:1], None, A_.is_ge),
                [stg, thrp[2]], [lad_d])
            vop(lambda e, stg=stg: e.scalar_tensor_tensor(
                wqst.h[:, 0:2048], stg.h[:, 0:2048], thrn[2].h[:, 0:1],
                lad_d.h[:, 0:2048], A_.is_le, A_.subtract), [stg, thrn[2], lad_d], [wqst])
            for kb in range(16):
                transpose_to(wpool, (sg * 16 + kb) * D_MODEL + rb * 128, wqst,
                             wqst.h[:, kb * 128:(kb + 1) * 128], 128)

    xatr = [xg[0], xg[1]]
    outb = [xn32] * 2
    XQ2R = [(wb16[0], wb16[1]), (sbf[0], sbf[1])]
    k2c = nmr

    def p4b_load(k):
        par = k % 2
        ra, rb_ = XQ2R[par]
        B.dma("qs", "dq2r%d" % par, lambda q, k=k, ra=ra: q.dma_start(
            out=bass.AP(ra.h, 0, [[2048, 128], [128, 16], [1, 128]]),
            in_=bass.AP(T["xq2_sp"].tensor, k * 128,
                        [[32 * n_own, 128], [n_own, 16], [1, 128]])), [dq2], [ra])
        B.dma("qs", "dq2r%d" % par, lambda q, k=k, rb_=rb_: q.dma_start(
            out=bass.AP(rb_.h, 0, [[2048, 128], [128, 16], [1, 128]]),
            in_=bass.AP(T["xq2_sp"].tensor, 16 * n_own + k * 128,
                        [[32 * n_own, 128], [n_own, 16], [1, 128]])), [dq2], [rb_])
        B.dma("qs", "dxatr%d" % par, lambda q, k=k, par=par: q.dma_start(
            out=bass.AP(xatr[par].h, 0, [[1024, 128], [128, 8], [1, 128]]),
            in_=bass.AP(T["xat_sp"].tensor, k * 128,
                        [[8 * n_own, 128], [n_own, 8], [1, 128]])), [dxat], [xatr[par]])

    p4b_load(0)
    for k in range(NCH):
        par = k % 2
        if k + 1 < NCH:
            p4b_load(k + 1)
        for nh in range(2):
            dbk = pb[nh] if par == 0 else pb[2 + nh]
            for kc in range(32):
                src = XQ2R[par][kc // 16]
                top(lambda e, nh=nh, kc=kc, src=src, dbk=dbk: e.matmul(
                    dbk.h[:, :], src.h[:, (kc % 16) * 128:((kc % 16) + 1) * 128],
                    wpool.h[:, kc * D_MODEL + nh * 512:kc * D_MODEL + (nh + 1) * 512],
                    start=(kc == 0), stop=(kc == 31)), [src, wpool], [dbk],
                    accum=(kc != 0))
        vop(lambda e, k=k: e.tensor_mul(k2c.h[:, 0:1], swneg[2].h[:, 0:1],
                                        c2_all.h[:, k:k + 1]), [swneg[2], c2_all], [k2c])
        ob = outb[par]
        for nh in range(2):
            dbk = pb[nh] if par == 0 else pb[2 + nh]
            vop(lambda e, nh=nh, ob=ob, par=par, dbk=dbk: e.scalar_tensor_tensor(
                ob.h[:, nh * 512:(nh + 1) * 512], dbk.h[:, :], k2c.h[:, 0:1],
                xatr[par].h[:, nh * 512:(nh + 1) * 512], A_.mult, A_.add),
                [dbk, k2c, xatr[par]], [ob], accum=True)
        B.dma("qg", "dout", lambda q, k=k, ob=ob: q.dma_start(
            out=bass.AP(T["out"].tensor, k * 128 * D_MODEL, [[D_MODEL, 128], [1, D_MODEL]]),
            in_=ob.h[:, :]), [ob], [])

    B.wait_stream("qs", "dout")


_NC_CACHE = {}


def _get_nc(n_own, n_pre):
    key = (n_own, n_pre)
    if key not in _NC_CACHE:
        _NC_CACHE[key] = build_core_program(n_own, n_pre)
    return _NC_CACHE[key]


def kernel(L0_attn_x, L0_attn_W_in, L0_attn_conv_w, L0_attn_conv_b, L0_attn_W_xproj,
           L0_attn_A_log, L0_attn_D_param, L0_attn_W_out, L0_ffn_w_gate, L0_ffn_w_up,
           L0_ffn_w_down):
    x = np.ascontiguousarray(L0_attn_x, dtype=np.float32)
    b_, s_, d_ = x.shape
    half = s_ // 2
    nc = _get_nc(half, half)
    shared = {
        "W_in": np.ascontiguousarray(L0_attn_W_in, np.float32),
        "conv_w": np.ascontiguousarray(L0_attn_conv_w, np.float32),
        "conv_b": np.ascontiguousarray(L0_attn_conv_b, np.float32),
        "W_xproj": np.ascontiguousarray(L0_attn_W_xproj, np.float32),
        "A_log": np.ascontiguousarray(L0_attn_A_log, np.float32),
        "D_param": np.ascontiguousarray(L0_attn_D_param, np.float32),
        "W_out": np.ascontiguousarray(L0_attn_W_out, np.float32),
        "w_gate": np.ascontiguousarray(L0_ffn_w_gate, np.float32),
        "w_up": np.ascontiguousarray(L0_ffn_w_up, np.float32),
        "w_down": np.ascontiguousarray(L0_ffn_w_down, np.float32),
        "ident": np.eye(128, dtype=bfdt),
        "identf": np.eye(16, dtype=np.float32),
        "maskbig": np.ascontiguousarray(
            (np.triu(np.ones((128, 128), np.float32), 1).T * MASK_BIG).astype(bfdt)),
    }
    zwin = np.zeros((D_INNER, D_MODEL), np.float32)
    zcb = np.zeros((D_INNER,), np.float32)
    zx = np.zeros((half, D_MODEL), np.float32)
    in_maps = []
    for c in range(8):
        b = c // 2
        h = c % 2
        m = dict(shared)
        m["x_own"] = np.ascontiguousarray(x[b, h * half:(h + 1) * half])
        if h == 1:
            m["x_pre"] = np.ascontiguousarray(x[b, 0:half])
            m["pre_Win"] = shared["W_in"][0:D_INNER].copy()
            m["pre_conv_b"] = shared["conv_b"].copy()
        else:
            m["x_pre"] = zx
            m["pre_Win"] = zwin
            m["pre_conv_b"] = zcb
        in_maps.append(m)
    res = run_bass_kernel_spmd(nc, in_maps, core_ids=list(range(8)))
    out = np.empty((b_, s_, d_), np.float32)
    for c in range(8):
        b = c // 2
        h = c % 2
        out[b, h * half:(h + 1) * half] = np.asarray(res.results[c]["out"]).reshape(half, d_)
    return out
